# version 03
# baseline (speedup 1.0000x reference)
import numpy as np
import ml_dtypes
from contextlib import ExitStack

import concourse.bass as bass
import concourse.mybir as mybir
from concourse.bass_utils import run_bass_kernel_spmd

F32 = mybir.dt.float32
BF16 = mybir.dt.bfloat16
U8 = mybir.dt.uint8
AF = mybir.ActivationFunctionType
ALU = mybir.AluOpType
AX = mybir.AxisListType

B, S, D = 2, 8192, 1024
NC = 8
TPC = 2048
NH, DH, KVH = 16, 64, 4
NKT = S // 128
NQG = S // 512
SHIFT = 44.0
LEPS = 1e-30
NCH = 16


class Sem:
    def __init__(self, h):
        self.h = h
        self.c = 0



def build_attn_core(layer, nqg=None, dump=False):
    causal = (layer == 0)
    qscale = (DH ** -0.5) if layer == 0 else (D ** -0.5)
    if nqg is None:
        nqg = NQG

    nc = bass.Bass("TRN2", target_bir_lowering=False, debug=False)
    xfm = nc.dram_tensor("xfm", [D, S], F32, kind="ExternalInput").ap()
    wq = nc.dram_tensor("wq", [D, 256], F32, kind="ExternalInput").ap()
    wk = nc.dram_tensor("wk", [D, 64], F32, kind="ExternalInput").ap()
    wvz = nc.dram_tensor("wvz", [D, 65], F32, kind="ExternalInput").ap()
    o_out0 = nc.dram_tensor("o_out0", [128, S], BF16, kind="ExternalOutput").ap()
    o_out1 = nc.dram_tensor("o_out1", [128, S], BF16, kind="ExternalOutput").ap()
    if causal:
        cmask_d = nc.dram_tensor("cmask", [4, 128, 512], BF16, kind="ExternalInput").ap()
    else:
        rqc_d = nc.dram_tensor("rqc", [128, 2 * S], F32, kind="ExternalInput").ap()
        rqs_d = nc.dram_tensor("rqs", [128, 2 * S], F32, kind="ExternalInput").ap()
        rkc_d = nc.dram_tensor("rkc", [64, S], F32, kind="ExternalInput").ap()
        rks_d = nc.dram_tensor("rks", [64, S], F32, kind="ExternalInput").ap()
    if dump:
        qdmp = nc.dram_tensor("qdmp", [128, 2 * S], BF16, kind="ExternalOutput").ap()
        kdmp = nc.dram_tensor("kdmp", [128, S], BF16, kind="ExternalOutput").ap()
        vdmp = nc.dram_tensor("vdmp", [128, NKT * 65], BF16, kind="ExternalOutput").ap()
        bdmp = nc.dram_tensor("bdmp", [128, 4], F32, kind="ExternalOutput").ap()

    ctx = ExitStack()
    with ctx:
        sb = lambda n, sh, dt: ctx.enter_context(nc.sbuf_tensor(n, sh, dt))
        psa = lambda n, sh: ctx.enter_context(nc.psum_tensor(n, sh, F32))
        sem = lambda n: Sem(ctx.enter_context(nc.semaphore(n)))

        q_pk = sb("q_pk", [128, 2 * S], BF16)
        ksb = sb("ksb", [128, S], BF16)
        vsb = sb("vsb", [128, NKT * 65], BF16)
        o_fm = sb("o_fm", [128, 2 * S], BF16)
        xin = sb("xin", [128, 2 * 4096], BF16)
        ptb = sb("ptb", [128, 8 * 512], BF16)
        wq_sb = sb("wq_sb", [128, 8 * 256], BF16)
        wk_sb = sb("wk_sb", [128, 8 * 64], BF16)
        wv_sb = sb("wv_sb", [128, 8 * 65], BF16)
        qmx = sb("qmx", [128, 8], F32)
        sqmx = sb("sqmx", [128, 8], F32)
        ltmp = sb("ltmp", [32, 512], F32)
        linv = sb("linv", [1, 512], F32)
        linv2 = sb("linv2", [1, 512], F32)
        lb = sb("lb", [64, 512], F32)
        ones_f = sb("ones_f", [128, 1], F32)
        ones_r = sb("ones_r", [1, 128], F32)
        stat = sb("stat", [1, 16], F32)
        stat2 = sb("stat2", [1, 16], F32)
        bias_c = sb("bias_c", [128, 4], F32)
        if causal:
            cm_sb = sb("cm_sb", [128, 4 * 512], BF16)
        else:
            q32 = sb("q32", [128, 512], F32)
            qsw = sb("qsw", [128, 512], F32)
            qt1 = sb("qt1", [128, 512], F32)
            qt2 = sb("qt2", [128, 512], F32)
            rtc = sb("rtc", [128, 2 * 512], F32)
            rts = sb("rts", [128, 2 * 512], F32)
            ktc = sb("ktc", [64, 512], F32)
            kts = sb("kts", [64, 512], F32)

        ring = psa("ring", [128, 2048])
        av = [psa("av0", [128, 512]), psa("av1", [128, 512])]
        psb_ = psa("psb", [128, 512])
        pst = psa("pst", [128, 512])

        s_wld = sem("s_wld")
        s_xin = sem("s_xin")
        s_xfree = sem("s_xfree")
        s_pmm = sem("s_pmm")
        s_pev = sem("s_pev")
        s_pevv = sem("s_pevv")
        s_st = sem("s_st")
        s_exp = sem("s_exp")
        s_msk = sem("s_msk")
        s_av = sem("s_av")
        s_ep = sem("s_ep")
        s_on = sem("s_on")
        s_bias = sem("s_bias")
        s_msc = sem("s_msc")
        s_out = sem("s_out")
        s_cml = sem("s_cml")

        with nc.Block() as block:
            def ph0_g(g):
                g.dma_start(out=wq_sb[:].rearrange("p (c n) -> p c n", c=8),
                            in_=wq.rearrange("(c p) n -> p c n", p=128)).then_inc(s_wld.h, 16)
                g.dma_start(out=wk_sb[:].rearrange("p (c n) -> p c n", c=8),
                            in_=wk.rearrange("(c p) n -> p c n", p=128)).then_inc(s_wld.h, 16)
                g.dma_start(out=wv_sb[:].rearrange("p (c n) -> p c n", c=8),
                            in_=wvz.rearrange("(c p) n -> p c n", p=128)).then_inc(s_wld.h, 16)
            block.gpsimd(ph0_g)
            s_wld.c += 48

            if causal:
                def ph0_s(sy):
                    sy.dma_start(out=cm_sb[:].rearrange("p (d n) -> p d n", d=4),
                                 in_=cmask_d.rearrange("d p n -> p d n")).then_inc(s_cml.h, 16)
                block.sync(ph0_s)
                s_cml.c += 16

            def ph0_v(v):
                v.memset(o_fm[:], 0.0)
                v.memset(ones_f[:], 1.0)
                v.memset(ones_r[:], 1.0)
                v.memset(stat[:], 0.0)
                v.memset(qmx[:], 0.0)
                v.memset(stat2[:], 0.0)
                v.memset(av[0][64:96, :], 0.0)
                v.memset(av[1][64:96, :], 0.0)
                v.memset(vsb[:, 64::65], 1.0).then_inc(s_msc.h, 1)
            block.vector(ph0_v)
            s_msc.c += 1

            s_rt = sem("s_rt")
            s_rp = sem("s_rp")
            rope_guard = [0]
            rope_end = {}
            rope_sub = {}
            for kc in range(NCH):
                xo = (kc % 2) * 4096

                def fB_g(g, kc=kc, xo=xo):
                    if kc >= 1:
                        g.wait_ge(s_xin.h, 16 * kc)
                    if kc >= 2:
                        if causal:
                            g.wait_ge(s_pev.h, 2 * (kc - 1))
                            g.wait_ge(s_pevv.h, kc - 1)
                        else:
                            g.wait_ge(s_rp.h, rope_end[kc - 2])
                    g.dma_start(
                        out=xin[:, xo:xo + 4096].rearrange("p (c t) -> p c t", c=8),
                        in_=xfm[:, kc * 512:(kc + 1) * 512].rearrange("(c p) t -> p c t", p=128),
                    ).then_inc(s_xin.h, 16)
                block.gpsimd(fB_g)
                s_xin.c += 16

                def fB_t(t, kc=kc, xo=xo):
                    t.wait_ge(s_xin.h, 16 * (kc + 1))
                    if kc == 0:
                        t.wait_ge(s_wld.h, 48)
                    else:
                        if causal:
                            t.wait_ge(s_pev.h, 2 * kc)
                            t.wait_ge(s_pevv.h, kc)
                        else:
                            t.wait_ge(s_rp.h, rope_sub[kc - 1][0])
                    for c in range(2):
                        if (not causal) and kc > 0 and c == 1:
                            t.wait_ge(s_rp.h, rope_sub[kc - 1][1])
                        for k in range(8):
                            inst = t.matmul(out=ring[:, c * 512:(c + 1) * 512],
                                     lhsT=wq_sb[:, k * 256 + c * 128: k * 256 + (c + 1) * 128],
                                     rhs=xin[:, xo + k * 512: xo + (k + 1) * 512],
                                     start=(k == 0), stop=(k == 7))
                            if c == 1 and k == 7:
                                inst.then_inc(s_pmm.h, 1)
                    if (not causal) and kc > 0:
                        t.wait_ge(s_rp.h, rope_sub[kc - 1][2])
                    for k in range(8):
                        inst = t.matmul(out=ring[0:64, 1024:1536],
                                 lhsT=wk_sb[:, k * 64:(k + 1) * 64],
                                 rhs=xin[:, xo + k * 512: xo + (k + 1) * 512],
                                 start=(k == 0), stop=(k == 7))
                        if k == 7:
                            inst.then_inc(s_pmm.h, 1)
                    if (not causal) and kc > 0:
                        t.wait_ge(s_rp.h, rope_sub[kc - 1][3])
                    for kt in range(4):
                        for k in range(8):
                            inst = t.matmul(out=ring[:, 1536 + kt * 65: 1536 + kt * 65 + 65],
                                     lhsT=xin[:, xo + k * 512 + kt * 128: xo + k * 512 + (kt + 1) * 128],
                                     rhs=wv_sb[:, k * 65:(k + 1) * 65],
                                     start=(k == 0), stop=(k == 7))
                            if kt == 3 and k == 7:
                                inst.then_inc(s_pmm.h, 1)
                block.tensor(fB_t)
                s_pmm.c += 3

                if causal:
                    def fB_s(s, kc=kc):
                        s.wait_ge(s_pmm.h, 3 * kc + 1)
                        for c in range(2):
                            inst = s.activation(out=q_pk[:, c * S + kc * 512: c * S + (kc + 1) * 512],
                                         in_=ring[:, c * 512:(c + 1) * 512],
                                         func=AF.Copy, scale=float(qscale))
                            if c == 1:
                                inst.then_inc(s_pev.h, 1)
                        s.wait_ge(s_pmm.h, 3 * kc + 3)
                        for kt in range(4):
                            inst = s.activation(out=vsb[:, (kc * 4 + kt) * 65: (kc * 4 + kt) * 65 + 64],
                                         in_=ring[:, 1536 + kt * 65: 1536 + kt * 65 + 64],
                                         func=AF.Copy)
                            if kt == 3:
                                inst.then_inc(s_pev.h, 1)
                    block.scalar(fB_s)
                    s_pev.c += 2

                    def fB_v(v, kc=kc):
                        v.wait_ge(s_pmm.h, 3 * kc + 2)
                        v.tensor_copy(out=ksb[0:64, kc * 512:(kc + 1) * 512], in_=ring[0:64, 1024:1536])
                        v.tensor_copy(out=ksb[64:128, kc * 512:(kc + 1) * 512],
                                      in_=ring[0:64, 1024:1536]).then_inc(s_pevv.h, 1)
                    block.vector(fB_v)
                    s_pevv.c += 1
                else:
                    def rt_s(sy, kc=kc, prev=s_rt.c, guard=rope_guard[0]):
                        sy.wait_ge(s_rt.h, prev)
                        if guard > 0:
                            sy.wait_ge(s_rp.h, guard)
                        for c in range(2):
                            sy.dma_start(out=rtc[:, c * 512:(c + 1) * 512],
                                         in_=rqc_d[:, c * S + kc * 512: c * S + (kc + 1) * 512]).then_inc(s_rt.h, 16)
                            sy.dma_start(out=rts[:, c * 512:(c + 1) * 512],
                                         in_=rqs_d[:, c * S + kc * 512: c * S + (kc + 1) * 512]).then_inc(s_rt.h, 16)
                        sy.dma_start(out=ktc[:], in_=rkc_d[:, kc * 512:(kc + 1) * 512]).then_inc(s_rt.h, 16)
                        sy.dma_start(out=kts[:], in_=rks_d[:, kc * 512:(kc + 1) * 512]).then_inc(s_rt.h, 16)
                    block.sync(rt_s)
                    s_rt.c += 96
                    rt_need = s_rt.c

                    def rchain_s(emitfn):
                        need = s_rp.c

                        def f(s, need=need):
                            s.wait_ge(s_rp.h, need)
                            emitfn(s).then_inc(s_rp.h, 1)
                        block.scalar(f)
                        s_rp.c += 1

                    def rchain_v(emitfn):
                        need = s_rp.c

                        def f(v, need=need):
                            v.wait_ge(s_rp.h, need)
                            emitfn(v).then_inc(s_rp.h, 1)
                        block.vector(f)
                        s_rp.c += 1

                    _sub = []
                    for c in range(2):
                        def ev_q(s, kc=kc, c=c, rt_need=rt_need):
                            s.wait_ge(s_pmm.h, 3 * kc + 1)
                            s.wait_ge(s_rt.h, rt_need)
                            return s.activation(out=q32[:], in_=ring[:, c * 512:(c + 1) * 512],
                                                func=AF.Copy)
                        rchain_s(ev_q)
                        _sub.append(s_rp.c)

                        def swp(v, c=c):
                            v.tensor_copy(out=qsw[0:32, :], in_=q32[32:64, :])
                            v.tensor_copy(out=qsw[32:64, :], in_=q32[0:32, :])
                            v.tensor_copy(out=qsw[64:96, :], in_=q32[96:128, :])
                            v.tensor_copy(out=qsw[96:128, :], in_=q32[64:96, :])
                            return v.tensor_tensor(out=qt1[:], in0=q32[:],
                                                   in1=rtc[:, c * 512:(c + 1) * 512],
                                                   op=ALU.mult)
                        rchain_v(swp)
                        rchain_v(lambda v, c=c: v.tensor_tensor(
                            out=qt2[:], in0=qsw[:], in1=rts[:, c * 512:(c + 1) * 512],
                            op=ALU.mult))
                        rchain_v(lambda v: v.tensor_tensor(
                            out=qt1[:], in0=qt1[:], in1=qt2[:], op=ALU.add))
                        rchain_s(lambda s, kc=kc, c=c: s.activation(
                            out=q_pk[:, c * S + kc * 512: c * S + (kc + 1) * 512],
                            in_=qt1[:], func=AF.Copy, scale=float(qscale)))
                    def ev_k(s, kc=kc):
                        s.wait_ge(s_pmm.h, 3 * kc + 2)
                        return s.activation(out=q32[0:64, :], in_=ring[0:64, 1024:1536],
                                            func=AF.Copy)
                    rchain_s(ev_k)
                    _sub.append(s_rp.c)

                    def swpk(v):
                        v.tensor_copy(out=qsw[0:32, :], in_=q32[32:64, :])
                        v.tensor_copy(out=qsw[32:64, :], in_=q32[0:32, :])
                        return v.tensor_tensor(out=qt1[0:64, :], in0=q32[0:64, :],
                                               in1=ktc[:], op=ALU.mult)
                    rchain_v(swpk)
                    rchain_v(lambda v: v.tensor_tensor(
                        out=qt2[0:64, :], in0=qsw[0:64, :], in1=kts[:], op=ALU.mult))
                    rchain_v(lambda v: v.tensor_tensor(
                        out=qt1[0:64, :], in0=qt1[0:64, :], in1=qt2[0:64, :], op=ALU.add))
                    rchain_s(lambda s, kc=kc: s.activation(
                        out=ksb[0:64, kc * 512:(kc + 1) * 512], in_=qt1[0:64, :],
                        func=AF.Copy))
                    rchain_v(lambda v, kc=kc: v.tensor_copy(
                        out=ksb[64:128, kc * 512:(kc + 1) * 512],
                        in_=ksb[0:64, kc * 512:(kc + 1) * 512]))
                    def ev_v(s, kc=kc):
                        s.wait_ge(s_pmm.h, 3 * kc + 3)
                        for kt in range(4):
                            inst = s.activation(out=vsb[:, (kc * 4 + kt) * 65: (kc * 4 + kt) * 65 + 64],
                                         in_=ring[:, 1536 + kt * 65: 1536 + kt * 65 + 64],
                                         func=AF.Copy)
                        return inst
                    rchain_s(ev_v)
                    _sub.append(s_rp.c)
                    rope_guard[0] = s_rp.c
                    rope_end[kc] = s_rp.c
                    rope_sub[kc] = tuple(_sub)
            s_sq = sem("s_sq")
            s_qn = sem("s_qn")
            s_qnv = sem("s_qnv")
            sqq = sb("sqq", [128, 2 * 512], F32)
            sqk = sb("sqk", [64, 2 * 512], F32)
            qmaxrow = sb("qmaxrow", [1, 80], F32)
            statbank = [pst, psb_]

            sqq2 = sb("sqq2", [128, 2 * 512], F32)

            def phC_act(s):
                for kc in range(NCH):
                    if kc >= 2:
                        s.wait_ge(s_qn.h, 5 * (kc - 1))
                    if kc == 0:
                        s.wait_ge(s_pev.h, s_pev.c)
                        s.wait_ge(s_pevv.h, s_pevv.c)
                        if not causal:
                            s.wait_ge(s_rp.h, s_rp.c)
                    bo = (kc % 2) * 512
                    s.activation(out=sqq[:, bo:bo + 512],
                                 in_=q_pk[:, kc * 512:kc * 512 + 512],
                                 func=AF.Square).then_inc(s_sq.h, 1)
                    s.activation(out=sqq2[:, bo:bo + 512],
                                 in_=q_pk[:, S + kc * 512: S + kc * 512 + 512],
                                 func=AF.Square).then_inc(s_sq.h, 1)
                    s.activation(out=sqk[:, bo:bo + 512],
                                 in_=ksb[0:64, kc * 512:kc * 512 + 512],
                                 func=AF.Square).then_inc(s_sq.h, 1)
            block.scalar(phC_act)
            s_sq.c += 3 * NCH

            def phC_pe(t):
                for kc in range(NCH):
                    bo = (kc % 2) * 512
                    for j in range(5):
                        mi = kc * 5 + j
                        if mi >= 2:
                            t.wait_ge(s_qnv.h, mi - 1)
                        if j < 4:
                            c, half = j // 2, j % 2
                            t.wait_ge(s_sq.h, 3 * kc + c + 1)
                            src_ = (sqq if c == 0 else sqq2)
                            base = half * 64
                            t.matmul(out=statbank[mi % 2][0:1, 0:512],
                                     lhsT=ones_f[base:base + 64, :],
                                     rhs=src_[base:base + 64, bo:bo + 512],
                                     start=True, stop=True).then_inc(s_qn.h, 1)
                        else:
                            t.wait_ge(s_sq.h, 3 * kc + 3)
                            t.matmul(out=statbank[mi % 2][0:1, 0:512],
                                     lhsT=ones_f[0:64, :],
                                     rhs=sqk[:, bo:bo + 512],
                                     start=True, stop=True).then_inc(s_qn.h, 1)
            block.tensor(phC_pe)
            s_qn.c += 5 * NCH

            def phC_dve(v):
                for kc in range(NCH):
                    for j in range(5):
                        mi = kc * 5 + j
                        v.wait_ge(s_qn.h, mi + 1)
                        if j < 4:
                            h = 2 * (j // 2) + (j % 2)
                            col = h * 16 + kc
                        else:
                            col = 64 + kc
                        v.reduce_max(out=qmaxrow[0:1, col:col + 1],
                                     in_=statbank[mi % 2][0:1, 0:512],
                                     axis=AX.X).then_inc(s_qnv.h, 1)
                v.nop().then_inc(s_qnv.h, 1)
            block.vector(phC_dve)
            s_qnv.c += 5 * NCH + 1

            def phC_fin(v):
                v.wait_ge(s_qnv.h, s_qnv.c)
                for h in range(4):
                    v.reduce_max(out=stat[0:1, h:h + 1],
                                 in_=qmaxrow[0:1, h * 16:(h + 1) * 16], axis=AX.X)
                v.reduce_max(out=stat[0:1, 4:5],
                             in_=qmaxrow[0:1, 64:80], axis=AX.X).then_inc(s_ep.h, 1)
            block.vector(phC_fin)
            s_ep.c += 1

            def phC_s2(s):
                s.wait_ge(s_ep.h, s_ep.c)
                s.activation(out=stat2[0:1, 0:4], in_=stat[0:1, 0:4], func=AF.Ln)
                s.activation(out=stat2[0:1, 4:5], in_=stat[0:1, 4:5], func=AF.Ln).then_inc(s_ep.h, 1)
            block.scalar(phC_s2)
            s_ep.c += 1

            def phC_v3(v):
                v.wait_ge(s_ep.h, s_ep.c)
                v.tensor_tensor(out=stat2[0:1, 8:12], in0=stat2[0:1, 0:4],
                                in1=stat2[0:1, 4:5].broadcast_to([1, 4]),
                                op=ALU.add).then_inc(s_ep.h, 1)
            block.vector(phC_v3)
            s_ep.c += 1

            def phC_s3(s):
                s.wait_ge(s_ep.h, s_ep.c)
                s.activation(out=stat2[0:1, 12:16], in_=stat2[0:1, 8:12],
                             func=AF.Exp, scale=0.5).then_inc(s_ep.h, 1)
            block.scalar(phC_s3)
            s_ep.c += 1

            def phC_v4(v):
                v.wait_ge(s_ep.h, s_ep.c)
                v.tensor_scalar(out=stat[0:1, 8:12], in0=stat2[0:1, 12:16],
                                scalar1=-1.0, scalar2=SHIFT,
                                op0=ALU.mult, op1=ALU.add).then_inc(s_ep.h, 1)
            block.vector(phC_v4)
            s_ep.c += 1

            def phC_t2(t):
                t.wait_ge(s_ep.h, s_ep.c)
                for h in range(4):
                    inst = t.matmul(out=psb_[:, h:h + 1], lhsT=ones_r[:],
                             rhs=stat[0:1, 8 + h:9 + h], start=True, stop=True)
                inst.then_inc(s_ep.h, 1)
            block.tensor(phC_t2)
            s_ep.c += 1

            def phC_v5(v):
                v.wait_ge(s_ep.h, s_ep.c)
                v.tensor_copy(out=bias_c[:], in_=psb_[:, 0:4]).then_inc(s_bias.h, 1)
            block.vector(phC_v5)
            s_bias.c += 1

            class WC:
                def __init__(self):
                    self.seen = {}

                def w(self, eng, key, semx, val):
                    if val <= 0:
                        return
                    if self.seen.get((key, id(semx)), -1) >= val:
                        return
                    eng.wait_ge(semx.h, val)
                    self.seen[(key, id(semx))] = val

            wc = WC()
            pending_e4 = []
            first_attn = True
            bank_free_at = [0, 0, 0, 0]
            av_free_at = [0, 0]
            first_mask = [True]

            for qg in range(NQG):
                bound = 4 * qg + 4 if causal else NKT
                npairs = bound // 2
                for hp in range(2):
                    heads = (2 * hp, 2 * hp + 1)
                    av_S0 = {h: s_av.c for h in heads}
                    st_S0 = s_st.c
                    exp_at = {h: {} for h in heads}
                    msk_at = {h: {} for h in heads}
                    av_guard = {heads[0]: av_free_at[0], heads[1]: av_free_at[1]}
                    av_base = {}
                    for h in heads:
                        av_base[h] = s_av.c + (0 if h == heads[0] else 0)
                    for pi in range(npairs):
                        for h in heads:
                            s_exp.c += 1
                            exp_at[h][2 * pi] = s_exp.c
                            exp_at[h][2 * pi + 1] = s_exp.c
                    if causal:
                        for h in heads:
                            for kt in range(bound - 4, bound):
                                s_msk.c += 1
                                msk_at[h][kt] = s_msk.c
                    ring_entry = list(bank_free_at)

                    av_cnt = {heads[0]: 0, heads[1]: 0}
                    av_done_at = {heads[0]: {}, heads[1]: {}}

                    def fD_t(t, qg=qg, hp=hp, heads=heads, bound=bound,
                             npairs=npairs, exp_at=exp_at, msk_at=msk_at,
                             av_guard=av_guard, first=first_attn,
                             ring_entry=ring_entry, pend=list(pending_e4),
                             av_base_c=s_av.c):
                        if first:
                            t.wait_ge(s_pev.h, s_pev.c)
                            t.wait_ge(s_pevv.h, s_pevv.c)
                            t.wait_ge(s_rp.h, s_rp.c)
                            t.wait_ge(s_msc.h, 1)
                        avc = [av_base_c]

                        def emit_av(h, kt):
                            hi = 0 if h == heads[0] else 1
                            avb = av[hi]
                            if kt == 0:
                                wc.w(t, "pe", s_on, av_guard[h])
                            wc.w(t, "pe", s_exp, exp_at[h][kt])
                            if kt in msk_at[h]:
                                wc.w(t, "pe", s_msk, msk_at[h][kt])
                            slot = (kt % 4) + 4 * hi
                            t.matmul(out=avb[0:65, :],
                                     lhsT=vsb[:, kt * 65:(kt + 1) * 65],
                                     rhs=ptb[:, slot * 512:(slot + 1) * 512],
                                     start=(kt == 0), stop=(kt == bound - 1)
                                     ).then_inc(s_av.h, 1)
                            avc[0] += 1
                            av_done_at[h][kt] = avc[0]

                        for pi in range(npairs):
                            for h in heads:
                                hi = 0 if h == heads[0] else 1
                                base = 64 * (h % 2)
                                cb = (h // 2) * S
                                for kt in (2 * pi, 2 * pi + 1):
                                    bk2 = (kt % 2) + 2 * hi
                                    if kt >= 2:
                                        wc.w(t, "pe", s_exp, exp_at[h][kt - 2])
                                    else:
                                        wc.w(t, "pe", s_exp, ring_entry[bk2])
                                    inst = t.matmul(
                                        out=ring[:, bk2 * 512:(bk2 + 1) * 512],
                                        lhsT=ksb[base:base + 64, kt * 128:(kt + 1) * 128],
                                        rhs=q_pk[base:base + 64, cb + qg * 512: cb + (qg + 1) * 512],
                                        start=True, stop=True)
                                    if kt % 2 == 1:
                                        inst.then_inc(s_st.h, 1)
                            if pi == 1 and pend:
                                for e4 in pend:
                                    e4(t)
                                pend.clear()
                            if pi >= 2:
                                for h in heads:
                                    emit_av(h, 2 * pi - 4)
                                    emit_av(h, 2 * pi - 3)
                        for e4 in pend:
                            e4(t)
                        pend.clear()
                        for h in heads:
                            for kt in range(0 if npairs <= 2 else 2 * npairs - 4, bound):
                                emit_av(h, kt)
                    block.tensor(fD_t)
                    pending_e4 = []
                    s_st.c += 2 * npairs
                    s_av.c += 2 * bound
                    first_attn = False
                    for h in heads:
                        hi = 0 if h == heads[0] else 1
                        for kt in (bound - 2, bound - 1):
                            bank_free_at[(kt % 2) + 2 * hi] = exp_at[h][kt]

                    def fD_s(s, heads=heads, npairs=npairs, st_S0=st_S0,
                             exp_at=exp_at, bound=bound,
                             av_done_at=av_done_at):
                        for pi in range(npairs):
                            for h in heads:
                                hi = 0 if h == heads[0] else 1
                                kt0 = 2 * pi
                                wc.w(s, "act", s_st, st_S0 + 2 * pi + hi + 1)
                                if exp_at[h][kt0] <= 2:
                                    s.wait_ge(s_bias.h, 1)
                                if kt0 >= 4:
                                    wc.w(s, "act", s_av,
                                         av_done_at[h].get(kt0 - 3, 0))
                                bk2 = 2 * hi
                                s.activation(out=ptb[:, (4 * hi + kt0 % 4) * 512: (4 * hi + kt0 % 4) * 512 + 1024],
                                             in_=ring[:, bk2 * 1024 // 2 * 0 + bk2 * 512: bk2 * 512 + 1024],
                                             func=AF.Exp, bias=bias_c[:, h:h + 1]
                                             ).then_inc(s_exp.h, 1)
                    block.scalar(fD_s)

                    if causal:
                        def fD_vm(v, qg=qg, heads=heads, bound=bound,
                                  exp_at=exp_at, msk_at=msk_at):
                            if first_mask[0]:
                                v.wait_ge(s_cml.h, 16)
                                first_mask[0] = False
                            for h in heads:
                                hi = 0 if h == heads[0] else 1
                                for kt in range(bound - 4, bound):
                                    d = kt - 4 * qg
                                    sl = ((kt % 4) + 4 * hi) * 512
                                    wc.w(v, "dve", s_exp, exp_at[h][kt])
                                    v.tensor_tensor(out=ptb[:, sl:sl + 512],
                                                    in0=ptb[:, sl:sl + 512],
                                                    in1=cm_sb[:, d * 512:(d + 1) * 512],
                                                    op=ALU.mult).then_inc(s_msk.h, 1)
                        block.vector(fD_vm)

                    for h in heads:
                        hi = 0 if h == heads[0] else 1
                        base = 64 * (h % 2)
                        cb = (h // 2) * S
                        avb = av[hi]
                        e1_val = s_ep.c + 1
                        e2_val = s_ep.c + 2
                        e3_val = s_ep.c + 3
                        e4_val = s_ep.c + 4
                        e5_val = s_ep.c + 5

                        def fD_v(v, qg=qg, h=h, hi=hi, base=base, cb=cb, avb=avb,
                                 bound=bound, exp_at=dict(exp_at[h]),
                                 msk_at=dict(msk_at[h]),
                                 av_tot=av_done_at[h], e1_val=e1_val,
                                 e2_val=e2_val, e4_val=e4_val, e5_val=e5_val):
                            wc.w(v, "dve", s_av, av_tot[bound - 1])
                            v.tensor_copy(out=ltmp[:], in_=avb[64:96, :]).then_inc(s_ep.h, 1)
                            v.wait_ge(s_ep.h, e1_val)
                            v.tensor_scalar(out=linv[:], in0=ltmp[0:1, :], scalar1=LEPS,
                                            scalar2=None, op0=ALU.add).then_inc(s_ep.h, 1)
                            v.wait_ge(s_ep.h, e2_val)
                            v.reciprocal(out=linv2[:], in_=linv[:]).then_inc(s_ep.h, 1)
                            v.wait_ge(s_ep.h, e4_val)
                            v.tensor_copy(out=lb[:], in_=psb_[0:64, :]).then_inc(s_ep.h, 1)
                            v.wait_ge(s_ep.h, e5_val)
                            v.tensor_tensor(out=o_fm[base:base + 64, cb + qg * 512: cb + (qg + 1) * 512],
                                            in0=avb[0:64, :], in1=lb[:],
                                            op=ALU.mult).then_inc(s_on.h, 1)
                        block.vector(fD_v)
                        s_ep.c += 5
                        s_on.c += 1
                        av_free_at[hi] = s_on.c

                        def mk_e4(e3_val=e3_val):
                            def e4(t):
                                t.wait_ge(s_ep.h, e3_val)
                                t.matmul(out=psb_[0:64, :], lhsT=ones_r[0:1, 0:64],
                                         rhs=linv2[:], start=True, stop=True
                                         ).then_inc(s_ep.h, 1)
                            return e4
                        pending_e4.append(mk_e4())

            def fD_flush(t):
                for e4 in pending_e4:
                    e4(t)
                pending_e4.clear()
            block.tensor(fD_flush)

            ndma = [2]

            def phE(sy):
                sy.wait_ge(s_msc.h, 1)
                sy.wait_ge(s_cml.h, s_cml.c)
                sy.wait_ge(s_xin.h, s_xin.c)
                sy.wait_ge(s_wld.h, s_wld.c)
                sy.wait_ge(s_rt.h, s_rt.c)
                sy.wait_ge(s_on.h, s_on.c)
                sy.dma_start(out=o_out0, in_=o_fm[:, 0:S]).then_inc(s_out.h, 16)
                sy.dma_start(out=o_out1, in_=o_fm[:, S:2 * S]).then_inc(s_out.h, 16)
                if dump:
                    sy.wait_ge(s_bias.h, 1)
                    sy.dma_start(out=qdmp, in_=q_pk[:]).then_inc(s_out.h, 16)
                    sy.dma_start(out=kdmp, in_=ksb[:]).then_inc(s_out.h, 16)
                    sy.dma_start(out=vdmp, in_=vsb[:]).then_inc(s_out.h, 16)
                    sy.dma_start(out=bdmp, in_=bias_c[:]).then_inc(s_out.h, 16)
                    ndma[0] += 4
                sy.wait_ge(s_out.h, 16 * ndma[0])
            block.sync(phE)
            s_out.c += 16 * ndma[0]

    return nc



def build_block_tail(layer, ntb=4, dump=False, do_ffn=True):
    nc = bass.Bass("TRN2", target_bir_lowering=False, debug=False)
    o_in = nc.dram_tensor("o_in", [D, TPC], BF16, kind="ExternalInput").ap()
    resid_in = nc.dram_tensor("resid_in", [D, TPC], F32, kind="ExternalInput").ap()
    wo_d = nc.dram_tensor("wo", [D, D], F32, kind="ExternalInput").ap()
    s2_out = nc.dram_tensor("s2_out", [D, TPC], F32, kind="ExternalOutput").ap()
    if layer == 0:
        k0t_d = nc.dram_tensor("k0t", [D, 4096], U8, kind="ExternalInput").ap()
        k0b_d = nc.dram_tensor("k0b", [D, 4096], U8, kind="ExternalInput").ap()
        k1_d = nc.dram_tensor("k1", [4096, 4096], U8, kind="ExternalInput").ap()
        k2_d = nc.dram_tensor("k2", [4096, D], U8, kind="ExternalInput").ap()
        scl_d = nc.dram_tensor("scl", [1, 4], F32, kind="ExternalInput").ap()
    else:
        w1_d = nc.dram_tensor("w1t", [D, 4096], F32, kind="ExternalInput").ap()
        w2_d = nc.dram_tensor("w2t", [D, 4096], F32, kind="ExternalInput").ap()
        h1_d = nc.dram_tensor("h1t", [4096, 4096], F32, kind="ExternalInput").ap()
        h2_d = nc.dram_tensor("h2t", [4096, 4096], F32, kind="ExternalInput").ap()
        w3_d = nc.dram_tensor("w3t", [4096, D], F32, kind="ExternalInput").ap()
        sv1_d = nc.dram_tensor("sv1", [128, 32], F32, kind="ExternalInput").ap()
        sv2_d = nc.dram_tensor("sv2", [128, 32], F32, kind="ExternalInput").ap()
        sh1_d = nc.dram_tensor("sh1", [128, 32], F32, kind="ExternalInput").ap()
        sh2_d = nc.dram_tensor("sh2", [128, 32], F32, kind="ExternalInput").ap()
        sv3_d = nc.dram_tensor("sv3", [128, 8], F32, kind="ExternalInput").ap()
    if dump:
        s1_out = nc.dram_tensor("s1_out", [D, TPC], F32, kind="ExternalOutput").ap()
    s1_dram = nc.dram_tensor("s1_dram", [D, TPC], F32, kind="Internal").ap()

    ctx = ExitStack()
    with ctx:
        sb = lambda n, sh, dt: ctx.enter_context(nc.sbuf_tensor(n, sh, dt))
        psa = lambda n, sh: ctx.enter_context(nc.psum_tensor(n, sh, F32))
        sem = lambda n: Sem(ctx.enter_context(nc.semaphore(n)))

        bufA = sb("bufA", [128, 32 * 1024], BF16)
        bufB = sb("bufB", [128, 32 * 1024], BF16)
        rsm = sb("rsm", [128, 8 * 512], F32)
        xh = sb("xh", [128, 8 * 1024], BF16)
        wbig = sb("wbig", [128, 32 * 128], BF16)
        weff = sb("weff", [128, 8 * 512], BF16)
        wstg = sb("wstg", [128, 8 * 512], BF16)
        sb_corr = sb("sb_corr", [128, 1024], F32)
        sqv = sb("sqv", [128, 512], F32)
        if layer == 0:
            tmp2 = sb("tmp2_sb", [128, 1024], F32)
        rowm = sb("rowm", [1, 1536], F32)
        rb = sb("rb", [128, 512], F32)
        mb = sb("mb", [128, 512], F32)
        tmp1 = sb("tmp1", [128, 512], F32)
        ones_f = sb("ones_f", [128, 1], F32)
        ones_b = sb("ones_b", [128, 1], BF16)
        ones_r = sb("ones_r", [1, 128], F32)
        scl = sb("scl_sb", [128, 4], F32)
        scl4 = sb("scl4_sb", [128, 4], F32)
        if layer == 1:
            sv1 = sb("sv1_sb", [128, 32], F32)
            sv2 = sb("sv2_sb", [128, 32], F32)
            sh1 = sb("sh1_sb", [128, 32], F32)
            sh2 = sb("sh2_sb", [128, 32], F32)
            sv3 = sb("sv3_sb", [128, 8], F32)
            gtmp = sb("gtmp", [128, 1024], BF16)

        ring = psa("ring", [128, 2048])
        pb4 = psa("pb4", [128, 512])
        pb5 = psa("pb5", [128, 512])
        pb6 = psa("pb6", [128, 512])
        pb7 = psa("pb7", [128, 512])

        s_ldg = sem("s_ldg")
        s_lds = sem("s_lds")
        s_wo = sem("s_wo")
        s_ev = sem("s_ev")
        s_ch = sem("s_ch")
        s_msc = sem("s_msc")
        s_st1 = sem("s_st1")
        s_out = sem("s_out")

        with nc.Block() as block:
            def t0_g(g):
                g.dma_start(out=bufA[:, 0:16384].rearrange("p (c n) -> p c n", c=8),
                            in_=o_in.rearrange("(c p) n -> p c n", p=128)).then_inc(s_ldg.h, 16)
                g.dma_start(out=bufB[:, 0:8192].rearrange("p (c n) -> p c n", c=8),
                            in_=wo_d.rearrange("(c p) n -> p c n", p=128)).then_inc(s_ldg.h, 16)
            block.gpsimd(t0_g)
            s_ldg.c += 32

            def t0_s(sy):
                if layer == 0:
                    sy.dma_start(out=scl[0:1, :], in_=scl_d).then_inc(s_lds.h, 16)
                else:
                    sy.dma_start(out=sv1[:], in_=sv1_d).then_inc(s_lds.h, 16)
                    sy.dma_start(out=sv2[:], in_=sv2_d).then_inc(s_lds.h, 16)
                    sy.dma_start(out=sh1[:], in_=sh1_d).then_inc(s_lds.h, 16)
                    sy.dma_start(out=sh2[:], in_=sh2_d).then_inc(s_lds.h, 16)
                    sy.dma_start(out=sv3[:], in_=sv3_d).then_inc(s_lds.h, 16)
            block.sync(t0_s)
            s_lds.c += 16 if layer == 0 else 80

            def t0_v(v):
                v.memset(ones_f[:], 1.0)
                v.memset(ones_b[:], 1.0)
                v.memset(ones_r[:], 1.0).then_inc(s_msc.h, 1)
            block.vector(t0_v)
            s_msc.c += 1

            if layer == 0:
                def t0_t(t):
                    t.wait_ge(s_lds.h, s_lds.c)
                    t.wait_ge(s_msc.h, 1)
                    t.matmul(out=pb6[:, 0:4], lhsT=ones_r[:], rhs=scl[0:1, 0:4],
                             start=True, stop=True).then_inc(s_ch.h, 1)
                block.tensor(t0_t)
                s_ch.c += 1

                def t0_v2(v):
                    v.wait_ge(s_ch.h, s_ch.c)
                    v.tensor_scalar(out=scl[:], in0=pb6[:, 0:4], scalar1=2.0,
                                    scalar2=None, op0=ALU.mult).then_inc(s_ch.h, 1)
                block.vector(t0_v2)
                s_ch.c += 1

                def t0_v3(v):
                    v.wait_ge(s_ch.h, s_ch.c)
                    v.tensor_scalar(out=scl4[:], in0=scl[:], scalar1=2.0,
                                    scalar2=None, op0=ALU.mult).then_inc(s_ch.h, 1)
                block.vector(t0_v3)
                s_ch.c += 1

            def chain_t(emitfn):
                need = s_ch.c

                def f(t, need=need):
                    t.wait_ge(s_ch.h, need)
                    emitfn(t).then_inc(s_ch.h, 1)
                block.tensor(f)
                s_ch.c += 1

            def chain_v(emitfn):
                need = s_ch.c

                def f(v, need=need):
                    v.wait_ge(s_ch.h, need)
                    emitfn(v).then_inc(s_ch.h, 1)
                block.vector(f)
                s_ch.c += 1

            def chain_s(emitfn):
                need = s_ch.c

                def f(s, need=need):
                    s.wait_ge(s_ch.h, need)
                    emitfn(s).then_inc(s_ch.h, 1)
                block.scalar(f)
                s_ch.c += 1

            def emit_ln(src_resid, dst_xh, extra_pe_wait=None):
                def sum_mm(t):
                    if extra_pe_wait is not None:
                        extra_pe_wait(t)
                    for c in range(8):
                        inst = t.matmul(out=pb4[0:1, :], lhsT=ones_f[:],
                                        rhs=src_resid(c), start=(c == 0),
                                        stop=(c == 7))
                    return inst
                chain_t(sum_mm)
                for c in range(8):
                    chain_s(lambda s, c=c: s.activation(
                        out=sqv[:], in_=src_resid(c), func=AF.Square))
                    chain_t(lambda t, c=c: t.matmul(
                        out=pb5[0:1, :], lhsT=ones_f[:], rhs=sqv[:],
                        start=(c == 0), stop=(c == 7)))

                def v1(v):
                    v.tensor_scalar(out=rowm[0:1, 0:512], in0=pb4[0:1, :],
                                    scalar1=1.0 / D, scalar2=None, op0=ALU.mult)
                    return v.tensor_scalar(out=rowm[0:1, 512:1024], in0=pb5[0:1, :],
                                           scalar1=1.0 / D, scalar2=None,
                                           op0=ALU.mult)
                chain_v(v1)
                chain_s(lambda s: s.activation(out=rowm[0:1, 1024:1536],
                                               in_=rowm[0:1, 0:512], func=AF.Square))
                chain_v(lambda v: v.scalar_tensor_tensor(
                    out=rowm[0:1, 512:1024], in0=rowm[0:1, 512:1024], scalar=1e-5,
                    in1=rowm[0:1, 1024:1536], op0=ALU.add, op1=ALU.subtract))
                chain_s(lambda s: s.activation(out=rowm[0:1, 1024:1536],
                                               in_=rowm[0:1, 512:1024], func=AF.Ln))
                chain_s(lambda s: s.activation(out=rowm[0:1, 512:1024],
                                               in_=rowm[0:1, 1024:1536],
                                               func=AF.Exp, scale=-0.5))
                chain_v(lambda v: v.tensor_tensor(out=rowm[0:1, 1024:1536],
                                                  in0=rowm[0:1, 0:512],
                                                  in1=rowm[0:1, 512:1024],
                                                  op=ALU.mult))
                chain_v(lambda v: v.tensor_scalar(out=rowm[0:1, 1024:1536],
                                                  in0=rowm[0:1, 1024:1536],
                                                  scalar1=-1.0, scalar2=None,
                                                  op0=ALU.mult))

                def bc(t):
                    t.matmul(out=pb6[:, :], lhsT=ones_r[:],
                             rhs=rowm[0:1, 512:1024], start=True, stop=True)
                    return t.matmul(out=pb7[:, :], lhsT=ones_r[:],
                                    rhs=rowm[0:1, 1024:1536], start=True, stop=True)
                chain_t(bc)

                def cp(v):
                    v.tensor_copy(out=rb[:], in_=pb6[:, :])
                    return v.tensor_copy(out=mb[:], in_=pb7[:, :])
                chain_v(cp)
                for c in range(8):
                    chain_v(lambda v, c=c: v.tensor_tensor(
                        out=tmp1[:], in0=src_resid(c), in1=rb[:], op=ALU.mult))
                    chain_v(lambda v, c=c: v.tensor_tensor(
                        out=src_resid(c), in0=tmp1[:], in1=mb[:], op=ALU.add))
                    if dst_xh is not None:
                        chain_s(lambda s, c=c: s.activation(
                            out=dst_xh(c), in_=src_resid(c), func=AF.Copy))

            for tb in range(ntb):
                def t1_s(sy, tb=tb, prev=s_lds.c, ch_need=s_ch.c):
                    sy.wait_ge(s_lds.h, prev)
                    sy.wait_ge(s_ch.h, ch_need)
                    sy.dma_start(
                        out=rsm[:].rearrange("p (c n) -> p c n", c=8),
                        in_=resid_in[:, tb * 512:(tb + 1) * 512].rearrange("(c p) n -> p c n", p=128),
                    ).then_inc(s_lds.h, 16)
                block.sync(t1_s)
                s_lds.c += 16

                def t1_t(t, tb=tb):
                    if tb == 0:
                        t.wait_ge(s_ldg.h, 32)
                    for dc in range(8):
                        bk = dc % 4
                        if tb * 8 + dc >= 4:
                            t.wait_ge(s_ev.h, tb * 8 + dc - 3)
                        for k in range(8):
                            inst = t.matmul(
                                out=ring[:, bk * 512:(bk + 1) * 512],
                                lhsT=bufB[:, k * 1024 + dc * 128: k * 1024 + (dc + 1) * 128],
                                rhs=bufA[:, k * 2048 + tb * 512: k * 2048 + tb * 512 + 512],
                                start=(k == 0), stop=(k == 7))
                        inst.then_inc(s_wo.h, 1)
                block.tensor(t1_t)
                s_wo.c += 8

                def t1_v(v, tb=tb, lds_need=s_lds.c, ch_need=s_ch.c):
                    v.wait_ge(s_lds.h, lds_need)
                    v.wait_ge(s_ch.h, ch_need)
                    for dc in range(8):
                        bk = dc % 4
                        v.wait_ge(s_wo.h, tb * 8 + dc + 1)
                        v.tensor_tensor(out=rsm[:, dc * 512:(dc + 1) * 512],
                                        in0=rsm[:, dc * 512:(dc + 1) * 512],
                                        in1=ring[:, bk * 512:(bk + 1) * 512],
                                        op=ALU.add).then_inc(s_ev.h, 1)
                block.vector(t1_v)
                s_ev.c += 8

                ev_need = s_ev.c

                def pe_wait(t, ev_need=ev_need):
                    t.wait_ge(s_ev.h, ev_need)

                def src_resid(c):
                    return rsm[:, c * 512:(c + 1) * 512]

                emit_ln(src_resid, None, extra_pe_wait=pe_wait)

                def park_chain(sy, tb=tb, need=s_ch.c, prev=s_st1.c):
                    sy.wait_ge(s_ch.h, need)
                    sy.wait_ge(s_st1.h, prev)
                    sy.dma_start(
                        out=s1_dram[:, tb * 512:(tb + 1) * 512].rearrange("(c p) n -> p c n", p=128),
                        in_=rsm[:].rearrange("p (c n) -> p c n", c=8),
                    ).then_inc(s_st1.h, 16)
                    if dump:
                        sy.dma_start(
                            out=s1_out[:, tb * 512:(tb + 1) * 512].rearrange("(c p) n -> p c n", p=128),
                            in_=rsm[:].rearrange("p (c n) -> p c n", c=8),
                        ).then_inc(s_st1.h, 16)
                block.sync(park_chain)
                s_st1.c += 32 if dump else 16
                def park_guard(v, need=s_st1.c):
                    v.wait_ge(s_st1.h, need)
                    v.nop().then_inc(s_ch.h, 1)
                block.vector(park_guard)
                s_ch.c += 1

            s_fw = sem("s_fw")
            fwsem = [sem("s_fwA"), sem("s_fwB")]
            s_fm = sem("s_fm")
            s_ed = sem("s_ed")
            s_es = sem("s_es")
            wbig2 = wstg

            class Banks:
                def __init__(self):
                    self.aps = [ring[:, i * 512:(i + 1) * 512] for i in range(4)] +                                [pb4[:, :], pb5[:, :], pb6[:, :], pb7[:, :]]
                    self.freed = [(None, 0)] * 8

                def wait_free(self, t, b):
                    semx, cnt = self.freed[b]
                    if semx is not None and cnt > 0:
                        t.wait_ge(semx.h, cnt)

            bk = Banks()

            def pipe_mm_stage(nblocks, nbuf, emit_dma, n_dma, emit_mm_group,
                              emit_evac, pe_extra_first=None, mm_per_group=1,
                              groups_per_block=2):
                m_after_block = {}
                entry_waits = [(s_fm, s_fm.c), (s_ed, s_ed.c), (s_es, s_es.c)]
                for i in range(nblocks):
                    buf = i % nbuf

                    def f_g(g, i=i, buf=buf, prev_m=m_after_block.get(i - nbuf, 0),
                            prev_fw=fwsem[buf].c, entry_waits=entry_waits,
                            nbuf=nbuf):
                        if prev_fw > 0:
                            g.wait_ge(fwsem[buf].h, prev_fw)
                        if i < nbuf:
                            for semx, cnt in entry_waits:
                                if cnt > 0:
                                    g.wait_ge(semx.h, cnt)
                        if prev_m > 0:
                            g.wait_ge(s_fm.h, prev_m)
                        emit_dma(g, i, buf)
                    block.gpsimd(f_g)
                    fwsem[buf].c += 16 * n_dma
                    fw_need = (fwsem[buf], fwsem[buf].c)

                    for ts in range(groups_per_block):
                        b = (2 * (i % 2) + ts) % 8 if groups_per_block == 2 else (i % 4)

                        def f_t(t, i=i, buf=buf, ts=ts, b=b, fw_need=fw_need,
                                first=(i == 0 and ts == 0)):
                            t.wait_ge(fw_need[0].h, fw_need[1])
                            if first and pe_extra_first is not None:
                                pe_extra_first(t)
                            bk.wait_free(t, b)
                            emit_mm_group(t, i, buf, ts, bk.aps[b]).then_inc(s_fm.h, 1)
                        block.tensor(f_t)
                        s_fm.c += 1
                        bk.freed[b] = emit_evac(i, ts, b, s_fm.c)
                    m_after_block[i] = s_fm.c

            def evac_act(fn):
                def mk(i, ts, b, m_cnt):
                    def f(s, fn=fn, i=i, ts=ts, b=b, m_cnt=m_cnt):
                        s.wait_ge(s_fm.h, m_cnt)
                        fn(s, i, ts, bk.aps[b]).then_inc(s_es.h, 1)
                    block.scalar(f)
                    s_es.c += 1
                    return (s_es, s_es.c)
                return mk

            def evac_dve(fn):
                def mk(i, ts, b, m_cnt):
                    def f(v, fn=fn, i=i, ts=ts, b=b, m_cnt=m_cnt):
                        v.wait_ge(s_fm.h, m_cnt)
                        fn(v, i, ts, bk.aps[b]).then_inc(s_ed.h, 1)
                    block.vector(f)
                    s_ed.c += 1
                    return (s_ed, s_ed.c)
                return mk

            def stage_barrier(t):
                t.wait_ge(s_ed.h, s_ed.c)
                t.wait_ge(s_es.h, s_es.c)

            n_s2out = [0]
            if do_ffn:
                wo_done = s_wo.c
                park_all = s_st1.c
                lds_all = s_lds.c
                for th in range(2):
                    stage1_mm_after = [0]

                    def fA_g(g, th=th, st1=park_all, prev_m=stage1_mm_after[0],
                             prev_fw=s_fw.c):
                        g.wait_ge(s_st1.h, st1)
                        if prev_fw > 0:
                            g.wait_ge(s_fw.h, prev_fw)
                        if prev_m > 0:
                            g.wait_ge(s_fm.h, prev_m)
                        g.dma_start(
                            out=xh[:].rearrange("p (c n) -> p c n", c=8),
                            in_=s1_dram[:, th * 1024:(th + 1) * 1024].rearrange("(c p) n -> p c n", p=128),
                        ).then_inc(s_fw.h, 16)
                    block.gpsimd(fA_g)
                    s_fw.c += 16
                    xh_ready = s_fw.c

                    def pe_first(t, th=th, xh_ready=xh_ready, ch_now=s_ch.c,
                                 ed_now=s_ed.c, es_now=s_es.c):
                        t.wait_ge(s_fw.h, xh_ready)
                        t.wait_ge(s_ch.h, ch_now)
                        if th == 0:
                            t.wait_ge(s_wo.h, wo_done)
                            t.wait_ge(s_lds.h, lds_all)
                        else:
                            t.wait_ge(s_ed.h, ed_now)
                            t.wait_ge(s_es.h, es_now)
                    s_ch_entry = [s_ch.c]

                    if layer == 1:
                        def dma1(g, i, buf):
                            g.dma_start(
                                out=weff[:, buf * 2048: buf * 2048 + 1024].rearrange("p (c n) -> p c n", c=8),
                                in_=w1_d[:, i * 128:(i + 1) * 128].rearrange("(c p) n -> p c n", p=128),
                            ).then_inc(fwsem[buf].h, 16)
                            g.dma_start(
                                out=weff[:, buf * 2048 + 1024: buf * 2048 + 2048].rearrange("p (c n) -> p c n", c=8),
                                in_=w2_d[:, i * 128:(i + 1) * 128].rearrange("(c p) n -> p c n", p=128),
                            ).then_inc(fwsem[buf].h, 16)

                        def mm1(t, i, buf, ts, bank_ap):
                            bA = 2 * (i % 2) + ts
                            bB = 4 + bA
                            bk.wait_free(t, bB)
                            for k in range(8):
                                t.matmul(out=bk.aps[bA],
                                         lhsT=weff[:, buf * 2048 + k * 128: buf * 2048 + (k + 1) * 128],
                                         rhs=xh[:, k * 1024 + ts * 512: k * 1024 + ts * 512 + 512],
                                         start=(k == 0), stop=(k == 7))
                            for k in range(8):
                                inst = t.matmul(out=bk.aps[bB],
                                         lhsT=weff[:, buf * 2048 + 1024 + k * 128: buf * 2048 + 1024 + (k + 1) * 128],
                                         rhs=xh[:, k * 1024 + ts * 512: k * 1024 + ts * 512 + 512],
                                         start=(k == 0), stop=(k == 7))
                            return inst

                        gslot_free = {0: 0, 1: 0}

                        def evac1(i, ts, b, m_cnt):
                            bA = 2 * (i % 2) + ts
                            bB = 4 + bA
                            gslot = ts * 512

                            def f_s(s, i=i, gslot=gslot, bA=bA, m_cnt=m_cnt,
                                    war=gslot_free[ts]):
                                s.wait_ge(s_fm.h, m_cnt)
                                if war > 0:
                                    s.wait_ge(s_ed.h, war)
                                s.activation(out=gtmp[:, gslot:gslot + 512],
                                             in_=bk.aps[bA], func=AF.Silu,
                                             scale=sv1[:, i:i + 1]).then_inc(s_es.h, 1)
                            block.scalar(f_s)
                            s_es.c += 1
                            esc = s_es.c

                            def f_v(v, i=i, ts=ts, gslot=gslot, bB=bB, esc=esc):
                                v.wait_ge(s_es.h, esc)
                                v.scalar_tensor_tensor(
                                    out=bufA[:, i * 1024 + ts * 512: i * 1024 + ts * 512 + 512],
                                    in0=bk.aps[bB], scalar=sv2[:, i:i + 1],
                                    in1=gtmp[:, gslot:gslot + 512],
                                    op0=ALU.mult, op1=ALU.mult).then_inc(s_ed.h, 1)
                            block.vector(f_v)
                            s_ed.c += 1
                            gslot_free[ts] = s_ed.c
                            bk.freed[bA] = (s_es, esc)
                            return (s_ed, s_ed.c)

                        pipe_mm_stage(32, 2, dma1, 2, mm1, evac1,
                                      pe_extra_first=pe_first)
                        stage1_mm_after[0] = s_fm.c

                        def dma2(g, i, buf, wsrc=h1_d):
                            dst = wbig if buf == 0 else wbig2
                            g.dma_start(
                                out=dst[:].rearrange("p (c n) -> p c n", c=32),
                                in_=wsrc[:, i * 128:(i + 1) * 128].rearrange("(c p) n -> p c n", p=128),
                            ).then_inc(fwsem[buf].h, 16)

                        def mk_mm_big(src_buf):
                            def mm(t, i, buf, ts, bank_ap):
                                w = wbig if buf == 0 else wbig2
                                for k in range(32):
                                    inst = t.matmul(out=bank_ap,
                                             lhsT=w[:, k * 128:(k + 1) * 128],
                                             rhs=src_buf[:, k * 1024 + ts * 512: k * 1024 + ts * 512 + 512],
                                             start=(k == 0), stop=(k == 31))
                                return inst
                            return mm

                        def mk_gelu_evac(dst_buf, scol):
                            def fn(s, i, ts, bank_ap):
                                return s.activation(out=dst_buf[:, i * 1024 + ts * 512: i * 1024 + ts * 512 + 512],
                                             in_=bank_ap, func=AF.Gelu,
                                             scale=scol[:, i:i + 1])
                            return evac_act(fn)

                        def barrier_first(t):
                            stage_barrier(t)
                        pipe_mm_stage(32, 2, dma2, 1, mk_mm_big(bufA),
                                      mk_gelu_evac(bufB, sh1)(0, 0, 0, 0) if False else mk_gelu_evac(bufB, sh1),
                                      pe_extra_first=barrier_first)

                        def dma3(g, i, buf):
                            dst = wbig if buf == 0 else wbig2
                            g.dma_start(
                                out=dst[:].rearrange("p (c n) -> p c n", c=32),
                                in_=h2_d[:, i * 128:(i + 1) * 128].rearrange("(c p) n -> p c n", p=128),
                            ).then_inc(fwsem[buf].h, 16)
                        pipe_mm_stage(32, 2, dma3, 1, mk_mm_big(bufB),
                                      mk_gelu_evac(bufA, sh2),
                                      pe_extra_first=barrier_first)

                        for ts in range(2):
                            def s4_s(sy, th=th, ts=ts, prev=s_lds.c, need=s_ch.c,
                                     ed=s_ed.c, es=s_es.c):
                                sy.wait_ge(s_lds.h, prev)
                                sy.wait_ge(s_ch.h, need)
                                col = th * 1024 + ts * 512
                                sy.dma_start(
                                    out=rsm[:].rearrange("p (c n) -> p c n", c=8),
                                    in_=s1_dram[:, col:col + 512].rearrange("(c p) n -> p c n", p=128),
                                ).then_inc(s_lds.h, 16)
                            block.sync(s4_s)
                            s_lds.c += 16
                            lds_need = s_lds.c

                            def dma4(g, i, buf):
                                dst = wbig if buf == 0 else wbig2
                                g.dma_start(
                                    out=dst[:].rearrange("p (c n) -> p c n", c=32),
                                    in_=w3_d[:, i * 128:(i + 1) * 128].rearrange("(c p) n -> p c n", p=128),
                                ).then_inc(fwsem[buf].h, 16)

                            def mm4(t, i, buf, _ts, bank_ap, ts=ts,
                                    lds_need=lds_need):
                                if i == 0:
                                    t.wait_ge(s_lds.h, lds_need)
                                w = wbig if buf == 0 else wbig2
                                for k in range(32):
                                    inst = t.matmul(out=bank_ap,
                                             lhsT=w[:, k * 128:(k + 1) * 128],
                                             rhs=bufA[:, k * 1024 + ts * 512: k * 1024 + ts * 512 + 512],
                                             start=(k == 0), stop=(k == 31))
                                return inst

                            def ev4(v, i, _ts, bank_ap, ts=ts):
                                return v.scalar_tensor_tensor(
                                    out=rsm[:, i * 512:(i + 1) * 512],
                                    in0=bank_ap, scalar=sv3[:, i:i + 1],
                                    in1=rsm[:, i * 512:(i + 1) * 512],
                                    op0=ALU.mult, op1=ALU.add)
                            pipe_mm_stage(8, 2, dma4, 1, mm4, evac_dve(ev4),
                                          pe_extra_first=(barrier_first if ts == 0 else None),
                                          groups_per_block=1)

                            def lnent(t, ed=s_ed.c, es=s_es.c):
                                t.wait_ge(s_ed.h, ed)
                                t.wait_ge(s_es.h, es)
                            emit_ln(lambda c: rsm[:, c * 512:(c + 1) * 512], None,
                                    extra_pe_wait=lnent)

                            def s2out_s(sy, th=th, ts=ts, need=s_ch.c, prev=s_out.c):
                                sy.wait_ge(s_ch.h, need)
                                sy.wait_ge(s_out.h, prev)
                                col = th * 1024 + ts * 512
                                sy.dma_start(
                                    out=s2_out[:, col:col + 512].rearrange("(c p) n -> p c n", p=128),
                                    in_=rsm[:].rearrange("p (c n) -> p c n", c=8),
                                ).then_inc(s_out.h, 16)
                            block.sync(s2out_s)
                            s_out.c += 16
                            n_s2out[0] += 1

                            def s2out_guard(v, need=s_out.c):
                                v.wait_ge(s_out.h, need)
                                v.nop().then_inc(s_ch.h, 1)
                            block.vector(s2out_guard)
                            s_ch.c += 1

                    else:
                        def dma1(g, i, buf):
                            g.dma_start(
                                out=weff[:, buf * 2048: buf * 2048 + 2048].rearrange("p (c n) -> p c n", c=8),
                                in_=k0t_d[:, i * 256:(i + 1) * 256].rearrange("(c p) n -> p c n", p=128),
                            ).then_inc(fwsem[buf].h, 16)
                            g.dma_start(
                                out=wstg[:, buf * 2048: buf * 2048 + 2048].rearrange("p (c n) -> p c n", c=8),
                                in_=k0b_d[:, i * 256:(i + 1) * 256].rearrange("(c p) n -> p c n", p=128),
                            ).then_inc(fwsem[buf].h, 16)

                        sub_done = {}

                        def mm1(t, i, buf, ts, bank_ap, sub_done=sub_done):
                            t.wait_ge(s_ed.h, sub_done[i])
                            for d in range(2):
                                b2 = 2 * (i % 2) + d
                                if ts == 0:
                                    bk.wait_free(t, b2)
                                    bk2 = 4 + b2
                                    bk.wait_free(t, bk2)
                                for k in range(8):
                                    inst = t.matmul(
                                        out=bk.aps[2 * (i % 2) + d] if ts == 0 else bk.aps[4 + 2 * (i % 2) + d],
                                        lhsT=weff[:, buf * 2048 + k * 256 + d * 128: buf * 2048 + k * 256 + (d + 1) * 128],
                                        rhs=xh[:, k * 1024 + ts * 512: k * 1024 + ts * 512 + 512],
                                        start=(k == 0), stop=(k == 7))
                            return inst

                        def evac1(i, ts, b, m_cnt):
                            def f_v(v, i=i, ts=ts, m_cnt=m_cnt):
                                v.wait_ge(s_fm.h, m_cnt)
                                for d in range(2):
                                    bidx = (2 * (i % 2) + d) + (0 if ts == 0 else 4)
                                    dd = i * 2 + d
                                    inst = v.tensor_scalar(
                                        out=bufA[:, dd * 1024 + ts * 512: dd * 1024 + ts * 512 + 512],
                                        in0=bk.aps[bidx], scalar1=scl[:, 0:1],
                                        scalar2=0.0, op0=ALU.mult, op1=ALU.max)
                                inst.then_inc(s_ed.h, 1)
                            block.vector(f_v)
                            s_ed.c += 1
                            for d in range(2):
                                bidx = (2 * (i % 2) + d) + (0 if ts == 0 else 4)
                                bk.freed[bidx] = (s_ed, s_ed.c)
                            return (s_ed, s_ed.c)

                        m_after = {}
                        s1_entry = [(s_fm, s_fm.c), (s_ed, s_ed.c), (s_es, s_es.c)]
                        for i in range(16):
                            buf = i % 2

                            def f_g(g, i=i, buf=buf, prev_m=m_after.get(i - 2, 0),
                                    prev_fw=fwsem[buf].c, s1_entry=s1_entry):
                                if prev_fw > 0:
                                    g.wait_ge(fwsem[buf].h, prev_fw)
                                if i < 2:
                                    for semx, cnt in s1_entry:
                                        if cnt > 0:
                                            g.wait_ge(semx.h, cnt)
                                if prev_m > 0:
                                    g.wait_ge(s_fm.h, prev_m)
                                dma1(g, i, buf)
                            block.gpsimd(f_g)
                            fwsem[buf].c += 32
                            fw_need = (fwsem[buf], fwsem[buf].c)

                            def f_sub(v, i=i, buf=buf, fw_need=fw_need,
                                      prev_m=m_after.get(i - 2, 0)):
                                v.wait_ge(fw_need[0].h, fw_need[1])
                                if prev_m > 0:
                                    v.wait_ge(s_fm.h, prev_m)
                                v.tensor_tensor(out=weff[:, buf * 2048: buf * 2048 + 2048],
                                                in0=weff[:, buf * 2048: buf * 2048 + 2048],
                                                in1=wstg[:, buf * 2048: buf * 2048 + 2048],
                                                op=ALU.subtract).then_inc(s_ed.h, 1)
                            block.vector(f_sub)
                            s_ed.c += 1
                            sub_done[i] = s_ed.c

                            for ts in range(2):
                                def f_t(t, i=i, buf=buf, ts=ts,
                                        first=(i == 0 and ts == 0)):
                                    if first:
                                        pe_first(t)
                                    mm1(t, i, buf, ts, None).then_inc(s_fm.h, 1)
                                block.tensor(f_t)
                                s_fm.c += 1
                                evac1(i, ts, None, s_fm.c)
                            m_after[i] = s_fm.c
                        stage1_mm_after[0] = s_fm.c

                        def barrier_first(t):
                            stage_barrier(t)

                        for ts in range(2):
                            def srow_t(t, ts=ts, first=(ts == 0)):
                                if first:
                                    stage_barrier(t)
                                bk.wait_free(t, 4 + ts)
                                for k in range(32):
                                    inst = t.matmul(out=bk.aps[4 + ts][0:1, :],
                                                    lhsT=ones_b[:],
                                                    rhs=bufA[:, k * 1024 + ts * 512: k * 1024 + ts * 512 + 512],
                                                    start=(k == 0), stop=(k == 31))
                                inst.then_inc(s_fm.h, 1)
                            block.tensor(srow_t)
                            s_fm.c += 1

                            def srow_v(v, ts=ts, m_cnt=s_fm.c):
                                v.wait_ge(s_fm.h, m_cnt)
                                v.tensor_scalar(out=rowm[0:1, 0:512],
                                                in0=bk.aps[4 + ts][0:1, :],
                                                scalar1=scl[0:1, 1:2], scalar2=None,
                                                op0=ALU.mult).then_inc(s_ed.h, 1)
                            block.vector(srow_v)
                            s_ed.c += 1
                            bk.freed[4 + ts] = (s_ed, s_ed.c)

                            def srow_t2(t, ts=ts, e_cnt=s_ed.c):
                                t.wait_ge(s_ed.h, e_cnt)
                                bk.wait_free(t, 6 + ts)
                                t.matmul(out=bk.aps[6 + ts], lhsT=ones_r[:],
                                         rhs=rowm[0:1, 0:512], start=True,
                                         stop=True).then_inc(s_fm.h, 1)
                            block.tensor(srow_t2)
                            s_fm.c += 1

                            def srow_v2(v, ts=ts, m_cnt=s_fm.c):
                                v.wait_ge(s_fm.h, m_cnt)
                                v.tensor_copy(out=sb_corr[:, ts * 512:(ts + 1) * 512],
                                              in_=bk.aps[6 + ts]).then_inc(s_ed.h, 1)
                            block.vector(srow_v2)
                            s_ed.c += 1
                            bk.freed[6 + ts] = (s_ed, s_ed.c)

                        def dma2(g, i, buf):
                            dst = wbig if buf == 0 else wbig2
                            g.dma_start(
                                out=dst[:].rearrange("p (c n) -> p c n", c=32),
                                in_=k1_d[:, i * 128:(i + 1) * 128].rearrange("(c p) n -> p c n", p=128),
                            ).then_inc(fwsem[buf].h, 16)

                        def mk_mm_big(src_buf):
                            def mm(t, i, buf, ts, bank_ap):
                                w = wbig if buf == 0 else wbig2
                                for k in range(32):
                                    inst = t.matmul(out=bank_ap,
                                             lhsT=w[:, k * 128:(k + 1) * 128],
                                             rhs=src_buf[:, k * 1024 + ts * 512: k * 1024 + ts * 512 + 512],
                                             start=(k == 0), stop=(k == 31))
                                return inst
                            return mm

                        tslot_free = {0: 0, 1: 0}

                        def evac2(i, ts, b, m_cnt):
                            tslot = ts * 512

                            def f_v(v, i=i, ts=ts, b=b, m_cnt=m_cnt, tslot=tslot,
                                    war=tslot_free[ts]):
                                v.wait_ge(s_fm.h, m_cnt)
                                if war > 0:
                                    v.wait_ge(s_es.h, war)
                                v.scalar_tensor_tensor(
                                    out=tmp2[:, tslot:tslot + 512],
                                    in0=bk.aps[b], scalar=scl4[:, 1:2],
                                    in1=sb_corr[:, ts * 512:(ts + 1) * 512],
                                    op0=ALU.mult, op1=ALU.subtract).then_inc(s_ed.h, 1)
                            block.vector(f_v)
                            s_ed.c += 1
                            edc = s_ed.c

                            def f_s(s, i=i, ts=ts, edc=edc, tslot=tslot):
                                s.wait_ge(s_ed.h, edc)
                                s.activation(
                                    out=bufB[:, i * 1024 + ts * 512: i * 1024 + ts * 512 + 512],
                                    in_=tmp2[:, tslot:tslot + 512], func=AF.Relu).then_inc(s_es.h, 1)
                            block.scalar(f_s)
                            s_es.c += 1
                            tslot_free[ts] = s_es.c
                            return (s_ed, edc)
                        pipe_mm_stage(32, 2, dma2, 1, mk_mm_big(bufA), evac2)

                        for ts in range(2):
                            def srow_t(t, ts=ts, first=(ts == 0)):
                                if first:
                                    stage_barrier(t)
                                bk.wait_free(t, 4 + ts)
                                for k in range(32):
                                    inst = t.matmul(out=bk.aps[4 + ts][0:1, :],
                                                    lhsT=ones_b[:],
                                                    rhs=bufB[:, k * 1024 + ts * 512: k * 1024 + ts * 512 + 512],
                                                    start=(k == 0), stop=(k == 31))
                                inst.then_inc(s_fm.h, 1)
                            block.tensor(srow_t)
                            s_fm.c += 1

                            def srow_v(v, ts=ts, m_cnt=s_fm.c):
                                v.wait_ge(s_fm.h, m_cnt)
                                v.tensor_scalar(out=rowm[0:1, 0:512],
                                                in0=bk.aps[4 + ts][0:1, :],
                                                scalar1=scl[0:1, 2:3], scalar2=None,
                                                op0=ALU.mult).then_inc(s_ed.h, 1)
                            block.vector(srow_v)
                            s_ed.c += 1
                            bk.freed[4 + ts] = (s_ed, s_ed.c)

                            def srow_t2(t, ts=ts, e_cnt=s_ed.c):
                                t.wait_ge(s_ed.h, e_cnt)
                                bk.wait_free(t, 6 + ts)
                                t.matmul(out=bk.aps[6 + ts], lhsT=ones_r[:],
                                         rhs=rowm[0:1, 0:512], start=True,
                                         stop=True).then_inc(s_fm.h, 1)
                            block.tensor(srow_t2)
                            s_fm.c += 1

                            def srow_v2(v, ts=ts, m_cnt=s_fm.c):
                                v.wait_ge(s_fm.h, m_cnt)
                                v.tensor_copy(out=sb_corr[:, ts * 512:(ts + 1) * 512],
                                              in_=bk.aps[6 + ts]).then_inc(s_ed.h, 1)
                            block.vector(srow_v2)
                            s_ed.c += 1
                            bk.freed[6 + ts] = (s_ed, s_ed.c)

                        for ts in range(2):
                            def s3_s(sy, th=th, ts=ts, prev=s_lds.c, need=s_ch.c):
                                sy.wait_ge(s_lds.h, prev)
                                sy.wait_ge(s_ch.h, need)
                                col = th * 1024 + ts * 512
                                sy.dma_start(
                                    out=rsm[:].rearrange("p (c n) -> p c n", c=8),
                                    in_=s1_dram[:, col:col + 512].rearrange("(c p) n -> p c n", p=128),
                                ).then_inc(s_lds.h, 16)
                            block.sync(s3_s)
                            s_lds.c += 16
                            lds_need = s_lds.c

                            def dma3(g, i, buf):
                                dst = wbig if buf == 0 else wbig2
                                g.dma_start(
                                    out=dst[:].rearrange("p (c n) -> p c n", c=32),
                                    in_=k2_d[:, i * 128:(i + 1) * 128].rearrange("(c p) n -> p c n", p=128),
                                ).then_inc(fwsem[buf].h, 16)

                            def mm3(t, i, buf, _ts, bank_ap, ts=ts,
                                    lds_need=lds_need):
                                if i == 0:
                                    t.wait_ge(s_lds.h, lds_need)
                                w = wbig if buf == 0 else wbig2
                                for k in range(32):
                                    inst = t.matmul(out=bank_ap,
                                             lhsT=w[:, k * 128:(k + 1) * 128],
                                             rhs=bufB[:, k * 1024 + ts * 512: k * 1024 + ts * 512 + 512],
                                             start=(k == 0), stop=(k == 31))
                                return inst

                            def ev3(i, _ts, b, m_cnt, ts=ts):
                                def f_v(v, i=i, b=b, m_cnt=m_cnt, ts=ts,
                                        war=s_ed.c):
                                    v.wait_ge(s_fm.h, m_cnt)
                                    if war > 0:
                                        v.wait_ge(s_ed.h, war)
                                    v.scalar_tensor_tensor(
                                        out=tmp1[:], in0=bk.aps[b],
                                        scalar=scl4[:, 2:3],
                                        in1=sb_corr[:, ts * 512:(ts + 1) * 512],
                                        op0=ALU.mult, op1=ALU.subtract).then_inc(s_ed.h, 1)
                                block.vector(f_v)
                                s_ed.c += 1
                                edc = s_ed.c

                                def f_v2(v, i=i, edc=edc):
                                    v.wait_ge(s_ed.h, edc)
                                    v.tensor_tensor(
                                        out=rsm[:, i * 512:(i + 1) * 512],
                                        in0=rsm[:, i * 512:(i + 1) * 512],
                                        in1=tmp1[:], op=ALU.add).then_inc(s_ed.h, 1)
                                block.vector(f_v2)
                                s_ed.c += 1
                                return (s_ed, edc)
                            pipe_mm_stage(8, 2, dma3, 1, mm3, ev3,
                                          pe_extra_first=(barrier_first if ts == 0 else None),
                                          groups_per_block=1)

                            def lnent(t, ed=s_ed.c, es=s_es.c):
                                t.wait_ge(s_ed.h, ed)
                                t.wait_ge(s_es.h, es)
                            emit_ln(lambda c: rsm[:, c * 512:(c + 1) * 512], None,
                                    extra_pe_wait=lnent)

                            def s2out_s(sy, th=th, ts=ts, need=s_ch.c, prev=s_out.c):
                                sy.wait_ge(s_ch.h, need)
                                sy.wait_ge(s_out.h, prev)
                                col = th * 1024 + ts * 512
                                sy.dma_start(
                                    out=s2_out[:, col:col + 512].rearrange("(c p) n -> p c n", p=128),
                                    in_=rsm[:].rearrange("p (c n) -> p c n", c=8),
                                ).then_inc(s_out.h, 16)
                            block.sync(s2out_s)
                            s_out.c += 16
                            n_s2out[0] += 1

                            def s2out_guard(v, need=s_out.c):
                                v.wait_ge(s_out.h, need)
                                v.nop().then_inc(s_ch.h, 1)
                            block.vector(s2out_guard)
                            s_ch.c += 1

            def tE(sy):
                sy.wait_ge(s_st1.h, s_st1.c)
                sy.wait_ge(s_ch.h, s_ch.c)
                if not do_ffn:
                    sy.dma_start(out=s2_out.rearrange("(c p) n -> p c n", p=128),
                                 in_=s1_dram.rearrange("(c p) n -> p c n", p=128)
                                 ).then_inc(s_out.h, 16)
                    s_out.c += 16
                sy.wait_ge(s_out.h, s_out.c)
                sy.wait_ge(s_ldg.h, s_ldg.c)
                sy.wait_ge(s_lds.h, s_lds.c)
                sy.wait_ge(s_fw.h, s_fw.c)
                sy.wait_ge(fwsem[0].h, fwsem[0].c)
                sy.wait_ge(fwsem[1].h, fwsem[1].c)
            block.sync(tE)

    return nc



def _f32c(a):
    return np.ascontiguousarray(np.asarray(a, dtype=np.float32))


def make_cmask():
    cm = np.zeros((4, 128, 512), dtype=ml_dtypes.bfloat16)
    tri = np.triu(np.ones((128, 128), dtype=np.float32))
    for d in range(4):
        for st in range(4):
            if st > d:
                cm[d, :, st * 128:(st + 1) * 128] = 1.0
            elif st == d:
                cm[d, :, st * 128:(st + 1) * 128] = tri.astype(ml_dtypes.bfloat16)
    return cm


def stage_l0_attn(x, Wq, Wk, Wv):
    xT = np.ascontiguousarray(x.transpose(0, 2, 1))
    WqT = _f32c(np.asarray(Wq).T)
    WkT = _f32c(np.asarray(Wk).T)
    WvT = _f32c(np.asarray(Wv).T)
    cm = make_cmask()
    in_maps = []
    for core in range(NC):
        b, g = core // 4, core % 4
        wvz = np.zeros((D, 65), dtype=np.float32)
        wvz[:, 0:64] = WvT[:, 64 * g:64 * g + 64]
        in_maps.append({
            "xfm": np.ascontiguousarray(xT[b]),
            "wq": np.ascontiguousarray(WqT[:, 256 * g:256 * g + 256]),
            "wk": np.ascontiguousarray(WkT[:, 64 * g:64 * g + 64]),
            "wvz": wvz,
            "cmask": cm,
        })
    return in_maps


def gather_o(results):
    o_full = np.zeros((B, D, S), dtype=ml_dtypes.bfloat16)
    for core in range(NC):
        b, g = core // 4, core % 4
        o_full[b, 256 * g:256 * g + 128] = results[core]["o_out0"]
        o_full[b, 256 * g + 128:256 * g + 256] = results[core]["o_out1"]
    return o_full


def _rope_tables():
    inv = 1.0 / (10000.0 ** (np.arange(0, D, 2, dtype=np.float64) / D))
    t = np.arange(S, dtype=np.float64)[:, None]
    ang = t * inv[None, :]
    return np.cos(ang).astype(np.float32), np.sin(ang).astype(np.float32)


_EO = np.concatenate([np.arange(0, 64, 2), np.arange(1, 64, 2)])


def stage_l1_attn(xfm_full, Wq, Wk, Wv):
    Wq = _f32c(Wq)
    Wk = _f32c(Wk)
    Wv = _f32c(Wv)
    cosT, sinT = _rope_tables()
    in_maps = []
    for core in range(NC):
        b, g = core // 4, core % 4
        wq = np.zeros((D, 256), np.float32)
        for i in range(4):
            h = 4 * g + i
            wq[:, 64 * i:64 * i + 64] = Wq[:, 64 * h:64 * h + 64][:, _EO]
        wk = np.ascontiguousarray(Wk[:, 64 * g:64 * g + 64][:, _EO])
        wvz = np.zeros((D, 65), np.float32)
        wvz[:, 0:64] = Wv[:, 64 * g:64 * g + 64]
        rqc = np.zeros((128, 2 * S), np.float32)
        rqs = np.zeros((128, 2 * S), np.float32)
        for c in range(2):
            for half in range(2):
                h = 4 * g + 2 * c + half
                cf = cosT[:, 32 * h:32 * h + 32].T
                sf = sinT[:, 32 * h:32 * h + 32].T
                r0 = half * 64
                rqc[r0:r0 + 32, c * S:(c + 1) * S] = cf
                rqc[r0 + 32:r0 + 64, c * S:(c + 1) * S] = cf
                rqs[r0:r0 + 32, c * S:(c + 1) * S] = -sf
                rqs[r0 + 32:r0 + 64, c * S:(c + 1) * S] = sf
        rkc = np.zeros((64, S), np.float32)
        rks = np.zeros((64, S), np.float32)
        ckf = cosT[:, 0:32].T
        skf = sinT[:, 0:32].T
        rkc[0:32] = ckf
        rkc[32:64] = ckf
        rks[0:32] = -skf
        rks[32:64] = skf
        in_maps.append({
            "xfm": np.ascontiguousarray(xfm_full[b]),
            "wq": wq, "wk": wk, "wvz": wvz,
            "rqc": rqc, "rqs": rqs, "rkc": rkc, "rks": rks,
        })
    return in_maps


def stage_l0_tail(o_full, x, Wo, k0, k1, k2, s0, s1, s2):
    xT = np.ascontiguousarray(x.transpose(0, 2, 1))
    WoT = _f32c(np.asarray(Wo).T)
    k0 = np.asarray(k0)
    k0t = np.ascontiguousarray(k0[0:D]).astype(np.uint8)
    k0b = np.ascontiguousarray(k0[D:2 * D]).astype(np.uint8)
    k1u = np.ascontiguousarray(np.asarray(k1)).astype(np.uint8)
    k2u = np.ascontiguousarray(np.asarray(k2)).astype(np.uint8)
    scl = np.array([[float(s0), float(s1), float(s2), 0.0]], np.float32)
    in_maps = []
    for core in range(NC):
        b, j = core // 4, core % 4
        in_maps.append({
            "o_in": np.ascontiguousarray(o_full[b][:, TPC * j:TPC * (j + 1)]),
            "resid_in": np.ascontiguousarray(xT[b][:, TPC * j:TPC * (j + 1)]),
            "wo": WoT,
            "k0t": k0t, "k0b": k0b, "k1": k1u, "k2": k2u, "scl": scl,
        })
    return in_maps


def gather_stream(results, key="s2_out"):
    s = np.zeros((B, D, S), dtype=np.float32)
    for core in range(NC):
        b, j = core // 4, core % 4
        s[b][:, TPC * j:TPC * (j + 1)] = results[core][key]
    return s


def run_layer0(x, inp):
    nc1 = build_attn_core(layer=0)
    r1 = run_bass_kernel_spmd(nc1, stage_l0_attn(
        x, inp["L0_attn_Wq"], inp["L0_attn_Wk"], inp["L0_attn_Wv"]),
        core_ids=list(range(NC)))
    o_full = gather_o(r1.results)
    nc2 = build_block_tail(layer=0)
    r2 = run_bass_kernel_spmd(nc2, stage_l0_tail(
        o_full, x, inp["L0_attn_Wo"], inp["L0_ffn_k0"], inp["L0_ffn_k1"],
        inp["L0_ffn_k2"], inp["L0_ffn_s0"], inp["L0_ffn_s1"], inp["L0_ffn_s2"]),
        core_ids=list(range(NC)))
    return gather_stream(r2.results)


def stage_l1_tail(o_full, stream1, inp):
    WoT = _f32c(inp["L1_attn_Wo"])
    w1t = _f32c(np.asarray(inp["L1_ffn_w1_w"]).T)
    w2t = _f32c(np.asarray(inp["L1_ffn_w2_w"]).T)
    h1t = _f32c(np.asarray(inp["L1_ffn_h1_w"]).T)
    h2t = _f32c(np.asarray(inp["L1_ffn_h2_w"]).T)
    w3t = _f32c(np.asarray(inp["L1_ffn_w3_w"]).T)
    pmj = lambda v: np.ascontiguousarray(np.asarray(v, np.float32).reshape(-1, 128).T)
    sv1 = pmj(inp["L1_ffn_w1_s"])
    sv2 = pmj(inp["L1_ffn_w2_s"])
    sh1 = pmj(inp["L1_ffn_h1_s"])
    sh2 = pmj(inp["L1_ffn_h2_s"])
    sv3 = pmj(inp["L1_ffn_w3_s"])
    in_maps = []
    for core in range(NC):
        b, j = core // 4, core % 4
        in_maps.append({
            "o_in": np.ascontiguousarray(o_full[b][:, TPC * j:TPC * (j + 1)]),
            "resid_in": np.ascontiguousarray(stream1[b][:, TPC * j:TPC * (j + 1)]),
            "wo": WoT,
            "w1t": w1t, "w2t": w2t, "h1t": h1t, "h2t": h2t, "w3t": w3t,
            "sv1": sv1, "sv2": sv2, "sh1": sh1, "sh2": sh2, "sv3": sv3,
        })
    return in_maps


_BUILT = {}


def _get(builder, key, *args, **kw):
    if key not in _BUILT:
        _BUILT[key] = builder(*args, **kw)
    return _BUILT[key]


def kernel(**inputs):
    x = np.asarray(inputs["L0_attn_hidden_states"], np.float32)
    cores = list(range(NC))

    nc = _get(build_attn_core, "a0", layer=0)
    r = run_bass_kernel_spmd(nc, stage_l0_attn(
        x, inputs["L0_attn_Wq"], inputs["L0_attn_Wk"], inputs["L0_attn_Wv"]),
        core_ids=cores)
    o_full = gather_o(r.results)
    nc = _get(build_block_tail, "t0", layer=0)
    r = run_bass_kernel_spmd(nc, stage_l0_tail(
        o_full, x, inputs["L0_attn_Wo"], inputs["L0_ffn_k0"], inputs["L0_ffn_k1"],
        inputs["L0_ffn_k2"], inputs["L0_ffn_s0"], inputs["L0_ffn_s1"],
        inputs["L0_ffn_s2"]), core_ids=cores)
    s2 = gather_stream(r.results)

    nc = _get(build_attn_core, "a1", layer=1)
    r = run_bass_kernel_spmd(nc, stage_l1_attn(
        s2, inputs["L1_attn_Wq"], inputs["L1_attn_Wk"], inputs["L1_attn_Wv"]),
        core_ids=cores)
    o_full = gather_o(r.results)
    nc = _get(build_block_tail, "t1", layer=1)
    r = run_bass_kernel_spmd(nc, stage_l1_tail(o_full, s2, inputs), core_ids=cores)
    s4 = gather_stream(r.results)

    return np.ascontiguousarray(s4.transpose(0, 2, 1))


# version 04
# speedup vs baseline: 1.0466x; 1.0466x over previous
import numpy as np
import ml_dtypes
from contextlib import ExitStack

import concourse.bass as bass
import concourse.mybir as mybir
from concourse.bass_utils import run_bass_kernel_spmd

F32 = mybir.dt.float32
BF16 = mybir.dt.bfloat16
U8 = mybir.dt.uint8
AF = mybir.ActivationFunctionType
ALU = mybir.AluOpType
AX = mybir.AxisListType

B, S, D = 2, 8192, 1024
NC = 8
TPC = 2048
NH, DH, KVH = 16, 64, 4
NKT = S // 128
NQG = S // 512
SHIFT = 44.0
LEPS = 1e-30
NCH = 16


class Sem:
    def __init__(self, h):
        self.h = h
        self.c = 0



def build_attn_core(layer, nqg=None, dump=False):
    causal = (layer == 0)
    qscale = (DH ** -0.5) if layer == 0 else (D ** -0.5)
    if nqg is None:
        nqg = NQG

    nc = bass.Bass("TRN2", target_bir_lowering=False, debug=False)
    xfm = nc.dram_tensor("xfm", [D, S], F32, kind="ExternalInput").ap()
    wq = nc.dram_tensor("wq", [D, 256], F32, kind="ExternalInput").ap()
    wk = nc.dram_tensor("wk", [D, 64], F32, kind="ExternalInput").ap()
    wvz = nc.dram_tensor("wvz", [D, 65], F32, kind="ExternalInput").ap()
    o_out0 = nc.dram_tensor("o_out0", [128, S], BF16, kind="ExternalOutput").ap()
    o_out1 = nc.dram_tensor("o_out1", [128, S], BF16, kind="ExternalOutput").ap()
    if causal:
        cmask_d = nc.dram_tensor("cmask", [4, 128, 512], BF16, kind="ExternalInput").ap()
    else:
        rqc_d = nc.dram_tensor("rqc", [128, 2 * S], F32, kind="ExternalInput").ap()
        rqs_d = nc.dram_tensor("rqs", [128, 2 * S], F32, kind="ExternalInput").ap()
        rkc_d = nc.dram_tensor("rkc", [64, S], F32, kind="ExternalInput").ap()
        rks_d = nc.dram_tensor("rks", [64, S], F32, kind="ExternalInput").ap()
    if dump:
        qdmp = nc.dram_tensor("qdmp", [128, 2 * S], BF16, kind="ExternalOutput").ap()
        kdmp = nc.dram_tensor("kdmp", [128, S], BF16, kind="ExternalOutput").ap()
        vdmp = nc.dram_tensor("vdmp", [128, NKT * 65], BF16, kind="ExternalOutput").ap()
        bdmp = nc.dram_tensor("bdmp", [128, 4], F32, kind="ExternalOutput").ap()

    ctx = ExitStack()
    with ctx:
        sb = lambda n, sh, dt: ctx.enter_context(nc.sbuf_tensor(n, sh, dt))
        psa = lambda n, sh: ctx.enter_context(nc.psum_tensor(n, sh, F32))
        sem = lambda n: Sem(ctx.enter_context(nc.semaphore(n)))

        q_pk = sb("q_pk", [128, 2 * S], BF16)
        ksb = sb("ksb", [128, S], BF16)
        vsb = sb("vsb", [128, NKT * 65], BF16)
        o_fm = sb("o_fm", [128, 2 * S], BF16)
        xin = sb("xin", [128, 2 * 4096], BF16)
        ptb = sb("ptb", [128, 8 * 512], BF16)
        wq_sb = sb("wq_sb", [128, 8 * 256], BF16)
        wk_sb = sb("wk_sb", [128, 8 * 64], BF16)
        wv_sb = sb("wv_sb", [128, 8 * 65], BF16)
        qmx = sb("qmx", [128, 8], F32)
        sqmx = sb("sqmx", [128, 8], F32)
        ltmp = sb("ltmp", [32, 512], F32)
        linv = sb("linv", [1, 512], F32)
        linv2 = sb("linv2", [1, 512], F32)
        lb = sb("lb", [64, 512], F32)
        ones_f = sb("ones_f", [128, 1], F32)
        ones_r = sb("ones_r", [1, 128], F32)
        stat = sb("stat", [1, 16], F32)
        stat2 = sb("stat2", [1, 16], F32)
        bias_c = sb("bias_c", [128, 4], F32)
        if causal:
            cm_sb = sb("cm_sb", [128, 4 * 512], BF16)
        else:
            q32 = sb("q32", [128, 512], F32)
            qsw = sb("qsw", [128, 512], F32)
            qt1 = sb("qt1", [128, 512], F32)
            qt2 = sb("qt2", [128, 512], F32)
            rtc = sb("rtc", [128, 2 * 512], F32)
            rts = sb("rts", [128, 2 * 512], F32)
            ktc = sb("ktc", [64, 512], F32)
            kts = sb("kts", [64, 512], F32)

        ring = psa("ring", [128, 2048])
        av = [psa("av0", [128, 512]), psa("av1", [128, 512])]
        psb_ = psa("psb", [128, 512])
        pst = psa("pst", [128, 512])

        s_wld = sem("s_wld")
        s_xin = sem("s_xin")
        s_xfree = sem("s_xfree")
        s_pmm = sem("s_pmm")
        s_pev = sem("s_pev")
        s_pevv = sem("s_pevv")
        s_st = sem("s_st")
        s_exp = sem("s_exp")
        s_msk = sem("s_msk")
        s_av = sem("s_av")
        s_ep = sem("s_ep")
        s_on = sem("s_on")
        s_bias = sem("s_bias")
        s_msc = sem("s_msc")
        s_out = sem("s_out")
        s_cml = sem("s_cml")

        with nc.Block() as block:
            def ph0_g(g):
                g.dma_start(out=wq_sb[:].rearrange("p (c n) -> p c n", c=8),
                            in_=wq.rearrange("(c p) n -> p c n", p=128)).then_inc(s_wld.h, 16)
                g.dma_start(out=wk_sb[:].rearrange("p (c n) -> p c n", c=8),
                            in_=wk.rearrange("(c p) n -> p c n", p=128)).then_inc(s_wld.h, 16)
                g.dma_start(out=wv_sb[:].rearrange("p (c n) -> p c n", c=8),
                            in_=wvz.rearrange("(c p) n -> p c n", p=128)).then_inc(s_wld.h, 16)
            block.gpsimd(ph0_g)
            s_wld.c += 48

            if causal:
                def ph0_s(sy):
                    sy.dma_start(out=cm_sb[:].rearrange("p (d n) -> p d n", d=4),
                                 in_=cmask_d.rearrange("d p n -> p d n")).then_inc(s_cml.h, 16)
                block.sync(ph0_s)
                s_cml.c += 16

            def ph0_v(v):
                v.memset(o_fm[:], 0.0)
                v.memset(ones_f[:], 1.0)
                v.memset(ones_r[:], 1.0)
                v.memset(stat[:], 0.0)
                v.memset(qmx[:], 0.0)
                v.memset(stat2[:], 0.0)
                v.memset(av[0][64:96, :], 0.0)
                v.memset(av[1][64:96, :], 0.0)
                v.memset(vsb[:, 64::65], 1.0).then_inc(s_msc.h, 1)
            block.vector(ph0_v)
            s_msc.c += 1

            s_rt = sem("s_rt")
            s_rp = sem("s_rp")
            rope_guard = [0]
            rope_end = {}
            rope_sub = {}
            for kc in range(NCH):
                xo = (kc % 2) * 4096

                def fB_g(g, kc=kc, xo=xo):
                    if kc >= 1:
                        g.wait_ge(s_xin.h, 16 * kc)
                    if kc >= 2:
                        if causal:
                            g.wait_ge(s_pev.h, 2 * (kc - 1))
                            g.wait_ge(s_pevv.h, kc - 1)
                        else:
                            g.wait_ge(s_rp.h, rope_end[kc - 2])
                    g.dma_start(
                        out=xin[:, xo:xo + 4096].rearrange("p (c t) -> p c t", c=8),
                        in_=xfm[:, kc * 512:(kc + 1) * 512].rearrange("(c p) t -> p c t", p=128),
                    ).then_inc(s_xin.h, 16)
                block.gpsimd(fB_g)
                s_xin.c += 16

                def fB_t(t, kc=kc, xo=xo):
                    t.wait_ge(s_xin.h, 16 * (kc + 1))
                    if kc == 0:
                        t.wait_ge(s_wld.h, 48)
                    else:
                        if causal:
                            t.wait_ge(s_pev.h, 2 * kc)
                            t.wait_ge(s_pevv.h, kc)
                        else:
                            t.wait_ge(s_rp.h, rope_sub[kc - 1][0])
                    for c in range(2):
                        if (not causal) and kc > 0 and c == 1:
                            t.wait_ge(s_rp.h, rope_sub[kc - 1][1])
                        for k in range(8):
                            inst = t.matmul(out=ring[:, c * 512:(c + 1) * 512],
                                     lhsT=wq_sb[:, k * 256 + c * 128: k * 256 + (c + 1) * 128],
                                     rhs=xin[:, xo + k * 512: xo + (k + 1) * 512],
                                     start=(k == 0), stop=(k == 7))
                            if c == 1 and k == 7:
                                inst.then_inc(s_pmm.h, 1)
                    if (not causal) and kc > 0:
                        t.wait_ge(s_rp.h, rope_sub[kc - 1][2])
                    for k in range(8):
                        inst = t.matmul(out=ring[0:64, 1024:1536],
                                 lhsT=wk_sb[:, k * 64:(k + 1) * 64],
                                 rhs=xin[:, xo + k * 512: xo + (k + 1) * 512],
                                 start=(k == 0), stop=(k == 7))
                        if k == 7:
                            inst.then_inc(s_pmm.h, 1)
                    if (not causal) and kc > 0:
                        t.wait_ge(s_rp.h, rope_sub[kc - 1][3])
                    for kt in range(4):
                        for k in range(8):
                            inst = t.matmul(out=ring[:, 1536 + kt * 65: 1536 + kt * 65 + 65],
                                     lhsT=xin[:, xo + k * 512 + kt * 128: xo + k * 512 + (kt + 1) * 128],
                                     rhs=wv_sb[:, k * 65:(k + 1) * 65],
                                     start=(k == 0), stop=(k == 7))
                            if kt == 3 and k == 7:
                                inst.then_inc(s_pmm.h, 1)
                block.tensor(fB_t)
                s_pmm.c += 3

                if causal:
                    def fB_s(s, kc=kc):
                        s.wait_ge(s_pmm.h, 3 * kc + 1)
                        for c in range(2):
                            inst = s.activation(out=q_pk[:, c * S + kc * 512: c * S + (kc + 1) * 512],
                                         in_=ring[:, c * 512:(c + 1) * 512],
                                         func=AF.Copy, scale=float(qscale))
                            if c == 1:
                                inst.then_inc(s_pev.h, 1)
                        s.wait_ge(s_pmm.h, 3 * kc + 3)
                        for kt in range(4):
                            inst = s.activation(out=vsb[:, (kc * 4 + kt) * 65: (kc * 4 + kt) * 65 + 64],
                                         in_=ring[:, 1536 + kt * 65: 1536 + kt * 65 + 64],
                                         func=AF.Copy)
                            if kt == 3:
                                inst.then_inc(s_pev.h, 1)
                    block.scalar(fB_s)
                    s_pev.c += 2

                    def fB_v(v, kc=kc):
                        v.wait_ge(s_pmm.h, 3 * kc + 2)
                        v.tensor_copy(out=ksb[0:64, kc * 512:(kc + 1) * 512], in_=ring[0:64, 1024:1536])
                        v.tensor_copy(out=ksb[64:128, kc * 512:(kc + 1) * 512],
                                      in_=ring[0:64, 1024:1536]).then_inc(s_pevv.h, 1)
                    block.vector(fB_v)
                    s_pevv.c += 1
                else:
                    def rt_s(sy, kc=kc, prev=s_rt.c, guard=rope_guard[0]):
                        sy.wait_ge(s_rt.h, prev)
                        if guard > 0:
                            sy.wait_ge(s_rp.h, guard)
                        for c in range(2):
                            sy.dma_start(out=rtc[:, c * 512:(c + 1) * 512],
                                         in_=rqc_d[:, c * S + kc * 512: c * S + (kc + 1) * 512]).then_inc(s_rt.h, 16)
                            sy.dma_start(out=rts[:, c * 512:(c + 1) * 512],
                                         in_=rqs_d[:, c * S + kc * 512: c * S + (kc + 1) * 512]).then_inc(s_rt.h, 16)
                        sy.dma_start(out=ktc[:], in_=rkc_d[:, kc * 512:(kc + 1) * 512]).then_inc(s_rt.h, 16)
                        sy.dma_start(out=kts[:], in_=rks_d[:, kc * 512:(kc + 1) * 512]).then_inc(s_rt.h, 16)
                    block.sync(rt_s)
                    s_rt.c += 96
                    rt_need = s_rt.c

                    def rchain_s(emitfn):
                        need = s_rp.c

                        def f(s, need=need):
                            s.wait_ge(s_rp.h, need)
                            emitfn(s).then_inc(s_rp.h, 1)
                        block.scalar(f)
                        s_rp.c += 1

                    def rchain_v(emitfn):
                        need = s_rp.c

                        def f(v, need=need):
                            v.wait_ge(s_rp.h, need)
                            emitfn(v).then_inc(s_rp.h, 1)
                        block.vector(f)
                        s_rp.c += 1

                    _sub = []
                    for c in range(2):
                        def ev_q(s, kc=kc, c=c, rt_need=rt_need):
                            s.wait_ge(s_pmm.h, 3 * kc + 1)
                            s.wait_ge(s_rt.h, rt_need)
                            return s.activation(out=q32[:], in_=ring[:, c * 512:(c + 1) * 512],
                                                func=AF.Copy)
                        rchain_s(ev_q)
                        _sub.append(s_rp.c)

                        def swp(v, c=c):
                            v.tensor_copy(out=qsw[0:32, :], in_=q32[32:64, :])
                            v.tensor_copy(out=qsw[32:64, :], in_=q32[0:32, :])
                            v.tensor_copy(out=qsw[64:96, :], in_=q32[96:128, :])
                            v.tensor_copy(out=qsw[96:128, :], in_=q32[64:96, :])
                            return v.tensor_tensor(out=qt1[:], in0=q32[:],
                                                   in1=rtc[:, c * 512:(c + 1) * 512],
                                                   op=ALU.mult)
                        rchain_v(swp)
                        rchain_v(lambda v, c=c: v.tensor_tensor(
                            out=qt2[:], in0=qsw[:], in1=rts[:, c * 512:(c + 1) * 512],
                            op=ALU.mult))
                        rchain_v(lambda v: v.tensor_tensor(
                            out=qt1[:], in0=qt1[:], in1=qt2[:], op=ALU.add))
                        rchain_s(lambda s, kc=kc, c=c: s.activation(
                            out=q_pk[:, c * S + kc * 512: c * S + (kc + 1) * 512],
                            in_=qt1[:], func=AF.Copy, scale=float(qscale)))
                    def ev_k(s, kc=kc):
                        s.wait_ge(s_pmm.h, 3 * kc + 2)
                        return s.activation(out=q32[0:64, :], in_=ring[0:64, 1024:1536],
                                            func=AF.Copy)
                    rchain_s(ev_k)
                    _sub.append(s_rp.c)

                    def swpk(v):
                        v.tensor_copy(out=qsw[0:32, :], in_=q32[32:64, :])
                        v.tensor_copy(out=qsw[32:64, :], in_=q32[0:32, :])
                        return v.tensor_tensor(out=qt1[0:64, :], in0=q32[0:64, :],
                                               in1=ktc[:], op=ALU.mult)
                    rchain_v(swpk)
                    rchain_v(lambda v: v.tensor_tensor(
                        out=qt2[0:64, :], in0=qsw[0:64, :], in1=kts[:], op=ALU.mult))
                    rchain_v(lambda v: v.tensor_tensor(
                        out=qt1[0:64, :], in0=qt1[0:64, :], in1=qt2[0:64, :], op=ALU.add))
                    rchain_s(lambda s, kc=kc: s.activation(
                        out=ksb[0:64, kc * 512:(kc + 1) * 512], in_=qt1[0:64, :],
                        func=AF.Copy))
                    rchain_v(lambda v, kc=kc: v.tensor_copy(
                        out=ksb[64:128, kc * 512:(kc + 1) * 512],
                        in_=ksb[0:64, kc * 512:(kc + 1) * 512]))
                    def ev_v(s, kc=kc):
                        s.wait_ge(s_pmm.h, 3 * kc + 3)
                        for kt in range(4):
                            inst = s.activation(out=vsb[:, (kc * 4 + kt) * 65: (kc * 4 + kt) * 65 + 64],
                                         in_=ring[:, 1536 + kt * 65: 1536 + kt * 65 + 64],
                                         func=AF.Copy)
                        return inst
                    rchain_s(ev_v)
                    _sub.append(s_rp.c)
                    rope_guard[0] = s_rp.c
                    rope_end[kc] = s_rp.c
                    rope_sub[kc] = tuple(_sub)
            s_sq = sem("s_sq")
            s_qn = sem("s_qn")
            s_qnv = sem("s_qnv")
            sqq = sb("sqq", [128, 2 * 512], F32)
            sqk = sb("sqk", [64, 2 * 512], F32)
            qmaxrow = sb("qmaxrow", [1, 80], F32)
            statbank = [pst, psb_]

            sqq2 = sb("sqq2", [128, 2 * 512], F32)

            def phC_act(s):
                for kc in range(NCH):
                    if kc >= 2:
                        s.wait_ge(s_qn.h, 5 * (kc - 1))
                    if kc == 0:
                        s.wait_ge(s_pev.h, s_pev.c)
                        s.wait_ge(s_pevv.h, s_pevv.c)
                        if not causal:
                            s.wait_ge(s_rp.h, s_rp.c)
                    bo = (kc % 2) * 512
                    s.activation(out=sqq[:, bo:bo + 512],
                                 in_=q_pk[:, kc * 512:kc * 512 + 512],
                                 func=AF.Square).then_inc(s_sq.h, 1)
                    s.activation(out=sqq2[:, bo:bo + 512],
                                 in_=q_pk[:, S + kc * 512: S + kc * 512 + 512],
                                 func=AF.Square).then_inc(s_sq.h, 1)
                    s.activation(out=sqk[:, bo:bo + 512],
                                 in_=ksb[0:64, kc * 512:kc * 512 + 512],
                                 func=AF.Square).then_inc(s_sq.h, 1)
            block.scalar(phC_act)
            s_sq.c += 3 * NCH

            def phC_pe(t):
                for kc in range(NCH):
                    bo = (kc % 2) * 512
                    for j in range(5):
                        mi = kc * 5 + j
                        if mi >= 2:
                            t.wait_ge(s_qnv.h, mi - 1)
                        if j < 4:
                            c, half = j // 2, j % 2
                            t.wait_ge(s_sq.h, 3 * kc + c + 1)
                            src_ = (sqq if c == 0 else sqq2)
                            base = half * 64
                            t.matmul(out=statbank[mi % 2][0:1, 0:512],
                                     lhsT=ones_f[base:base + 64, :],
                                     rhs=src_[base:base + 64, bo:bo + 512],
                                     start=True, stop=True).then_inc(s_qn.h, 1)
                        else:
                            t.wait_ge(s_sq.h, 3 * kc + 3)
                            t.matmul(out=statbank[mi % 2][0:1, 0:512],
                                     lhsT=ones_f[0:64, :],
                                     rhs=sqk[:, bo:bo + 512],
                                     start=True, stop=True).then_inc(s_qn.h, 1)
            block.tensor(phC_pe)
            s_qn.c += 5 * NCH

            def phC_dve(v):
                for kc in range(NCH):
                    for j in range(5):
                        mi = kc * 5 + j
                        v.wait_ge(s_qn.h, mi + 1)
                        if j < 4:
                            h = 2 * (j // 2) + (j % 2)
                            col = h * 16 + kc
                        else:
                            col = 64 + kc
                        v.reduce_max(out=qmaxrow[0:1, col:col + 1],
                                     in_=statbank[mi % 2][0:1, 0:512],
                                     axis=AX.X).then_inc(s_qnv.h, 1)
                v.nop().then_inc(s_qnv.h, 1)
            block.vector(phC_dve)
            s_qnv.c += 5 * NCH + 1

            def phC_fin(v):
                v.wait_ge(s_qnv.h, s_qnv.c)
                for h in range(4):
                    v.reduce_max(out=stat[0:1, h:h + 1],
                                 in_=qmaxrow[0:1, h * 16:(h + 1) * 16], axis=AX.X)
                v.reduce_max(out=stat[0:1, 4:5],
                             in_=qmaxrow[0:1, 64:80], axis=AX.X).then_inc(s_ep.h, 1)
            block.vector(phC_fin)
            s_ep.c += 1

            def phC_s2(s):
                s.wait_ge(s_ep.h, s_ep.c)
                s.activation(out=stat2[0:1, 0:4], in_=stat[0:1, 0:4], func=AF.Ln)
                s.activation(out=stat2[0:1, 4:5], in_=stat[0:1, 4:5], func=AF.Ln).then_inc(s_ep.h, 1)
            block.scalar(phC_s2)
            s_ep.c += 1

            def phC_v3(v):
                v.wait_ge(s_ep.h, s_ep.c)
                v.tensor_tensor(out=stat2[0:1, 8:12], in0=stat2[0:1, 0:4],
                                in1=stat2[0:1, 4:5].broadcast_to([1, 4]),
                                op=ALU.add).then_inc(s_ep.h, 1)
            block.vector(phC_v3)
            s_ep.c += 1

            def phC_s3(s):
                s.wait_ge(s_ep.h, s_ep.c)
                s.activation(out=stat2[0:1, 12:16], in_=stat2[0:1, 8:12],
                             func=AF.Exp, scale=0.5).then_inc(s_ep.h, 1)
            block.scalar(phC_s3)
            s_ep.c += 1

            def phC_v4(v):
                v.wait_ge(s_ep.h, s_ep.c)
                v.tensor_scalar(out=stat[0:1, 8:12], in0=stat2[0:1, 12:16],
                                scalar1=-1.0, scalar2=SHIFT,
                                op0=ALU.mult, op1=ALU.add).then_inc(s_ep.h, 1)
            block.vector(phC_v4)
            s_ep.c += 1

            def phC_t2(t):
                t.wait_ge(s_ep.h, s_ep.c)
                for h in range(4):
                    inst = t.matmul(out=psb_[:, h:h + 1], lhsT=ones_r[:],
                             rhs=stat[0:1, 8 + h:9 + h], start=True, stop=True)
                inst.then_inc(s_ep.h, 1)
            block.tensor(phC_t2)
            s_ep.c += 1

            def phC_v5(v):
                v.wait_ge(s_ep.h, s_ep.c)
                v.tensor_copy(out=bias_c[:], in_=psb_[:, 0:4]).then_inc(s_bias.h, 1)
            block.vector(phC_v5)
            s_bias.c += 1

            class WC:
                def __init__(self):
                    self.seen = {}

                def w(self, eng, key, semx, val):
                    if val <= 0:
                        return
                    if self.seen.get((key, id(semx)), -1) >= val:
                        return
                    eng.wait_ge(semx.h, val)
                    self.seen[(key, id(semx))] = val

            wc = WC()
            bank_free_at = [0, 0, 0, 0]
            av_free_at = [0, 0]
            first_mask = [True]
            pending_e4 = []
            avi = 0
            first_attn = True

            for qg in range(NQG):
                bound = 4 * qg + 4 if causal else NKT
                npairs = bound // 2
                for h in range(4):
                    base = 64 * (h % 2)
                    cb = (h // 2) * S
                    avb = av[avi]
                    av_S0 = s_av.c
                    st_S0 = s_st.c
                    exp_S0 = s_exp.c
                    av_guard = av_free_at[avi]
                    exp_at = {}
                    msk_at = {}
                    for pi in range(npairs):
                        s_exp.c += 1
                        exp_at[2 * pi] = s_exp.c
                        exp_at[2 * pi + 1] = s_exp.c
                    if causal:
                        for kt in range(bound - 4, bound):
                            s_msk.c += 1
                            msk_at[kt] = s_msk.c
                    ring_entry = list(bank_free_at)

                    def fD_t(t, qg=qg, h=h, base=base, cb=cb, avb=avb,
                             bound=bound, npairs=npairs, exp_at=exp_at,
                             msk_at=msk_at, av_S0=av_S0, av_guard=av_guard,
                             first=first_attn, ring_entry=ring_entry,
                             pend=list(pending_e4)):
                        if first:
                            t.wait_ge(s_pev.h, s_pev.c)
                            t.wait_ge(s_pevv.h, s_pevv.c)
                            t.wait_ge(s_rp.h, s_rp.c)
                            t.wait_ge(s_msc.h, 1)
                        n_av = [0]

                        def emit_av(kt):
                            if kt == 0:
                                wc.w(t, "pe", s_on, av_guard)
                            wc.w(t, "pe", s_exp, exp_at[kt])
                            if kt in msk_at:
                                wc.w(t, "pe", s_msk, msk_at[kt])
                            t.matmul(out=avb[0:65, :],
                                     lhsT=vsb[:, kt * 65:(kt + 1) * 65],
                                     rhs=ptb[:, (kt % 8) * 512:(kt % 8 + 1) * 512],
                                     start=(kt == 0), stop=(kt == bound - 1)
                                     ).then_inc(s_av.h, 1)
                            n_av[0] += 1

                        for pi in range(npairs):
                            for kt in (2 * pi, 2 * pi + 1):
                                bkx = kt % 4
                                if kt >= 4:
                                    wc.w(t, "pe", s_exp, exp_at[kt - 4])
                                else:
                                    wc.w(t, "pe", s_exp, ring_entry[bkx])
                                inst = t.matmul(
                                    out=ring[:, bkx * 512:(bkx + 1) * 512],
                                    lhsT=ksb[base:base + 64, kt * 128:(kt + 1) * 128],
                                    rhs=q_pk[base:base + 64, cb + qg * 512: cb + (qg + 1) * 512],
                                    start=True, stop=True)
                                if kt % 2 == 1:
                                    inst.then_inc(s_st.h, 1)
                            if pi == 1 and pend:
                                for e4 in pend:
                                    e4(t)
                                pend.clear()
                            if pi >= 2:
                                emit_av(2 * pi - 4)
                                emit_av(2 * pi - 3)
                        for e4 in pend:
                            e4(t)
                        pend.clear()
                        for kt in range(n_av[0], bound):
                            emit_av(kt)
                    block.tensor(fD_t)
                    pending_e4 = []
                    first_attn = False
                    s_st.c += npairs
                    s_av.c += bound
                    for kt in range(max(0, bound - 4), bound):
                        bank_free_at[kt % 4] = exp_at[kt]

                    def fD_s(s, h=h, npairs=npairs, st_S0=st_S0, exp_S0=exp_S0,
                             av_S0=av_S0):
                        for pi in range(npairs):
                            kt0 = 2 * pi
                            wc.w(s, "act", s_st, st_S0 + pi + 1)
                            if exp_S0 + pi == 0:
                                s.wait_ge(s_bias.h, 1)
                            if kt0 >= 8:
                                wc.w(s, "act", s_av, av_S0 + kt0 - 6)
                            half = (kt0 % 4) // 2
                            slot = kt0 % 8
                            s.activation(out=ptb[:, slot * 512:(slot + 2) * 512],
                                         in_=ring[:, half * 1024:(half + 1) * 1024],
                                         func=AF.Exp, bias=bias_c[:, h:h + 1]
                                         ).then_inc(s_exp.h, 1)
                    block.scalar(fD_s)

                    if causal:
                        def fD_vm(v, qg=qg, bound=bound, exp_at=dict(exp_at)):
                            if first_mask[0]:
                                v.wait_ge(s_cml.h, 16)
                                first_mask[0] = False
                            for kt in range(bound - 4, bound):
                                d = kt - 4 * qg
                                sl = (kt % 8) * 512
                                wc.w(v, "dve", s_exp, exp_at[kt])
                                v.tensor_tensor(out=ptb[:, sl:sl + 512],
                                                in0=ptb[:, sl:sl + 512],
                                                in1=cm_sb[:, d * 512:(d + 1) * 512],
                                                op=ALU.mult).then_inc(s_msk.h, 1)
                        block.vector(fD_vm)

                    e1_val = s_ep.c + 1
                    e2_val = s_ep.c + 2
                    e3_val = s_ep.c + 3
                    e4_val = s_ep.c + 4
                    e5_val = s_ep.c + 5

                    def fD_v(v, qg=qg, h=h, base=base, cb=cb, avb=avb,
                             av_tot=s_av.c, e1_val=e1_val, e2_val=e2_val,
                             e4_val=e4_val, e5_val=e5_val):
                        wc.w(v, "dve", s_av, av_tot)
                        v.tensor_copy(out=ltmp[:], in_=avb[64:96, :]).then_inc(s_ep.h, 1)
                        v.wait_ge(s_ep.h, e1_val)
                        v.tensor_scalar(out=linv[:], in0=ltmp[0:1, :], scalar1=LEPS,
                                        scalar2=None, op0=ALU.add).then_inc(s_ep.h, 1)
                        v.wait_ge(s_ep.h, e2_val)
                        v.reciprocal(out=linv2[:], in_=linv[:]).then_inc(s_ep.h, 1)
                        v.wait_ge(s_ep.h, e4_val)
                        v.tensor_copy(out=lb[:], in_=psb_[0:64, :]).then_inc(s_ep.h, 1)
                        v.wait_ge(s_ep.h, e5_val)
                        v.tensor_tensor(out=o_fm[base:base + 64, cb + qg * 512: cb + (qg + 1) * 512],
                                        in0=avb[0:64, :], in1=lb[:],
                                        op=ALU.mult).then_inc(s_on.h, 1)
                    block.vector(fD_v)
                    s_ep.c += 5
                    s_on.c += 1
                    av_free_at[avi] = s_on.c
                    avi ^= 1

                    def mk_e4(e3_val=e3_val):
                        def e4(t):
                            t.wait_ge(s_ep.h, e3_val)
                            t.matmul(out=psb_[0:64, :], lhsT=ones_r[0:1, 0:64],
                                     rhs=linv2[:], start=True, stop=True
                                     ).then_inc(s_ep.h, 1)
                        return e4
                    pending_e4.append(mk_e4())

            def fD_flush(t):
                for e4 in pending_e4:
                    e4(t)
                pending_e4.clear()
            block.tensor(fD_flush)

            ndma = [2]

            def phE(sy):
                sy.wait_ge(s_msc.h, 1)
                sy.wait_ge(s_cml.h, s_cml.c)
                sy.wait_ge(s_xin.h, s_xin.c)
                sy.wait_ge(s_wld.h, s_wld.c)
                sy.wait_ge(s_rt.h, s_rt.c)
                sy.wait_ge(s_on.h, s_on.c)
                sy.dma_start(out=o_out0, in_=o_fm[:, 0:S]).then_inc(s_out.h, 16)
                sy.dma_start(out=o_out1, in_=o_fm[:, S:2 * S]).then_inc(s_out.h, 16)
                if dump:
                    sy.wait_ge(s_bias.h, 1)
                    sy.dma_start(out=qdmp, in_=q_pk[:]).then_inc(s_out.h, 16)
                    sy.dma_start(out=kdmp, in_=ksb[:]).then_inc(s_out.h, 16)
                    sy.dma_start(out=vdmp, in_=vsb[:]).then_inc(s_out.h, 16)
                    sy.dma_start(out=bdmp, in_=bias_c[:]).then_inc(s_out.h, 16)
                    ndma[0] += 4
                sy.wait_ge(s_out.h, 16 * ndma[0])
            block.sync(phE)
            s_out.c += 16 * ndma[0]

    return nc



def build_block_tail(layer, ntb=4, dump=False, do_ffn=True):
    nc = bass.Bass("TRN2", target_bir_lowering=False, debug=False)
    o_in = nc.dram_tensor("o_in", [D, TPC], BF16, kind="ExternalInput").ap()
    resid_in = nc.dram_tensor("resid_in", [D, TPC], F32, kind="ExternalInput").ap()
    wo_d = nc.dram_tensor("wo", [D, D], F32, kind="ExternalInput").ap()
    s2_out = nc.dram_tensor("s2_out", [D, TPC], F32, kind="ExternalOutput").ap()
    if layer == 0:
        k0t_d = nc.dram_tensor("k0t", [D, 4096], U8, kind="ExternalInput").ap()
        k0b_d = nc.dram_tensor("k0b", [D, 4096], U8, kind="ExternalInput").ap()
        k1_d = nc.dram_tensor("k1", [4096, 4096], U8, kind="ExternalInput").ap()
        k2_d = nc.dram_tensor("k2", [4096, D], U8, kind="ExternalInput").ap()
        scl_d = nc.dram_tensor("scl", [1, 4], F32, kind="ExternalInput").ap()
    else:
        w1_d = nc.dram_tensor("w1t", [D, 4096], F32, kind="ExternalInput").ap()
        w2_d = nc.dram_tensor("w2t", [D, 4096], F32, kind="ExternalInput").ap()
        h1_d = nc.dram_tensor("h1t", [4096, 4096], F32, kind="ExternalInput").ap()
        h2_d = nc.dram_tensor("h2t", [4096, 4096], F32, kind="ExternalInput").ap()
        w3_d = nc.dram_tensor("w3t", [4096, D], F32, kind="ExternalInput").ap()
        sv1_d = nc.dram_tensor("sv1", [128, 32], F32, kind="ExternalInput").ap()
        sv2_d = nc.dram_tensor("sv2", [128, 32], F32, kind="ExternalInput").ap()
        sh1_d = nc.dram_tensor("sh1", [128, 32], F32, kind="ExternalInput").ap()
        sh2_d = nc.dram_tensor("sh2", [128, 32], F32, kind="ExternalInput").ap()
        sv3_d = nc.dram_tensor("sv3", [128, 8], F32, kind="ExternalInput").ap()
    if dump:
        s1_out = nc.dram_tensor("s1_out", [D, TPC], F32, kind="ExternalOutput").ap()
    s1_dram = nc.dram_tensor("s1_dram", [D, TPC], F32, kind="Internal").ap()

    ctx = ExitStack()
    with ctx:
        sb = lambda n, sh, dt: ctx.enter_context(nc.sbuf_tensor(n, sh, dt))
        psa = lambda n, sh: ctx.enter_context(nc.psum_tensor(n, sh, F32))
        sem = lambda n: Sem(ctx.enter_context(nc.semaphore(n)))

        bufA = sb("bufA", [128, 32 * 1024], BF16)
        bufB = sb("bufB", [128, 32 * 1024], BF16)
        rsm = sb("rsm", [128, 8 * 512], F32)
        xh = sb("xh", [128, 8 * 1024], BF16)
        wbig = sb("wbig", [128, 32 * 128], BF16)
        weff = sb("weff", [128, 8 * 512], BF16)
        wstg = sb("wstg", [128, 8 * 512], BF16)
        sb_corr = sb("sb_corr", [128, 1024], F32)
        sqv = sb("sqv", [128, 512], F32)
        if layer == 0:
            tmp2 = sb("tmp2_sb", [128, 1024], F32)
        rowm = sb("rowm", [1, 1536], F32)
        rb = sb("rb", [128, 512], F32)
        mb = sb("mb", [128, 512], F32)
        tmp1 = sb("tmp1", [128, 512], F32)
        ones_f = sb("ones_f", [128, 1], F32)
        ones_b = sb("ones_b", [128, 1], BF16)
        ones_r = sb("ones_r", [1, 128], F32)
        scl = sb("scl_sb", [128, 4], F32)
        scl4 = sb("scl4_sb", [128, 4], F32)
        if layer == 1:
            sv1 = sb("sv1_sb", [128, 32], F32)
            sv2 = sb("sv2_sb", [128, 32], F32)
            sh1 = sb("sh1_sb", [128, 32], F32)
            sh2 = sb("sh2_sb", [128, 32], F32)
            sv3 = sb("sv3_sb", [128, 8], F32)
            gtmp = sb("gtmp", [128, 1024], BF16)

        ring = psa("ring", [128, 2048])
        pb4 = psa("pb4", [128, 512])
        pb5 = psa("pb5", [128, 512])
        pb6 = psa("pb6", [128, 512])
        pb7 = psa("pb7", [128, 512])

        s_ldg = sem("s_ldg")
        s_lds = sem("s_lds")
        s_wo = sem("s_wo")
        s_ev = sem("s_ev")
        s_ch = sem("s_ch")
        s_msc = sem("s_msc")
        s_st1 = sem("s_st1")
        s_out = sem("s_out")

        with nc.Block() as block:
            def t0_g(g):
                g.dma_start(out=bufA[:, 0:16384].rearrange("p (c n) -> p c n", c=8),
                            in_=o_in.rearrange("(c p) n -> p c n", p=128)).then_inc(s_ldg.h, 16)
                g.dma_start(out=bufB[:, 0:8192].rearrange("p (c n) -> p c n", c=8),
                            in_=wo_d.rearrange("(c p) n -> p c n", p=128)).then_inc(s_ldg.h, 16)
            block.gpsimd(t0_g)
            s_ldg.c += 32

            def t0_s(sy):
                if layer == 0:
                    sy.dma_start(out=scl[0:1, :], in_=scl_d).then_inc(s_lds.h, 16)
                else:
                    sy.dma_start(out=sv1[:], in_=sv1_d).then_inc(s_lds.h, 16)
                    sy.dma_start(out=sv2[:], in_=sv2_d).then_inc(s_lds.h, 16)
                    sy.dma_start(out=sh1[:], in_=sh1_d).then_inc(s_lds.h, 16)
                    sy.dma_start(out=sh2[:], in_=sh2_d).then_inc(s_lds.h, 16)
                    sy.dma_start(out=sv3[:], in_=sv3_d).then_inc(s_lds.h, 16)
            block.sync(t0_s)
            s_lds.c += 16 if layer == 0 else 80

            def t0_v(v):
                v.memset(ones_f[:], 1.0)
                v.memset(ones_b[:], 1.0)
                v.memset(ones_r[:], 1.0).then_inc(s_msc.h, 1)
            block.vector(t0_v)
            s_msc.c += 1

            if layer == 0:
                def t0_t(t):
                    t.wait_ge(s_lds.h, s_lds.c)
                    t.wait_ge(s_msc.h, 1)
                    t.matmul(out=pb6[:, 0:4], lhsT=ones_r[:], rhs=scl[0:1, 0:4],
                             start=True, stop=True).then_inc(s_ch.h, 1)
                block.tensor(t0_t)
                s_ch.c += 1

                def t0_v2(v):
                    v.wait_ge(s_ch.h, s_ch.c)
                    v.tensor_scalar(out=scl[:], in0=pb6[:, 0:4], scalar1=2.0,
                                    scalar2=None, op0=ALU.mult).then_inc(s_ch.h, 1)
                block.vector(t0_v2)
                s_ch.c += 1

                def t0_v3(v):
                    v.wait_ge(s_ch.h, s_ch.c)
                    v.tensor_scalar(out=scl4[:], in0=scl[:], scalar1=2.0,
                                    scalar2=None, op0=ALU.mult).then_inc(s_ch.h, 1)
                block.vector(t0_v3)
                s_ch.c += 1

            def chain_t(emitfn):
                need = s_ch.c

                def f(t, need=need):
                    t.wait_ge(s_ch.h, need)
                    emitfn(t).then_inc(s_ch.h, 1)
                block.tensor(f)
                s_ch.c += 1

            def chain_v(emitfn):
                need = s_ch.c

                def f(v, need=need):
                    v.wait_ge(s_ch.h, need)
                    emitfn(v).then_inc(s_ch.h, 1)
                block.vector(f)
                s_ch.c += 1

            def chain_s(emitfn):
                need = s_ch.c

                def f(s, need=need):
                    s.wait_ge(s_ch.h, need)
                    emitfn(s).then_inc(s_ch.h, 1)
                block.scalar(f)
                s_ch.c += 1

            def emit_ln(src_resid, dst_xh, extra_pe_wait=None):
                def sum_mm(t):
                    if extra_pe_wait is not None:
                        extra_pe_wait(t)
                    for c in range(8):
                        inst = t.matmul(out=pb4[0:1, :], lhsT=ones_f[:],
                                        rhs=src_resid(c), start=(c == 0),
                                        stop=(c == 7))
                    return inst
                chain_t(sum_mm)
                for c in range(8):
                    chain_s(lambda s, c=c: s.activation(
                        out=sqv[:], in_=src_resid(c), func=AF.Square))
                    chain_t(lambda t, c=c: t.matmul(
                        out=pb5[0:1, :], lhsT=ones_f[:], rhs=sqv[:],
                        start=(c == 0), stop=(c == 7)))

                def v1(v):
                    v.tensor_scalar(out=rowm[0:1, 0:512], in0=pb4[0:1, :],
                                    scalar1=1.0 / D, scalar2=None, op0=ALU.mult)
                    return v.tensor_scalar(out=rowm[0:1, 512:1024], in0=pb5[0:1, :],
                                           scalar1=1.0 / D, scalar2=None,
                                           op0=ALU.mult)
                chain_v(v1)
                chain_s(lambda s: s.activation(out=rowm[0:1, 1024:1536],
                                               in_=rowm[0:1, 0:512], func=AF.Square))
                chain_v(lambda v: v.scalar_tensor_tensor(
                    out=rowm[0:1, 512:1024], in0=rowm[0:1, 512:1024], scalar=1e-5,
                    in1=rowm[0:1, 1024:1536], op0=ALU.add, op1=ALU.subtract))
                chain_s(lambda s: s.activation(out=rowm[0:1, 1024:1536],
                                               in_=rowm[0:1, 512:1024], func=AF.Ln))
                chain_s(lambda s: s.activation(out=rowm[0:1, 512:1024],
                                               in_=rowm[0:1, 1024:1536],
                                               func=AF.Exp, scale=-0.5))
                chain_v(lambda v: v.tensor_tensor(out=rowm[0:1, 1024:1536],
                                                  in0=rowm[0:1, 0:512],
                                                  in1=rowm[0:1, 512:1024],
                                                  op=ALU.mult))
                chain_v(lambda v: v.tensor_scalar(out=rowm[0:1, 1024:1536],
                                                  in0=rowm[0:1, 1024:1536],
                                                  scalar1=-1.0, scalar2=None,
                                                  op0=ALU.mult))

                def bc(t):
                    t.matmul(out=pb6[:, :], lhsT=ones_r[:],
                             rhs=rowm[0:1, 512:1024], start=True, stop=True)
                    return t.matmul(out=pb7[:, :], lhsT=ones_r[:],
                                    rhs=rowm[0:1, 1024:1536], start=True, stop=True)
                chain_t(bc)

                def cp(v):
                    v.tensor_copy(out=rb[:], in_=pb6[:, :])
                    return v.tensor_copy(out=mb[:], in_=pb7[:, :])
                chain_v(cp)
                for c in range(8):
                    chain_v(lambda v, c=c: v.tensor_tensor(
                        out=tmp1[:], in0=src_resid(c), in1=rb[:], op=ALU.mult))
                    chain_v(lambda v, c=c: v.tensor_tensor(
                        out=src_resid(c), in0=tmp1[:], in1=mb[:], op=ALU.add))
                    if dst_xh is not None:
                        chain_s(lambda s, c=c: s.activation(
                            out=dst_xh(c), in_=src_resid(c), func=AF.Copy))

            for tb in range(ntb):
                def t1_s(sy, tb=tb, prev=s_lds.c, ch_need=s_ch.c):
                    sy.wait_ge(s_lds.h, prev)
                    sy.wait_ge(s_ch.h, ch_need)
                    sy.dma_start(
                        out=rsm[:].rearrange("p (c n) -> p c n", c=8),
                        in_=resid_in[:, tb * 512:(tb + 1) * 512].rearrange("(c p) n -> p c n", p=128),
                    ).then_inc(s_lds.h, 16)
                block.sync(t1_s)
                s_lds.c += 16

                def t1_t(t, tb=tb):
                    if tb == 0:
                        t.wait_ge(s_ldg.h, 32)
                    for dc in range(8):
                        bk = dc % 4
                        if tb * 8 + dc >= 4:
                            t.wait_ge(s_ev.h, tb * 8 + dc - 3)
                        for k in range(8):
                            inst = t.matmul(
                                out=ring[:, bk * 512:(bk + 1) * 512],
                                lhsT=bufB[:, k * 1024 + dc * 128: k * 1024 + (dc + 1) * 128],
                                rhs=bufA[:, k * 2048 + tb * 512: k * 2048 + tb * 512 + 512],
                                start=(k == 0), stop=(k == 7))
                        inst.then_inc(s_wo.h, 1)
                block.tensor(t1_t)
                s_wo.c += 8

                def t1_v(v, tb=tb, lds_need=s_lds.c, ch_need=s_ch.c):
                    v.wait_ge(s_lds.h, lds_need)
                    v.wait_ge(s_ch.h, ch_need)
                    for dc in range(8):
                        bk = dc % 4
                        v.wait_ge(s_wo.h, tb * 8 + dc + 1)
                        v.tensor_tensor(out=rsm[:, dc * 512:(dc + 1) * 512],
                                        in0=rsm[:, dc * 512:(dc + 1) * 512],
                                        in1=ring[:, bk * 512:(bk + 1) * 512],
                                        op=ALU.add).then_inc(s_ev.h, 1)
                block.vector(t1_v)
                s_ev.c += 8

                ev_need = s_ev.c

                def pe_wait(t, ev_need=ev_need):
                    t.wait_ge(s_ev.h, ev_need)

                def src_resid(c):
                    return rsm[:, c * 512:(c + 1) * 512]

                emit_ln(src_resid, None, extra_pe_wait=pe_wait)

                def park_chain(sy, tb=tb, need=s_ch.c, prev=s_st1.c):
                    sy.wait_ge(s_ch.h, need)
                    sy.wait_ge(s_st1.h, prev)
                    sy.dma_start(
                        out=s1_dram[:, tb * 512:(tb + 1) * 512].rearrange("(c p) n -> p c n", p=128),
                        in_=rsm[:].rearrange("p (c n) -> p c n", c=8),
                    ).then_inc(s_st1.h, 16)
                    if dump:
                        sy.dma_start(
                            out=s1_out[:, tb * 512:(tb + 1) * 512].rearrange("(c p) n -> p c n", p=128),
                            in_=rsm[:].rearrange("p (c n) -> p c n", c=8),
                        ).then_inc(s_st1.h, 16)
                block.sync(park_chain)
                s_st1.c += 32 if dump else 16
                def park_guard(v, need=s_st1.c):
                    v.wait_ge(s_st1.h, need)
                    v.nop().then_inc(s_ch.h, 1)
                block.vector(park_guard)
                s_ch.c += 1

            s_fw = sem("s_fw")
            fwsem = [sem("s_fwA"), sem("s_fwB")]
            s_fm = sem("s_fm")
            s_ed = sem("s_ed")
            s_es = sem("s_es")
            wbig2 = wstg

            class Banks:
                def __init__(self):
                    self.aps = [ring[:, i * 512:(i + 1) * 512] for i in range(4)] +                                [pb4[:, :], pb5[:, :], pb6[:, :], pb7[:, :]]
                    self.freed = [(None, 0)] * 8

                def wait_free(self, t, b):
                    semx, cnt = self.freed[b]
                    if semx is not None and cnt > 0:
                        t.wait_ge(semx.h, cnt)

            bk = Banks()

            def pipe_mm_stage(nblocks, nbuf, emit_dma, n_dma, emit_mm_group,
                              emit_evac, pe_extra_first=None, mm_per_group=1,
                              groups_per_block=2):
                m_after_block = {}
                entry_waits = [(s_fm, s_fm.c), (s_ed, s_ed.c), (s_es, s_es.c)]
                for i in range(nblocks):
                    buf = i % nbuf

                    def f_g(g, i=i, buf=buf, prev_m=m_after_block.get(i - nbuf, 0),
                            prev_fw=fwsem[buf].c, entry_waits=entry_waits,
                            nbuf=nbuf):
                        if prev_fw > 0:
                            g.wait_ge(fwsem[buf].h, prev_fw)
                        if i < nbuf:
                            for semx, cnt in entry_waits:
                                if cnt > 0:
                                    g.wait_ge(semx.h, cnt)
                        if prev_m > 0:
                            g.wait_ge(s_fm.h, prev_m)
                        emit_dma(g, i, buf)
                    block.gpsimd(f_g)
                    fwsem[buf].c += 16 * n_dma
                    fw_need = (fwsem[buf], fwsem[buf].c)

                    for ts in range(groups_per_block):
                        b = (2 * (i % 2) + ts) % 8 if groups_per_block == 2 else (i % 4)

                        def f_t(t, i=i, buf=buf, ts=ts, b=b, fw_need=fw_need,
                                first=(i == 0 and ts == 0)):
                            t.wait_ge(fw_need[0].h, fw_need[1])
                            if first and pe_extra_first is not None:
                                pe_extra_first(t)
                            bk.wait_free(t, b)
                            emit_mm_group(t, i, buf, ts, bk.aps[b]).then_inc(s_fm.h, 1)
                        block.tensor(f_t)
                        s_fm.c += 1
                        bk.freed[b] = emit_evac(i, ts, b, s_fm.c)
                    m_after_block[i] = s_fm.c

            def evac_act(fn):
                def mk(i, ts, b, m_cnt):
                    def f(s, fn=fn, i=i, ts=ts, b=b, m_cnt=m_cnt):
                        s.wait_ge(s_fm.h, m_cnt)
                        fn(s, i, ts, bk.aps[b]).then_inc(s_es.h, 1)
                    block.scalar(f)
                    s_es.c += 1
                    return (s_es, s_es.c)
                return mk

            def evac_dve(fn):
                def mk(i, ts, b, m_cnt):
                    def f(v, fn=fn, i=i, ts=ts, b=b, m_cnt=m_cnt):
                        v.wait_ge(s_fm.h, m_cnt)
                        fn(v, i, ts, bk.aps[b]).then_inc(s_ed.h, 1)
                    block.vector(f)
                    s_ed.c += 1
                    return (s_ed, s_ed.c)
                return mk

            def stage_barrier(t):
                t.wait_ge(s_ed.h, s_ed.c)
                t.wait_ge(s_es.h, s_es.c)

            n_s2out = [0]
            if do_ffn:
                wo_done = s_wo.c
                park_all = s_st1.c
                lds_all = s_lds.c
                for th in range(2):
                    stage1_mm_after = [0]

                    def fA_g(g, th=th, st1=park_all, prev_m=stage1_mm_after[0],
                             prev_fw=s_fw.c):
                        g.wait_ge(s_st1.h, st1)
                        if prev_fw > 0:
                            g.wait_ge(s_fw.h, prev_fw)
                        if prev_m > 0:
                            g.wait_ge(s_fm.h, prev_m)
                        g.dma_start(
                            out=xh[:].rearrange("p (c n) -> p c n", c=8),
                            in_=s1_dram[:, th * 1024:(th + 1) * 1024].rearrange("(c p) n -> p c n", p=128),
                        ).then_inc(s_fw.h, 16)
                    block.gpsimd(fA_g)
                    s_fw.c += 16
                    xh_ready = s_fw.c

                    def pe_first(t, th=th, xh_ready=xh_ready, ch_now=s_ch.c,
                                 ed_now=s_ed.c, es_now=s_es.c):
                        t.wait_ge(s_fw.h, xh_ready)
                        t.wait_ge(s_ch.h, ch_now)
                        if th == 0:
                            t.wait_ge(s_wo.h, wo_done)
                            t.wait_ge(s_lds.h, lds_all)
                        else:
                            t.wait_ge(s_ed.h, ed_now)
                            t.wait_ge(s_es.h, es_now)
                    s_ch_entry = [s_ch.c]

                    if layer == 1:
                        def dma1(g, i, buf):
                            g.dma_start(
                                out=weff[:, buf * 2048: buf * 2048 + 1024].rearrange("p (c n) -> p c n", c=8),
                                in_=w1_d[:, i * 128:(i + 1) * 128].rearrange("(c p) n -> p c n", p=128),
                            ).then_inc(fwsem[buf].h, 16)
                            g.dma_start(
                                out=weff[:, buf * 2048 + 1024: buf * 2048 + 2048].rearrange("p (c n) -> p c n", c=8),
                                in_=w2_d[:, i * 128:(i + 1) * 128].rearrange("(c p) n -> p c n", p=128),
                            ).then_inc(fwsem[buf].h, 16)

                        def mm1(t, i, buf, ts, bank_ap):
                            bA = 2 * (i % 2) + ts
                            bB = 4 + bA
                            bk.wait_free(t, bB)
                            for k in range(8):
                                t.matmul(out=bk.aps[bA],
                                         lhsT=weff[:, buf * 2048 + k * 128: buf * 2048 + (k + 1) * 128],
                                         rhs=xh[:, k * 1024 + ts * 512: k * 1024 + ts * 512 + 512],
                                         start=(k == 0), stop=(k == 7))
                            for k in range(8):
                                inst = t.matmul(out=bk.aps[bB],
                                         lhsT=weff[:, buf * 2048 + 1024 + k * 128: buf * 2048 + 1024 + (k + 1) * 128],
                                         rhs=xh[:, k * 1024 + ts * 512: k * 1024 + ts * 512 + 512],
                                         start=(k == 0), stop=(k == 7))
                            return inst

                        gslot_free = {0: 0, 1: 0}

                        def evac1(i, ts, b, m_cnt):
                            bA = 2 * (i % 2) + ts
                            bB = 4 + bA
                            gslot = ts * 512

                            def f_s(s, i=i, gslot=gslot, bA=bA, m_cnt=m_cnt,
                                    war=gslot_free[ts]):
                                s.wait_ge(s_fm.h, m_cnt)
                                if war > 0:
                                    s.wait_ge(s_ed.h, war)
                                s.activation(out=gtmp[:, gslot:gslot + 512],
                                             in_=bk.aps[bA], func=AF.Silu,
                                             scale=sv1[:, i:i + 1]).then_inc(s_es.h, 1)
                            block.scalar(f_s)
                            s_es.c += 1
                            esc = s_es.c

                            def f_v(v, i=i, ts=ts, gslot=gslot, bB=bB, esc=esc):
                                v.wait_ge(s_es.h, esc)
                                v.scalar_tensor_tensor(
                                    out=bufA[:, i * 1024 + ts * 512: i * 1024 + ts * 512 + 512],
                                    in0=bk.aps[bB], scalar=sv2[:, i:i + 1],
                                    in1=gtmp[:, gslot:gslot + 512],
                                    op0=ALU.mult, op1=ALU.mult).then_inc(s_ed.h, 1)
                            block.vector(f_v)
                            s_ed.c += 1
                            gslot_free[ts] = s_ed.c
                            bk.freed[bA] = (s_es, esc)
                            return (s_ed, s_ed.c)

                        pipe_mm_stage(32, 2, dma1, 2, mm1, evac1,
                                      pe_extra_first=pe_first)
                        stage1_mm_after[0] = s_fm.c

                        def dma2(g, i, buf, wsrc=h1_d):
                            dst = wbig if buf == 0 else wbig2
                            g.dma_start(
                                out=dst[:].rearrange("p (c n) -> p c n", c=32),
                                in_=wsrc[:, i * 128:(i + 1) * 128].rearrange("(c p) n -> p c n", p=128),
                            ).then_inc(fwsem[buf].h, 16)

                        def mk_mm_big(src_buf):
                            def mm(t, i, buf, ts, bank_ap):
                                w = wbig if buf == 0 else wbig2
                                for k in range(32):
                                    inst = t.matmul(out=bank_ap,
                                             lhsT=w[:, k * 128:(k + 1) * 128],
                                             rhs=src_buf[:, k * 1024 + ts * 512: k * 1024 + ts * 512 + 512],
                                             start=(k == 0), stop=(k == 31))
                                return inst
                            return mm

                        def mk_gelu_evac(dst_buf, scol):
                            def fn(s, i, ts, bank_ap):
                                return s.activation(out=dst_buf[:, i * 1024 + ts * 512: i * 1024 + ts * 512 + 512],
                                             in_=bank_ap, func=AF.Gelu,
                                             scale=scol[:, i:i + 1])
                            return evac_act(fn)

                        def barrier_first(t):
                            stage_barrier(t)
                        pipe_mm_stage(32, 2, dma2, 1, mk_mm_big(bufA),
                                      mk_gelu_evac(bufB, sh1)(0, 0, 0, 0) if False else mk_gelu_evac(bufB, sh1),
                                      pe_extra_first=barrier_first)

                        def dma3(g, i, buf):
                            dst = wbig if buf == 0 else wbig2
                            g.dma_start(
                                out=dst[:].rearrange("p (c n) -> p c n", c=32),
                                in_=h2_d[:, i * 128:(i + 1) * 128].rearrange("(c p) n -> p c n", p=128),
                            ).then_inc(fwsem[buf].h, 16)
                        pipe_mm_stage(32, 2, dma3, 1, mk_mm_big(bufB),
                                      mk_gelu_evac(bufA, sh2),
                                      pe_extra_first=barrier_first)

                        for ts in range(2):
                            def s4_s(sy, th=th, ts=ts, prev=s_lds.c, need=s_ch.c,
                                     ed=s_ed.c, es=s_es.c):
                                sy.wait_ge(s_lds.h, prev)
                                sy.wait_ge(s_ch.h, need)
                                col = th * 1024 + ts * 512
                                sy.dma_start(
                                    out=rsm[:].rearrange("p (c n) -> p c n", c=8),
                                    in_=s1_dram[:, col:col + 512].rearrange("(c p) n -> p c n", p=128),
                                ).then_inc(s_lds.h, 16)
                            block.sync(s4_s)
                            s_lds.c += 16
                            lds_need = s_lds.c

                            def dma4(g, i, buf):
                                dst = wbig if buf == 0 else wbig2
                                g.dma_start(
                                    out=dst[:].rearrange("p (c n) -> p c n", c=32),
                                    in_=w3_d[:, i * 128:(i + 1) * 128].rearrange("(c p) n -> p c n", p=128),
                                ).then_inc(fwsem[buf].h, 16)

                            def mm4(t, i, buf, _ts, bank_ap, ts=ts,
                                    lds_need=lds_need):
                                if i == 0:
                                    t.wait_ge(s_lds.h, lds_need)
                                w = wbig if buf == 0 else wbig2
                                for k in range(32):
                                    inst = t.matmul(out=bank_ap,
                                             lhsT=w[:, k * 128:(k + 1) * 128],
                                             rhs=bufA[:, k * 1024 + ts * 512: k * 1024 + ts * 512 + 512],
                                             start=(k == 0), stop=(k == 31))
                                return inst

                            def ev4(v, i, _ts, bank_ap, ts=ts):
                                return v.scalar_tensor_tensor(
                                    out=rsm[:, i * 512:(i + 1) * 512],
                                    in0=bank_ap, scalar=sv3[:, i:i + 1],
                                    in1=rsm[:, i * 512:(i + 1) * 512],
                                    op0=ALU.mult, op1=ALU.add)
                            pipe_mm_stage(8, 2, dma4, 1, mm4, evac_dve(ev4),
                                          pe_extra_first=(barrier_first if ts == 0 else None),
                                          groups_per_block=1)

                            def lnent(t, ed=s_ed.c, es=s_es.c):
                                t.wait_ge(s_ed.h, ed)
                                t.wait_ge(s_es.h, es)
                            emit_ln(lambda c: rsm[:, c * 512:(c + 1) * 512], None,
                                    extra_pe_wait=lnent)

                            def s2out_s(sy, th=th, ts=ts, need=s_ch.c, prev=s_out.c):
                                sy.wait_ge(s_ch.h, need)
                                sy.wait_ge(s_out.h, prev)
                                col = th * 1024 + ts * 512
                                sy.dma_start(
                                    out=s2_out[:, col:col + 512].rearrange("(c p) n -> p c n", p=128),
                                    in_=rsm[:].rearrange("p (c n) -> p c n", c=8),
                                ).then_inc(s_out.h, 16)
                            block.sync(s2out_s)
                            s_out.c += 16
                            n_s2out[0] += 1

                            def s2out_guard(v, need=s_out.c):
                                v.wait_ge(s_out.h, need)
                                v.nop().then_inc(s_ch.h, 1)
                            block.vector(s2out_guard)
                            s_ch.c += 1

                    else:
                        def dma1(g, i, buf):
                            g.dma_start(
                                out=weff[:, buf * 2048: buf * 2048 + 2048].rearrange("p (c n) -> p c n", c=8),
                                in_=k0t_d[:, i * 256:(i + 1) * 256].rearrange("(c p) n -> p c n", p=128),
                            ).then_inc(fwsem[buf].h, 16)
                            g.dma_start(
                                out=wstg[:, buf * 2048: buf * 2048 + 2048].rearrange("p (c n) -> p c n", c=8),
                                in_=k0b_d[:, i * 256:(i + 1) * 256].rearrange("(c p) n -> p c n", p=128),
                            ).then_inc(fwsem[buf].h, 16)

                        sub_done = {}

                        def mm1(t, i, buf, ts, bank_ap, sub_done=sub_done):
                            t.wait_ge(s_ed.h, sub_done[i])
                            for d in range(2):
                                b2 = 2 * (i % 2) + d
                                if ts == 0:
                                    bk.wait_free(t, b2)
                                    bk2 = 4 + b2
                                    bk.wait_free(t, bk2)
                                for k in range(8):
                                    inst = t.matmul(
                                        out=bk.aps[2 * (i % 2) + d] if ts == 0 else bk.aps[4 + 2 * (i % 2) + d],
                                        lhsT=weff[:, buf * 2048 + k * 256 + d * 128: buf * 2048 + k * 256 + (d + 1) * 128],
                                        rhs=xh[:, k * 1024 + ts * 512: k * 1024 + ts * 512 + 512],
                                        start=(k == 0), stop=(k == 7))
                            return inst

                        def evac1(i, ts, b, m_cnt):
                            def f_v(v, i=i, ts=ts, m_cnt=m_cnt):
                                v.wait_ge(s_fm.h, m_cnt)
                                for d in range(2):
                                    bidx = (2 * (i % 2) + d) + (0 if ts == 0 else 4)
                                    dd = i * 2 + d
                                    inst = v.tensor_scalar(
                                        out=bufA[:, dd * 1024 + ts * 512: dd * 1024 + ts * 512 + 512],
                                        in0=bk.aps[bidx], scalar1=scl[:, 0:1],
                                        scalar2=0.0, op0=ALU.mult, op1=ALU.max)
                                inst.then_inc(s_ed.h, 1)
                            block.vector(f_v)
                            s_ed.c += 1
                            for d in range(2):
                                bidx = (2 * (i % 2) + d) + (0 if ts == 0 else 4)
                                bk.freed[bidx] = (s_ed, s_ed.c)
                            return (s_ed, s_ed.c)

                        m_after = {}
                        s1_entry = [(s_fm, s_fm.c), (s_ed, s_ed.c), (s_es, s_es.c)]
                        for i in range(16):
                            buf = i % 2

                            def f_g(g, i=i, buf=buf, prev_m=m_after.get(i - 2, 0),
                                    prev_fw=fwsem[buf].c, s1_entry=s1_entry):
                                if prev_fw > 0:
                                    g.wait_ge(fwsem[buf].h, prev_fw)
                                if i < 2:
                                    for semx, cnt in s1_entry:
                                        if cnt > 0:
                                            g.wait_ge(semx.h, cnt)
                                if prev_m > 0:
                                    g.wait_ge(s_fm.h, prev_m)
                                dma1(g, i, buf)
                            block.gpsimd(f_g)
                            fwsem[buf].c += 32
                            fw_need = (fwsem[buf], fwsem[buf].c)

                            def f_sub(v, i=i, buf=buf, fw_need=fw_need,
                                      prev_m=m_after.get(i - 2, 0)):
                                v.wait_ge(fw_need[0].h, fw_need[1])
                                if prev_m > 0:
                                    v.wait_ge(s_fm.h, prev_m)
                                v.tensor_tensor(out=weff[:, buf * 2048: buf * 2048 + 2048],
                                                in0=weff[:, buf * 2048: buf * 2048 + 2048],
                                                in1=wstg[:, buf * 2048: buf * 2048 + 2048],
                                                op=ALU.subtract).then_inc(s_ed.h, 1)
                            block.vector(f_sub)
                            s_ed.c += 1
                            sub_done[i] = s_ed.c

                            for ts in range(2):
                                def f_t(t, i=i, buf=buf, ts=ts,
                                        first=(i == 0 and ts == 0)):
                                    if first:
                                        pe_first(t)
                                    mm1(t, i, buf, ts, None).then_inc(s_fm.h, 1)
                                block.tensor(f_t)
                                s_fm.c += 1
                                evac1(i, ts, None, s_fm.c)
                            m_after[i] = s_fm.c
                        stage1_mm_after[0] = s_fm.c

                        def barrier_first(t):
                            stage_barrier(t)

                        for ts in range(2):
                            def srow_t(t, ts=ts, first=(ts == 0)):
                                if first:
                                    stage_barrier(t)
                                bk.wait_free(t, 4 + ts)
                                for k in range(32):
                                    inst = t.matmul(out=bk.aps[4 + ts][0:1, :],
                                                    lhsT=ones_b[:],
                                                    rhs=bufA[:, k * 1024 + ts * 512: k * 1024 + ts * 512 + 512],
                                                    start=(k == 0), stop=(k == 31))
                                inst.then_inc(s_fm.h, 1)
                            block.tensor(srow_t)
                            s_fm.c += 1

                            def srow_v(v, ts=ts, m_cnt=s_fm.c):
                                v.wait_ge(s_fm.h, m_cnt)
                                v.tensor_scalar(out=rowm[0:1, 0:512],
                                                in0=bk.aps[4 + ts][0:1, :],
                                                scalar1=scl[0:1, 1:2], scalar2=None,
                                                op0=ALU.mult).then_inc(s_ed.h, 1)
                            block.vector(srow_v)
                            s_ed.c += 1
                            bk.freed[4 + ts] = (s_ed, s_ed.c)

                            def srow_t2(t, ts=ts, e_cnt=s_ed.c):
                                t.wait_ge(s_ed.h, e_cnt)
                                bk.wait_free(t, 6 + ts)
                                t.matmul(out=bk.aps[6 + ts], lhsT=ones_r[:],
                                         rhs=rowm[0:1, 0:512], start=True,
                                         stop=True).then_inc(s_fm.h, 1)
                            block.tensor(srow_t2)
                            s_fm.c += 1

                            def srow_v2(v, ts=ts, m_cnt=s_fm.c):
                                v.wait_ge(s_fm.h, m_cnt)
                                v.tensor_copy(out=sb_corr[:, ts * 512:(ts + 1) * 512],
                                              in_=bk.aps[6 + ts]).then_inc(s_ed.h, 1)
                            block.vector(srow_v2)
                            s_ed.c += 1
                            bk.freed[6 + ts] = (s_ed, s_ed.c)

                        def dma2(g, i, buf):
                            dst = wbig if buf == 0 else wbig2
                            g.dma_start(
                                out=dst[:].rearrange("p (c n) -> p c n", c=32),
                                in_=k1_d[:, i * 128:(i + 1) * 128].rearrange("(c p) n -> p c n", p=128),
                            ).then_inc(fwsem[buf].h, 16)

                        def mk_mm_big(src_buf):
                            def mm(t, i, buf, ts, bank_ap):
                                w = wbig if buf == 0 else wbig2
                                for k in range(32):
                                    inst = t.matmul(out=bank_ap,
                                             lhsT=w[:, k * 128:(k + 1) * 128],
                                             rhs=src_buf[:, k * 1024 + ts * 512: k * 1024 + ts * 512 + 512],
                                             start=(k == 0), stop=(k == 31))
                                return inst
                            return mm

                        tslot_free = {0: 0, 1: 0}

                        def evac2(i, ts, b, m_cnt):
                            tslot = ts * 512

                            def f_v(v, i=i, ts=ts, b=b, m_cnt=m_cnt, tslot=tslot,
                                    war=tslot_free[ts]):
                                v.wait_ge(s_fm.h, m_cnt)
                                if war > 0:
                                    v.wait_ge(s_es.h, war)
                                v.scalar_tensor_tensor(
                                    out=tmp2[:, tslot:tslot + 512],
                                    in0=bk.aps[b], scalar=scl4[:, 1:2],
                                    in1=sb_corr[:, ts * 512:(ts + 1) * 512],
                                    op0=ALU.mult, op1=ALU.subtract).then_inc(s_ed.h, 1)
                            block.vector(f_v)
                            s_ed.c += 1
                            edc = s_ed.c

                            def f_s(s, i=i, ts=ts, edc=edc, tslot=tslot):
                                s.wait_ge(s_ed.h, edc)
                                s.activation(
                                    out=bufB[:, i * 1024 + ts * 512: i * 1024 + ts * 512 + 512],
                                    in_=tmp2[:, tslot:tslot + 512], func=AF.Relu).then_inc(s_es.h, 1)
                            block.scalar(f_s)
                            s_es.c += 1
                            tslot_free[ts] = s_es.c
                            return (s_ed, edc)
                        pipe_mm_stage(32, 2, dma2, 1, mk_mm_big(bufA), evac2)

                        for ts in range(2):
                            def srow_t(t, ts=ts, first=(ts == 0)):
                                if first:
                                    stage_barrier(t)
                                bk.wait_free(t, 4 + ts)
                                for k in range(32):
                                    inst = t.matmul(out=bk.aps[4 + ts][0:1, :],
                                                    lhsT=ones_b[:],
                                                    rhs=bufB[:, k * 1024 + ts * 512: k * 1024 + ts * 512 + 512],
                                                    start=(k == 0), stop=(k == 31))
                                inst.then_inc(s_fm.h, 1)
                            block.tensor(srow_t)
                            s_fm.c += 1

                            def srow_v(v, ts=ts, m_cnt=s_fm.c):
                                v.wait_ge(s_fm.h, m_cnt)
                                v.tensor_scalar(out=rowm[0:1, 0:512],
                                                in0=bk.aps[4 + ts][0:1, :],
                                                scalar1=scl[0:1, 2:3], scalar2=None,
                                                op0=ALU.mult).then_inc(s_ed.h, 1)
                            block.vector(srow_v)
                            s_ed.c += 1
                            bk.freed[4 + ts] = (s_ed, s_ed.c)

                            def srow_t2(t, ts=ts, e_cnt=s_ed.c):
                                t.wait_ge(s_ed.h, e_cnt)
                                bk.wait_free(t, 6 + ts)
                                t.matmul(out=bk.aps[6 + ts], lhsT=ones_r[:],
                                         rhs=rowm[0:1, 0:512], start=True,
                                         stop=True).then_inc(s_fm.h, 1)
                            block.tensor(srow_t2)
                            s_fm.c += 1

                            def srow_v2(v, ts=ts, m_cnt=s_fm.c):
                                v.wait_ge(s_fm.h, m_cnt)
                                v.tensor_copy(out=sb_corr[:, ts * 512:(ts + 1) * 512],
                                              in_=bk.aps[6 + ts]).then_inc(s_ed.h, 1)
                            block.vector(srow_v2)
                            s_ed.c += 1
                            bk.freed[6 + ts] = (s_ed, s_ed.c)

                        for ts in range(2):
                            def s3_s(sy, th=th, ts=ts, prev=s_lds.c, need=s_ch.c):
                                sy.wait_ge(s_lds.h, prev)
                                sy.wait_ge(s_ch.h, need)
                                col = th * 1024 + ts * 512
                                sy.dma_start(
                                    out=rsm[:].rearrange("p (c n) -> p c n", c=8),
                                    in_=s1_dram[:, col:col + 512].rearrange("(c p) n -> p c n", p=128),
                                ).then_inc(s_lds.h, 16)
                            block.sync(s3_s)
                            s_lds.c += 16
                            lds_need = s_lds.c

                            def dma3(g, i, buf):
                                dst = wbig if buf == 0 else wbig2
                                g.dma_start(
                                    out=dst[:].rearrange("p (c n) -> p c n", c=32),
                                    in_=k2_d[:, i * 128:(i + 1) * 128].rearrange("(c p) n -> p c n", p=128),
                                ).then_inc(fwsem[buf].h, 16)

                            def mm3(t, i, buf, _ts, bank_ap, ts=ts,
                                    lds_need=lds_need):
                                if i == 0:
                                    t.wait_ge(s_lds.h, lds_need)
                                w = wbig if buf == 0 else wbig2
                                for k in range(32):
                                    inst = t.matmul(out=bank_ap,
                                             lhsT=w[:, k * 128:(k + 1) * 128],
                                             rhs=bufB[:, k * 1024 + ts * 512: k * 1024 + ts * 512 + 512],
                                             start=(k == 0), stop=(k == 31))
                                return inst

                            def ev3(i, _ts, b, m_cnt, ts=ts):
                                def f_v(v, i=i, b=b, m_cnt=m_cnt, ts=ts,
                                        war=s_ed.c):
                                    v.wait_ge(s_fm.h, m_cnt)
                                    if war > 0:
                                        v.wait_ge(s_ed.h, war)
                                    v.scalar_tensor_tensor(
                                        out=tmp1[:], in0=bk.aps[b],
                                        scalar=scl4[:, 2:3],
                                        in1=sb_corr[:, ts * 512:(ts + 1) * 512],
                                        op0=ALU.mult, op1=ALU.subtract).then_inc(s_ed.h, 1)
                                block.vector(f_v)
                                s_ed.c += 1
                                edc = s_ed.c

                                def f_v2(v, i=i, edc=edc):
                                    v.wait_ge(s_ed.h, edc)
                                    v.tensor_tensor(
                                        out=rsm[:, i * 512:(i + 1) * 512],
                                        in0=rsm[:, i * 512:(i + 1) * 512],
                                        in1=tmp1[:], op=ALU.add).then_inc(s_ed.h, 1)
                                block.vector(f_v2)
                                s_ed.c += 1
                                return (s_ed, edc)
                            pipe_mm_stage(8, 2, dma3, 1, mm3, ev3,
                                          pe_extra_first=(barrier_first if ts == 0 else None),
                                          groups_per_block=1)

                            def lnent(t, ed=s_ed.c, es=s_es.c):
                                t.wait_ge(s_ed.h, ed)
                                t.wait_ge(s_es.h, es)
                            emit_ln(lambda c: rsm[:, c * 512:(c + 1) * 512], None,
                                    extra_pe_wait=lnent)

                            def s2out_s(sy, th=th, ts=ts, need=s_ch.c, prev=s_out.c):
                                sy.wait_ge(s_ch.h, need)
                                sy.wait_ge(s_out.h, prev)
                                col = th * 1024 + ts * 512
                                sy.dma_start(
                                    out=s2_out[:, col:col + 512].rearrange("(c p) n -> p c n", p=128),
                                    in_=rsm[:].rearrange("p (c n) -> p c n", c=8),
                                ).then_inc(s_out.h, 16)
                            block.sync(s2out_s)
                            s_out.c += 16
                            n_s2out[0] += 1

                            def s2out_guard(v, need=s_out.c):
                                v.wait_ge(s_out.h, need)
                                v.nop().then_inc(s_ch.h, 1)
                            block.vector(s2out_guard)
                            s_ch.c += 1

            def tE(sy):
                sy.wait_ge(s_st1.h, s_st1.c)
                sy.wait_ge(s_ch.h, s_ch.c)
                if not do_ffn:
                    sy.dma_start(out=s2_out.rearrange("(c p) n -> p c n", p=128),
                                 in_=s1_dram.rearrange("(c p) n -> p c n", p=128)
                                 ).then_inc(s_out.h, 16)
                    s_out.c += 16
                sy.wait_ge(s_out.h, s_out.c)
                sy.wait_ge(s_ldg.h, s_ldg.c)
                sy.wait_ge(s_lds.h, s_lds.c)
                sy.wait_ge(s_fw.h, s_fw.c)
                sy.wait_ge(fwsem[0].h, fwsem[0].c)
                sy.wait_ge(fwsem[1].h, fwsem[1].c)
            block.sync(tE)

    return nc



def _f32c(a):
    return np.ascontiguousarray(np.asarray(a, dtype=np.float32))


def make_cmask():
    cm = np.zeros((4, 128, 512), dtype=ml_dtypes.bfloat16)
    tri = np.triu(np.ones((128, 128), dtype=np.float32))
    for d in range(4):
        for st in range(4):
            if st > d:
                cm[d, :, st * 128:(st + 1) * 128] = 1.0
            elif st == d:
                cm[d, :, st * 128:(st + 1) * 128] = tri.astype(ml_dtypes.bfloat16)
    return cm


def stage_l0_attn(x, Wq, Wk, Wv):
    xT = np.ascontiguousarray(x.transpose(0, 2, 1))
    WqT = _f32c(np.asarray(Wq).T)
    WkT = _f32c(np.asarray(Wk).T)
    WvT = _f32c(np.asarray(Wv).T)
    cm = make_cmask()
    in_maps = []
    for core in range(NC):
        b, g = core // 4, core % 4
        wvz = np.zeros((D, 65), dtype=np.float32)
        wvz[:, 0:64] = WvT[:, 64 * g:64 * g + 64]
        in_maps.append({
            "xfm": np.ascontiguousarray(xT[b]),
            "wq": np.ascontiguousarray(WqT[:, 256 * g:256 * g + 256]),
            "wk": np.ascontiguousarray(WkT[:, 64 * g:64 * g + 64]),
            "wvz": wvz,
            "cmask": cm,
        })
    return in_maps


def gather_o(results):
    o_full = np.zeros((B, D, S), dtype=ml_dtypes.bfloat16)
    for core in range(NC):
        b, g = core // 4, core % 4
        o_full[b, 256 * g:256 * g + 128] = results[core]["o_out0"]
        o_full[b, 256 * g + 128:256 * g + 256] = results[core]["o_out1"]
    return o_full


def _rope_tables():
    inv = 1.0 / (10000.0 ** (np.arange(0, D, 2, dtype=np.float64) / D))
    t = np.arange(S, dtype=np.float64)[:, None]
    ang = t * inv[None, :]
    return np.cos(ang).astype(np.float32), np.sin(ang).astype(np.float32)


_EO = np.concatenate([np.arange(0, 64, 2), np.arange(1, 64, 2)])


def stage_l1_attn(xfm_full, Wq, Wk, Wv):
    Wq = _f32c(Wq)
    Wk = _f32c(Wk)
    Wv = _f32c(Wv)
    cosT, sinT = _rope_tables()
    in_maps = []
    for core in range(NC):
        b, g = core // 4, core % 4
        wq = np.zeros((D, 256), np.float32)
        for i in range(4):
            h = 4 * g + i
            wq[:, 64 * i:64 * i + 64] = Wq[:, 64 * h:64 * h + 64][:, _EO]
        wk = np.ascontiguousarray(Wk[:, 64 * g:64 * g + 64][:, _EO])
        wvz = np.zeros((D, 65), np.float32)
        wvz[:, 0:64] = Wv[:, 64 * g:64 * g + 64]
        rqc = np.zeros((128, 2 * S), np.float32)
        rqs = np.zeros((128, 2 * S), np.float32)
        for c in range(2):
            for half in range(2):
                h = 4 * g + 2 * c + half
                cf = cosT[:, 32 * h:32 * h + 32].T
                sf = sinT[:, 32 * h:32 * h + 32].T
                r0 = half * 64
                rqc[r0:r0 + 32, c * S:(c + 1) * S] = cf
                rqc[r0 + 32:r0 + 64, c * S:(c + 1) * S] = cf
                rqs[r0:r0 + 32, c * S:(c + 1) * S] = -sf
                rqs[r0 + 32:r0 + 64, c * S:(c + 1) * S] = sf
        rkc = np.zeros((64, S), np.float32)
        rks = np.zeros((64, S), np.float32)
        ckf = cosT[:, 0:32].T
        skf = sinT[:, 0:32].T
        rkc[0:32] = ckf
        rkc[32:64] = ckf
        rks[0:32] = -skf
        rks[32:64] = skf
        in_maps.append({
            "xfm": np.ascontiguousarray(xfm_full[b]),
            "wq": wq, "wk": wk, "wvz": wvz,
            "rqc": rqc, "rqs": rqs, "rkc": rkc, "rks": rks,
        })
    return in_maps


def stage_l0_tail(o_full, x, Wo, k0, k1, k2, s0, s1, s2):
    xT = np.ascontiguousarray(x.transpose(0, 2, 1))
    WoT = _f32c(np.asarray(Wo).T)
    k0 = np.asarray(k0)
    k0t = np.ascontiguousarray(k0[0:D]).astype(np.uint8)
    k0b = np.ascontiguousarray(k0[D:2 * D]).astype(np.uint8)
    k1u = np.ascontiguousarray(np.asarray(k1)).astype(np.uint8)
    k2u = np.ascontiguousarray(np.asarray(k2)).astype(np.uint8)
    scl = np.array([[float(s0), float(s1), float(s2), 0.0]], np.float32)
    in_maps = []
    for core in range(NC):
        b, j = core // 4, core % 4
        in_maps.append({
            "o_in": np.ascontiguousarray(o_full[b][:, TPC * j:TPC * (j + 1)]),
            "resid_in": np.ascontiguousarray(xT[b][:, TPC * j:TPC * (j + 1)]),
            "wo": WoT,
            "k0t": k0t, "k0b": k0b, "k1": k1u, "k2": k2u, "scl": scl,
        })
    return in_maps


def gather_stream(results, key="s2_out"):
    s = np.zeros((B, D, S), dtype=np.float32)
    for core in range(NC):
        b, j = core // 4, core % 4
        s[b][:, TPC * j:TPC * (j + 1)] = results[core][key]
    return s


def run_layer0(x, inp):
    nc1 = build_attn_core(layer=0)
    r1 = run_bass_kernel_spmd(nc1, stage_l0_attn(
        x, inp["L0_attn_Wq"], inp["L0_attn_Wk"], inp["L0_attn_Wv"]),
        core_ids=list(range(NC)))
    o_full = gather_o(r1.results)
    nc2 = build_block_tail(layer=0)
    r2 = run_bass_kernel_spmd(nc2, stage_l0_tail(
        o_full, x, inp["L0_attn_Wo"], inp["L0_ffn_k0"], inp["L0_ffn_k1"],
        inp["L0_ffn_k2"], inp["L0_ffn_s0"], inp["L0_ffn_s1"], inp["L0_ffn_s2"]),
        core_ids=list(range(NC)))
    return gather_stream(r2.results)


def stage_l1_tail(o_full, stream1, inp):
    WoT = _f32c(inp["L1_attn_Wo"])
    w1t = _f32c(np.asarray(inp["L1_ffn_w1_w"]).T)
    w2t = _f32c(np.asarray(inp["L1_ffn_w2_w"]).T)
    h1t = _f32c(np.asarray(inp["L1_ffn_h1_w"]).T)
    h2t = _f32c(np.asarray(inp["L1_ffn_h2_w"]).T)
    w3t = _f32c(np.asarray(inp["L1_ffn_w3_w"]).T)
    pmj = lambda v: np.ascontiguousarray(np.asarray(v, np.float32).reshape(-1, 128).T)
    sv1 = pmj(inp["L1_ffn_w1_s"])
    sv2 = pmj(inp["L1_ffn_w2_s"])
    sh1 = pmj(inp["L1_ffn_h1_s"])
    sh2 = pmj(inp["L1_ffn_h2_s"])
    sv3 = pmj(inp["L1_ffn_w3_s"])
    in_maps = []
    for core in range(NC):
        b, j = core // 4, core % 4
        in_maps.append({
            "o_in": np.ascontiguousarray(o_full[b][:, TPC * j:TPC * (j + 1)]),
            "resid_in": np.ascontiguousarray(stream1[b][:, TPC * j:TPC * (j + 1)]),
            "wo": WoT,
            "w1t": w1t, "w2t": w2t, "h1t": h1t, "h2t": h2t, "w3t": w3t,
            "sv1": sv1, "sv2": sv2, "sh1": sh1, "sh2": sh2, "sv3": sv3,
        })
    return in_maps


_BUILT = {}


def _get(builder, key, *args, **kw):
    if key not in _BUILT:
        _BUILT[key] = builder(*args, **kw)
    return _BUILT[key]


def kernel(**inputs):
    x = np.asarray(inputs["L0_attn_hidden_states"], np.float32)
    cores = list(range(NC))

    nc = _get(build_attn_core, "a0", layer=0)
    r = run_bass_kernel_spmd(nc, stage_l0_attn(
        x, inputs["L0_attn_Wq"], inputs["L0_attn_Wk"], inputs["L0_attn_Wv"]),
        core_ids=cores)
    o_full = gather_o(r.results)
    nc = _get(build_block_tail, "t0", layer=0)
    r = run_bass_kernel_spmd(nc, stage_l0_tail(
        o_full, x, inputs["L0_attn_Wo"], inputs["L0_ffn_k0"], inputs["L0_ffn_k1"],
        inputs["L0_ffn_k2"], inputs["L0_ffn_s0"], inputs["L0_ffn_s1"],
        inputs["L0_ffn_s2"]), core_ids=cores)
    s2 = gather_stream(r.results)

    nc = _get(build_attn_core, "a1", layer=1)
    r = run_bass_kernel_spmd(nc, stage_l1_attn(
        s2, inputs["L1_attn_Wq"], inputs["L1_attn_Wk"], inputs["L1_attn_Wv"]),
        core_ids=cores)
    o_full = gather_o(r.results)
    nc = _get(build_block_tail, "t1", layer=1)
    r = run_bass_kernel_spmd(nc, stage_l1_tail(o_full, s2, inputs), core_ids=cores)
    s4 = gather_stream(r.results)

    return np.ascontiguousarray(s4.transpose(0, 2, 1))


# version 05
# speedup vs baseline: 1.0622x; 1.0150x over previous
import numpy as np
import ml_dtypes
from contextlib import ExitStack

import concourse.bass as bass
import concourse.mybir as mybir
from concourse.bass_utils import run_bass_kernel_spmd

F32 = mybir.dt.float32
BF16 = mybir.dt.bfloat16
U8 = mybir.dt.uint8
AF = mybir.ActivationFunctionType
ALU = mybir.AluOpType
AX = mybir.AxisListType

B, S, D = 2, 8192, 1024
NC = 8
TPC = 2048
NH, DH, KVH = 16, 64, 4
NKT = S // 128
NQG = S // 512
SHIFT = 44.0
LEPS = 1e-30
NCH = 16


class Sem:
    def __init__(self, h):
        self.h = h
        self.c = 0



def build_attn_core(layer, nqg=None, dump=False, skip_stats=False, skip_b=False):
    causal = (layer == 0)
    qscale = (DH ** -0.5) if layer == 0 else (D ** -0.5)
    if nqg is None:
        nqg = NQG

    nc = bass.Bass("TRN2", target_bir_lowering=False, debug=False)
    xfm = nc.dram_tensor("xfm", [D, S], F32, kind="ExternalInput").ap()
    wq = nc.dram_tensor("wq", [D, 256], F32, kind="ExternalInput").ap()
    wk = nc.dram_tensor("wk", [D, 64], F32, kind="ExternalInput").ap()
    wvz = nc.dram_tensor("wvz", [D, 65], F32, kind="ExternalInput").ap()
    o_out0 = nc.dram_tensor("o_out0", [128, S], BF16, kind="ExternalOutput").ap()
    o_out1 = nc.dram_tensor("o_out1", [128, S], BF16, kind="ExternalOutput").ap()
    if causal:
        cmask_d = nc.dram_tensor("cmask", [4, 128, 512], BF16, kind="ExternalInput").ap()
    else:
        rqc_d = nc.dram_tensor("rqc", [128, 2 * S], F32, kind="ExternalInput").ap()
        rqs_d = nc.dram_tensor("rqs", [128, 2 * S], F32, kind="ExternalInput").ap()
        rkc_d = nc.dram_tensor("rkc", [64, S], F32, kind="ExternalInput").ap()
        rks_d = nc.dram_tensor("rks", [64, S], F32, kind="ExternalInput").ap()
    if dump:
        qdmp = nc.dram_tensor("qdmp", [128, 2 * S], BF16, kind="ExternalOutput").ap()
        kdmp = nc.dram_tensor("kdmp", [128, S], BF16, kind="ExternalOutput").ap()
        vdmp = nc.dram_tensor("vdmp", [128, NKT * 65], BF16, kind="ExternalOutput").ap()
        bdmp = nc.dram_tensor("bdmp", [128, 4], F32, kind="ExternalOutput").ap()

    ctx = ExitStack()
    with ctx:
        sb = lambda n, sh, dt: ctx.enter_context(nc.sbuf_tensor(n, sh, dt))
        psa = lambda n, sh: ctx.enter_context(nc.psum_tensor(n, sh, F32))
        sem = lambda n: Sem(ctx.enter_context(nc.semaphore(n)))

        q_pk = sb("q_pk", [128, 2 * S], BF16)
        ksb = sb("ksb", [128, S], BF16)
        vsb = sb("vsb", [128, NKT * 65], BF16)
        o_fm = sb("o_fm", [128, 2 * S], BF16)
        xin = sb("xin", [128, 2 * 4096], BF16)
        ptb = sb("ptb", [128, 8 * 512], BF16)
        wq_sb = sb("wq_sb", [128, 8 * 256], BF16)
        wk_sb = sb("wk_sb", [128, 8 * 64], BF16)
        wv_sb = sb("wv_sb", [128, 8 * 65], BF16)
        qmx = sb("qmx", [128, 8], F32)
        sqmx = sb("sqmx", [128, 8], F32)
        ltmp = sb("ltmp", [32, 512], F32)
        linv = sb("linv", [1, 512], F32)
        linv2 = sb("linv2", [1, 512], F32)
        lb = sb("lb", [64, 512], F32)
        ones_f = sb("ones_f", [128, 1], F32)
        ones_r = sb("ones_r", [1, 128], F32)
        stat = sb("stat", [1, 16], F32)
        stat2 = sb("stat2", [1, 16], F32)
        bias_c = sb("bias_c", [128, 4], F32)
        if causal:
            cm_sb = sb("cm_sb", [128, 4 * 512], BF16)
        else:
            q32 = sb("q32", [128, 512], F32)
            qsw = sb("qsw", [128, 512], F32)
            qt1 = sb("qt1", [128, 512], F32)
            qt2 = sb("qt2", [128, 512], F32)
            rtc = sb("rtc", [128, 2 * 512], F32)
            rts = sb("rts", [128, 2 * 512], F32)
            ktc = sb("ktc", [64, 512], F32)
            kts = sb("kts", [64, 512], F32)

        ring = psa("ring", [128, 3072])
        av = [psa("av0", [128, 512]), psa("av1", [128, 512])]
        psb_ = ring[:, 2048:2560]
        pst = ring[:, 2560:3072]

        s_wld = sem("s_wld")
        s_xin = sem("s_xin")
        s_xfree = sem("s_xfree")
        s_pmm = sem("s_pmm")
        s_pev = sem("s_pev")
        s_pevv = sem("s_pevv")
        s_st = sem("s_st")
        s_exp = sem("s_exp")
        s_msk = sem("s_msk")
        s_av = sem("s_av")
        s_ep = sem("s_ep")
        s_on = sem("s_on")
        s_bias = sem("s_bias")
        s_msc = sem("s_msc")
        s_out = sem("s_out")
        s_cml = sem("s_cml")

        with nc.Block() as block:
            def ph0_g(g):
                g.dma_start(out=wq_sb[:].rearrange("p (c n) -> p c n", c=8),
                            in_=wq.rearrange("(c p) n -> p c n", p=128)).then_inc(s_wld.h, 16)
                g.dma_start(out=wk_sb[:].rearrange("p (c n) -> p c n", c=8),
                            in_=wk.rearrange("(c p) n -> p c n", p=128)).then_inc(s_wld.h, 16)
                g.dma_start(out=wv_sb[:].rearrange("p (c n) -> p c n", c=8),
                            in_=wvz.rearrange("(c p) n -> p c n", p=128)).then_inc(s_wld.h, 16)
            block.gpsimd(ph0_g)
            s_wld.c += 48

            if causal:
                def ph0_s(sy):
                    sy.dma_start(out=cm_sb[:].rearrange("p (d n) -> p d n", d=4),
                                 in_=cmask_d.rearrange("d p n -> p d n")).then_inc(s_cml.h, 16)
                block.sync(ph0_s)
                s_cml.c += 16

            def ph0_v(v):
                v.memset(o_fm[:], 0.0)
                v.memset(ones_f[:], 1.0)
                v.memset(ones_r[:], 1.0)
                v.memset(stat[:], 0.0)
                v.memset(qmx[:], 0.0)
                v.memset(stat2[:], 0.0)
                v.memset(av[0][64:96, :], 0.0)
                v.memset(av[1][64:96, :], 0.0)
                v.memset(vsb[:, 64::65], 1.0).then_inc(s_msc.h, 1)
            block.vector(ph0_v)
            s_msc.c += 1

            s_rt = sem("s_rt")
            s_rp = sem("s_rp")
            rope_guard = [0]
            rope_end = {}
            rope_sub = {}
            for kc in range(0 if skip_b else NCH):
                xo = (kc % 2) * 4096

                def fB_g(g, kc=kc, xo=xo):
                    if kc >= 1:
                        g.wait_ge(s_xin.h, 16 * kc)
                    if kc >= 2:
                        if causal:
                            g.wait_ge(s_pev.h, 2 * (kc - 1))
                            g.wait_ge(s_pevv.h, kc - 1)
                        else:
                            g.wait_ge(s_rp.h, rope_end[kc - 2])
                    g.dma_start(
                        out=xin[:, xo:xo + 4096].rearrange("p (c t) -> p c t", c=8),
                        in_=xfm[:, kc * 512:(kc + 1) * 512].rearrange("(c p) t -> p c t", p=128),
                    ).then_inc(s_xin.h, 16)
                block.gpsimd(fB_g)
                s_xin.c += 16

                def fB_t(t, kc=kc, xo=xo):
                    t.wait_ge(s_xin.h, 16 * (kc + 1))
                    if kc == 0:
                        t.wait_ge(s_wld.h, 48)
                    else:
                        if causal:
                            t.wait_ge(s_pev.h, 2 * kc)
                            t.wait_ge(s_pevv.h, kc)
                        else:
                            t.wait_ge(s_rp.h, rope_sub[kc - 1][0])
                    for c in range(2):
                        if (not causal) and kc > 0 and c == 1:
                            t.wait_ge(s_rp.h, rope_sub[kc - 1][1])
                        for k in range(8):
                            inst = t.matmul(out=ring[:, c * 512:(c + 1) * 512],
                                     lhsT=wq_sb[:, k * 256 + c * 128: k * 256 + (c + 1) * 128],
                                     rhs=xin[:, xo + k * 512: xo + (k + 1) * 512],
                                     start=(k == 0), stop=(k == 7))
                            if c == 1 and k == 7:
                                inst.then_inc(s_pmm.h, 1)
                    if (not causal) and kc > 0:
                        t.wait_ge(s_rp.h, rope_sub[kc - 1][2])
                    for k in range(8):
                        inst = t.matmul(out=ring[0:64, 1024:1536],
                                 lhsT=wk_sb[:, k * 64:(k + 1) * 64],
                                 rhs=xin[:, xo + k * 512: xo + (k + 1) * 512],
                                 start=(k == 0), stop=(k == 7))
                        if k == 7:
                            inst.then_inc(s_pmm.h, 1)
                    if (not causal) and kc > 0:
                        t.wait_ge(s_rp.h, rope_sub[kc - 1][3])
                    for kt in range(4):
                        for k in range(8):
                            inst = t.matmul(out=ring[:, 1536 + kt * 65: 1536 + kt * 65 + 65],
                                     lhsT=xin[:, xo + k * 512 + kt * 128: xo + k * 512 + (kt + 1) * 128],
                                     rhs=wv_sb[:, k * 65:(k + 1) * 65],
                                     start=(k == 0), stop=(k == 7))
                            if kt == 3 and k == 7:
                                inst.then_inc(s_pmm.h, 1)
                block.tensor(fB_t)
                s_pmm.c += 3

                if causal:
                    def fB_s(s, kc=kc):
                        s.wait_ge(s_pmm.h, 3 * kc + 1)
                        for c in range(2):
                            inst = s.activation(out=q_pk[:, c * S + kc * 512: c * S + (kc + 1) * 512],
                                         in_=ring[:, c * 512:(c + 1) * 512],
                                         func=AF.Copy, scale=float(qscale))
                            if c == 1:
                                inst.then_inc(s_pev.h, 1)
                        s.wait_ge(s_pmm.h, 3 * kc + 3)
                        for kt in range(4):
                            inst = s.activation(out=vsb[:, (kc * 4 + kt) * 65: (kc * 4 + kt) * 65 + 64],
                                         in_=ring[:, 1536 + kt * 65: 1536 + kt * 65 + 64],
                                         func=AF.Copy)
                            if kt == 3:
                                inst.then_inc(s_pev.h, 1)
                    block.scalar(fB_s)
                    s_pev.c += 2

                    def fB_v(v, kc=kc):
                        v.wait_ge(s_pmm.h, 3 * kc + 2)
                        v.tensor_copy(out=ksb[0:64, kc * 512:(kc + 1) * 512], in_=ring[0:64, 1024:1536])
                        v.tensor_copy(out=ksb[64:128, kc * 512:(kc + 1) * 512],
                                      in_=ring[0:64, 1024:1536]).then_inc(s_pevv.h, 1)
                    block.vector(fB_v)
                    s_pevv.c += 1
                else:
                    def rt_s(sy, kc=kc, prev=s_rt.c, guard=rope_guard[0]):
                        sy.wait_ge(s_rt.h, prev)
                        if guard > 0:
                            sy.wait_ge(s_rp.h, guard)
                        for c in range(2):
                            sy.dma_start(out=rtc[:, c * 512:(c + 1) * 512],
                                         in_=rqc_d[:, c * S + kc * 512: c * S + (kc + 1) * 512]).then_inc(s_rt.h, 16)
                            sy.dma_start(out=rts[:, c * 512:(c + 1) * 512],
                                         in_=rqs_d[:, c * S + kc * 512: c * S + (kc + 1) * 512]).then_inc(s_rt.h, 16)
                        sy.dma_start(out=ktc[:], in_=rkc_d[:, kc * 512:(kc + 1) * 512]).then_inc(s_rt.h, 16)
                        sy.dma_start(out=kts[:], in_=rks_d[:, kc * 512:(kc + 1) * 512]).then_inc(s_rt.h, 16)
                    block.sync(rt_s)
                    s_rt.c += 96
                    rt_need = s_rt.c

                    def rchain_s(emitfn):
                        need = s_rp.c

                        def f(s, need=need):
                            s.wait_ge(s_rp.h, need)
                            emitfn(s).then_inc(s_rp.h, 1)
                        block.scalar(f)
                        s_rp.c += 1

                    def rchain_v(emitfn):
                        need = s_rp.c

                        def f(v, need=need):
                            v.wait_ge(s_rp.h, need)
                            emitfn(v).then_inc(s_rp.h, 1)
                        block.vector(f)
                        s_rp.c += 1

                    _sub = []
                    for c in range(2):
                        def ev_q(s, kc=kc, c=c, rt_need=rt_need):
                            s.wait_ge(s_pmm.h, 3 * kc + 1)
                            s.wait_ge(s_rt.h, rt_need)
                            return s.activation(out=q32[:], in_=ring[:, c * 512:(c + 1) * 512],
                                                func=AF.Copy)
                        rchain_s(ev_q)
                        _sub.append(s_rp.c)

                        def swp(v, c=c):
                            v.tensor_copy(out=qsw[0:32, :], in_=q32[32:64, :])
                            v.tensor_copy(out=qsw[32:64, :], in_=q32[0:32, :])
                            v.tensor_copy(out=qsw[64:96, :], in_=q32[96:128, :])
                            v.tensor_copy(out=qsw[96:128, :], in_=q32[64:96, :])
                            return v.tensor_tensor(out=qt1[:], in0=q32[:],
                                                   in1=rtc[:, c * 512:(c + 1) * 512],
                                                   op=ALU.mult)
                        rchain_v(swp)
                        rchain_v(lambda v, c=c: v.tensor_tensor(
                            out=qt2[:], in0=qsw[:], in1=rts[:, c * 512:(c + 1) * 512],
                            op=ALU.mult))
                        rchain_v(lambda v: v.tensor_tensor(
                            out=qt1[:], in0=qt1[:], in1=qt2[:], op=ALU.add))
                        rchain_s(lambda s, kc=kc, c=c: s.activation(
                            out=q_pk[:, c * S + kc * 512: c * S + (kc + 1) * 512],
                            in_=qt1[:], func=AF.Copy, scale=float(qscale)))
                    def ev_k(s, kc=kc):
                        s.wait_ge(s_pmm.h, 3 * kc + 2)
                        return s.activation(out=q32[0:64, :], in_=ring[0:64, 1024:1536],
                                            func=AF.Copy)
                    rchain_s(ev_k)
                    _sub.append(s_rp.c)

                    def swpk(v):
                        v.tensor_copy(out=qsw[0:32, :], in_=q32[32:64, :])
                        v.tensor_copy(out=qsw[32:64, :], in_=q32[0:32, :])
                        return v.tensor_tensor(out=qt1[0:64, :], in0=q32[0:64, :],
                                               in1=ktc[:], op=ALU.mult)
                    rchain_v(swpk)
                    rchain_v(lambda v: v.tensor_tensor(
                        out=qt2[0:64, :], in0=qsw[0:64, :], in1=kts[:], op=ALU.mult))
                    rchain_v(lambda v: v.tensor_tensor(
                        out=qt1[0:64, :], in0=qt1[0:64, :], in1=qt2[0:64, :], op=ALU.add))
                    rchain_s(lambda s, kc=kc: s.activation(
                        out=ksb[0:64, kc * 512:(kc + 1) * 512], in_=qt1[0:64, :],
                        func=AF.Copy))
                    rchain_v(lambda v, kc=kc: v.tensor_copy(
                        out=ksb[64:128, kc * 512:(kc + 1) * 512],
                        in_=ksb[0:64, kc * 512:(kc + 1) * 512]))
                    def ev_v(s, kc=kc):
                        s.wait_ge(s_pmm.h, 3 * kc + 3)
                        for kt in range(4):
                            inst = s.activation(out=vsb[:, (kc * 4 + kt) * 65: (kc * 4 + kt) * 65 + 64],
                                         in_=ring[:, 1536 + kt * 65: 1536 + kt * 65 + 64],
                                         func=AF.Copy)
                        return inst
                    rchain_s(ev_v)
                    _sub.append(s_rp.c)
                    rope_guard[0] = s_rp.c
                    rope_end[kc] = s_rp.c
                    rope_sub[kc] = tuple(_sub)
            if skip_stats:
                NCH_C = 0
            else:
                NCH_C = NCH
            s_sq = sem("s_sq")
            s_qn = sem("s_qn")
            s_qnv = sem("s_qnv")
            sqq = sb("sqq", [128, 2 * 512], F32)
            sqk = sb("sqk", [64, 2 * 512], F32)
            qmaxrow = sb("qmaxrow", [1, 80], F32)
            statbank = [pst, psb_]

            sqq2 = sb("sqq2", [128, 2 * 512], F32)

            def phC_act(s):
                for kc in range(NCH_C):
                    if kc >= 2:
                        s.wait_ge(s_qn.h, 5 * (kc - 1))
                    if kc == 0:
                        s.wait_ge(s_pev.h, s_pev.c)
                        s.wait_ge(s_pevv.h, s_pevv.c)
                        if not causal:
                            s.wait_ge(s_rp.h, s_rp.c)
                    bo = (kc % 2) * 512
                    s.activation(out=sqq[:, bo:bo + 512],
                                 in_=q_pk[:, kc * 512:kc * 512 + 512],
                                 func=AF.Square).then_inc(s_sq.h, 1)
                    s.activation(out=sqq2[:, bo:bo + 512],
                                 in_=q_pk[:, S + kc * 512: S + kc * 512 + 512],
                                 func=AF.Square).then_inc(s_sq.h, 1)
                    s.activation(out=sqk[:, bo:bo + 512],
                                 in_=ksb[0:64, kc * 512:kc * 512 + 512],
                                 func=AF.Square).then_inc(s_sq.h, 1)
            block.scalar(phC_act)
            s_sq.c += 3 * NCH_C

            def phC_pe(t):
                for kc in range(NCH_C):
                    bo = (kc % 2) * 512
                    for j in range(5):
                        mi = kc * 5 + j
                        if mi >= 2:
                            t.wait_ge(s_qnv.h, mi - 1)
                        if j < 4:
                            c, half = j // 2, j % 2
                            t.wait_ge(s_sq.h, 3 * kc + c + 1)
                            src_ = (sqq if c == 0 else sqq2)
                            base = half * 64
                            t.matmul(out=statbank[mi % 2][0:1, 0:512],
                                     lhsT=ones_f[base:base + 64, :],
                                     rhs=src_[base:base + 64, bo:bo + 512],
                                     start=True, stop=True).then_inc(s_qn.h, 1)
                        else:
                            t.wait_ge(s_sq.h, 3 * kc + 3)
                            t.matmul(out=statbank[mi % 2][0:1, 0:512],
                                     lhsT=ones_f[0:64, :],
                                     rhs=sqk[:, bo:bo + 512],
                                     start=True, stop=True).then_inc(s_qn.h, 1)
            block.tensor(phC_pe)
            s_qn.c += 5 * NCH_C

            def phC_dve(v):
                for kc in range(NCH_C):
                    for j in range(5):
                        mi = kc * 5 + j
                        v.wait_ge(s_qn.h, mi + 1)
                        if j < 4:
                            h = 2 * (j // 2) + (j % 2)
                            col = h * 16 + kc
                        else:
                            col = 64 + kc
                        v.reduce_max(out=qmaxrow[0:1, col:col + 1],
                                     in_=statbank[mi % 2][0:1, 0:512],
                                     axis=AX.X).then_inc(s_qnv.h, 1)
                v.nop().then_inc(s_qnv.h, 1)
            block.vector(phC_dve)
            s_qnv.c += 5 * NCH_C + 1

            def phC_fin(v):
                v.wait_ge(s_qnv.h, s_qnv.c)
                if skip_stats:
                    v.memset(stat[0:1, 0:5], 1.0).then_inc(s_ep.h, 1)
                    return
                for h in range(4):
                    v.reduce_max(out=stat[0:1, h:h + 1],
                                 in_=qmaxrow[0:1, h * 16:(h + 1) * 16], axis=AX.X)
                v.reduce_max(out=stat[0:1, 4:5],
                             in_=qmaxrow[0:1, 64:80], axis=AX.X).then_inc(s_ep.h, 1)
            block.vector(phC_fin)
            s_ep.c += 1

            def phC_s2(s):
                s.wait_ge(s_ep.h, s_ep.c)
                s.activation(out=stat2[0:1, 0:4], in_=stat[0:1, 0:4], func=AF.Ln)
                s.activation(out=stat2[0:1, 4:5], in_=stat[0:1, 4:5], func=AF.Ln).then_inc(s_ep.h, 1)
            block.scalar(phC_s2)
            s_ep.c += 1

            def phC_v3(v):
                v.wait_ge(s_ep.h, s_ep.c)
                v.tensor_tensor(out=stat2[0:1, 8:12], in0=stat2[0:1, 0:4],
                                in1=stat2[0:1, 4:5].broadcast_to([1, 4]),
                                op=ALU.add).then_inc(s_ep.h, 1)
            block.vector(phC_v3)
            s_ep.c += 1

            def phC_s3(s):
                s.wait_ge(s_ep.h, s_ep.c)
                s.activation(out=stat2[0:1, 12:16], in_=stat2[0:1, 8:12],
                             func=AF.Exp, scale=0.5).then_inc(s_ep.h, 1)
            block.scalar(phC_s3)
            s_ep.c += 1

            def phC_v4(v):
                v.wait_ge(s_ep.h, s_ep.c)
                v.tensor_scalar(out=stat[0:1, 8:12], in0=stat2[0:1, 12:16],
                                scalar1=-1.0, scalar2=SHIFT,
                                op0=ALU.mult, op1=ALU.add).then_inc(s_ep.h, 1)
            block.vector(phC_v4)
            s_ep.c += 1

            def phC_t2(t):
                t.wait_ge(s_ep.h, s_ep.c)
                for h in range(4):
                    inst = t.matmul(out=psb_[:, h:h + 1], lhsT=ones_r[:],
                             rhs=stat[0:1, 8 + h:9 + h], start=True, stop=True)
                inst.then_inc(s_ep.h, 1)
            block.tensor(phC_t2)
            s_ep.c += 1

            def phC_v5(v):
                v.wait_ge(s_ep.h, s_ep.c)
                v.tensor_copy(out=bias_c[:], in_=psb_[:, 0:4]).then_inc(s_bias.h, 1)
            block.vector(phC_v5)
            s_bias.c += 1

            class WC:
                def __init__(self):
                    self.seen = {}

                def w(self, eng, key, semx, val):
                    if val <= 0:
                        return
                    if self.seen.get((key, id(semx)), -1) >= val:
                        return
                    eng.wait_ge(semx.h, val)
                    self.seen[(key, id(semx))] = val

            wc = WC()
            bank_free_at = [0, 0, 0, 0, 0, 0]
            av_free_at = [0, 0]
            first_mask = [True]
            pending_e4 = []
            avi = 0
            first_attn = True

            for qg in range(nqg):
                bound = 4 * qg + 4 if causal else NKT
                npairs = bound // 2
                for h in range(4):
                    base = 64 * (h % 2)
                    cb = (h // 2) * S
                    avb = av[avi]
                    av_S0 = s_av.c
                    st_S0 = s_st.c
                    exp_S0 = s_exp.c
                    av_guard = av_free_at[avi]
                    exp_at = {}
                    msk_at = {}
                    for pi in range(npairs):
                        s_exp.c += 1
                        exp_at[2 * pi] = s_exp.c
                        exp_at[2 * pi + 1] = s_exp.c
                    if causal:
                        for kt in range(bound - 4, bound):
                            s_msk.c += 1
                            msk_at[kt] = s_msk.c
                    ring_entry = list(bank_free_at)

                    def fD_t(t, qg=qg, h=h, base=base, cb=cb, avb=avb,
                             bound=bound, npairs=npairs, exp_at=exp_at,
                             msk_at=msk_at, av_S0=av_S0, av_guard=av_guard,
                             first=first_attn, ring_entry=ring_entry,
                             pend=list(pending_e4)):
                        if first:
                            t.wait_ge(s_pev.h, s_pev.c)
                            t.wait_ge(s_pevv.h, s_pevv.c)
                            t.wait_ge(s_rp.h, s_rp.c)
                            t.wait_ge(s_msc.h, 1)
                            t.wait_ge(s_bias.h, 1)
                        n_av = [0]

                        def emit_av(kt):
                            if kt == 0:
                                wc.w(t, "pe", s_on, av_guard)
                            wc.w(t, "pe", s_exp, exp_at[kt])
                            if kt in msk_at:
                                wc.w(t, "pe", s_msk, msk_at[kt])
                            t.matmul(out=avb[0:65, :],
                                     lhsT=vsb[:, kt * 65:(kt + 1) * 65],
                                     rhs=ptb[:, (kt % 8) * 512:(kt % 8 + 1) * 512],
                                     start=(kt == 0), stop=(kt == bound - 1)
                                     ).then_inc(s_av.h, 1)
                            n_av[0] += 1

                        for pi in range(npairs):
                            for kt in (2 * pi, 2 * pi + 1):
                                bkx = ((kt // 2) % 3) * 2 + (kt % 2)
                                if kt >= 6:
                                    wc.w(t, "pe", s_exp, exp_at[kt - 6])
                                else:
                                    wc.w(t, "pe", s_exp, ring_entry[bkx])
                                inst = t.matmul(
                                    out=ring[:, bkx * 512:(bkx + 1) * 512],
                                    lhsT=ksb[base:base + 64, kt * 128:(kt + 1) * 128],
                                    rhs=q_pk[base:base + 64, cb + qg * 512: cb + (qg + 1) * 512],
                                    start=True, stop=True)
                                if kt % 2 == 1:
                                    inst.then_inc(s_st.h, 1)
                            if pi == 1 and pend:
                                for e4 in pend:
                                    e4(t)
                                pend.clear()
                            if pi >= 2:
                                emit_av(2 * pi - 4)
                                emit_av(2 * pi - 3)
                        for e4 in pend:
                            e4(t)
                        pend.clear()
                        for kt in range(n_av[0], bound):
                            emit_av(kt)
                    block.tensor(fD_t)
                    pending_e4 = []
                    first_attn = False
                    s_st.c += npairs
                    s_av.c += bound
                    for kt in range(max(0, bound - 6), bound):
                        bank_free_at[((kt // 2) % 3) * 2 + (kt % 2)] = exp_at[kt]

                    def fD_s(s, h=h, npairs=npairs, st_S0=st_S0, exp_S0=exp_S0,
                             av_S0=av_S0):
                        for pi in range(npairs):
                            kt0 = 2 * pi
                            wc.w(s, "act", s_st, st_S0 + pi + 1)
                            if exp_S0 + pi == 0:
                                s.wait_ge(s_bias.h, 1)
                            if kt0 >= 8:
                                wc.w(s, "act", s_av, av_S0 + kt0 - 6)
                            pb = (kt0 // 2) % 3
                            slot = kt0 % 8
                            s.activation(out=ptb[:, slot * 512:(slot + 2) * 512],
                                         in_=ring[:, pb * 1024:(pb + 1) * 1024],
                                         func=AF.Exp, bias=bias_c[:, h:h + 1]
                                         ).then_inc(s_exp.h, 1)
                    block.scalar(fD_s)

                    if causal:
                        def fD_vm(v, qg=qg, bound=bound, exp_at=dict(exp_at)):
                            if first_mask[0]:
                                v.wait_ge(s_cml.h, 16)
                                first_mask[0] = False
                            for kt in range(bound - 4, bound):
                                d = kt - 4 * qg
                                sl = (kt % 8) * 512
                                wc.w(v, "dve", s_exp, exp_at[kt])
                                v.tensor_tensor(out=ptb[:, sl:sl + 512],
                                                in0=ptb[:, sl:sl + 512],
                                                in1=cm_sb[:, d * 512:(d + 1) * 512],
                                                op=ALU.mult).then_inc(s_msk.h, 1)
                        block.vector(fD_vm)

                    e1_val = s_ep.c + 1
                    e2_val = s_ep.c + 2
                    e3_val = s_ep.c + 3
                    e4_val = s_ep.c + 4
                    e5_val = s_ep.c + 5

                    def fD_v(v, qg=qg, h=h, base=base, cb=cb, avb=avb,
                             av_tot=s_av.c, e1_val=e1_val, e2_val=e2_val,
                             e4_val=e4_val, e5_val=e5_val):
                        wc.w(v, "dve", s_av, av_tot)
                        v.tensor_copy(out=ltmp[:], in_=avb[64:96, :]).then_inc(s_ep.h, 1)
                        v.wait_ge(s_ep.h, e1_val)
                        v.tensor_scalar(out=linv[:], in0=ltmp[0:1, :], scalar1=LEPS,
                                        scalar2=None, op0=ALU.add).then_inc(s_ep.h, 1)
                        v.wait_ge(s_ep.h, e2_val)
                        v.reciprocal(out=linv2[:], in_=linv[:]).then_inc(s_ep.h, 1)
                        v.wait_ge(s_ep.h, e4_val)
                        v.tensor_copy(out=lb[:], in_=avb[64:128, :]).then_inc(s_ep.h, 1)
                        v.wait_ge(s_ep.h, e5_val)
                        v.tensor_tensor(out=o_fm[base:base + 64, cb + qg * 512: cb + (qg + 1) * 512],
                                        in0=avb[0:64, :], in1=lb[:],
                                        op=ALU.mult).then_inc(s_on.h, 1)
                    block.vector(fD_v)
                    s_ep.c += 5
                    s_on.c += 1
                    av_free_at[avi] = s_on.c
                    avi ^= 1

                    def mk_e4(e3_val=e3_val, avb=avb):
                        def e4(t):
                            t.wait_ge(s_ep.h, e3_val)
                            t.matmul(out=avb[64:128, :], lhsT=ones_r[0:1, 0:64],
                                     rhs=linv2[:], start=True, stop=True,
                                     tile_position=(0, 64)).then_inc(s_ep.h, 1)
                        return e4
                    pending_e4.append(mk_e4())

            def fD_flush(t):
                for e4 in pending_e4:
                    e4(t)
                pending_e4.clear()
            block.tensor(fD_flush)

            ndma = [2]

            def phE(sy):
                sy.wait_ge(s_msc.h, 1)
                sy.wait_ge(s_cml.h, s_cml.c)
                sy.wait_ge(s_xin.h, s_xin.c)
                sy.wait_ge(s_wld.h, s_wld.c)
                sy.wait_ge(s_rt.h, s_rt.c)
                sy.wait_ge(s_on.h, s_on.c)
                sy.dma_start(out=o_out0, in_=o_fm[:, 0:S]).then_inc(s_out.h, 16)
                sy.dma_start(out=o_out1, in_=o_fm[:, S:2 * S]).then_inc(s_out.h, 16)
                if dump:
                    sy.wait_ge(s_bias.h, 1)
                    sy.dma_start(out=qdmp, in_=q_pk[:]).then_inc(s_out.h, 16)
                    sy.dma_start(out=kdmp, in_=ksb[:]).then_inc(s_out.h, 16)
                    sy.dma_start(out=vdmp, in_=vsb[:]).then_inc(s_out.h, 16)
                    sy.dma_start(out=bdmp, in_=bias_c[:]).then_inc(s_out.h, 16)
                    ndma[0] += 4
                sy.wait_ge(s_out.h, 16 * ndma[0])
            block.sync(phE)
            s_out.c += 16 * ndma[0]

    return nc



def build_block_tail(layer, ntb=4, dump=False, do_ffn=True):
    nc = bass.Bass("TRN2", target_bir_lowering=False, debug=False)
    o_in = nc.dram_tensor("o_in", [D, TPC], BF16, kind="ExternalInput").ap()
    resid_in = nc.dram_tensor("resid_in", [D, TPC], F32, kind="ExternalInput").ap()
    wo_d = nc.dram_tensor("wo", [D, D], F32, kind="ExternalInput").ap()
    s2_out = nc.dram_tensor("s2_out", [D, TPC], F32, kind="ExternalOutput").ap()
    if layer == 0:
        k0t_d = nc.dram_tensor("k0t", [D, 4096], U8, kind="ExternalInput").ap()
        k0b_d = nc.dram_tensor("k0b", [D, 4096], U8, kind="ExternalInput").ap()
        k1_d = nc.dram_tensor("k1", [4096, 4096], U8, kind="ExternalInput").ap()
        k2_d = nc.dram_tensor("k2", [4096, D], U8, kind="ExternalInput").ap()
        scl_d = nc.dram_tensor("scl", [1, 4], F32, kind="ExternalInput").ap()
    else:
        w1_d = nc.dram_tensor("w1t", [D, 4096], F32, kind="ExternalInput").ap()
        w2_d = nc.dram_tensor("w2t", [D, 4096], F32, kind="ExternalInput").ap()
        h1_d = nc.dram_tensor("h1t", [4096, 4096], F32, kind="ExternalInput").ap()
        h2_d = nc.dram_tensor("h2t", [4096, 4096], F32, kind="ExternalInput").ap()
        w3_d = nc.dram_tensor("w3t", [4096, D], F32, kind="ExternalInput").ap()
        sv1_d = nc.dram_tensor("sv1", [128, 32], F32, kind="ExternalInput").ap()
        sv2_d = nc.dram_tensor("sv2", [128, 32], F32, kind="ExternalInput").ap()
        sh1_d = nc.dram_tensor("sh1", [128, 32], F32, kind="ExternalInput").ap()
        sh2_d = nc.dram_tensor("sh2", [128, 32], F32, kind="ExternalInput").ap()
        sv3_d = nc.dram_tensor("sv3", [128, 8], F32, kind="ExternalInput").ap()
    if dump:
        s1_out = nc.dram_tensor("s1_out", [D, TPC], F32, kind="ExternalOutput").ap()
    s1_dram = nc.dram_tensor("s1_dram", [D, TPC], F32, kind="Internal").ap()

    ctx = ExitStack()
    with ctx:
        sb = lambda n, sh, dt: ctx.enter_context(nc.sbuf_tensor(n, sh, dt))
        psa = lambda n, sh: ctx.enter_context(nc.psum_tensor(n, sh, F32))
        sem = lambda n: Sem(ctx.enter_context(nc.semaphore(n)))

        bufA = sb("bufA", [128, 32 * 1024], BF16)
        bufB = sb("bufB", [128, 32 * 1024], BF16)
        rsm = sb("rsm", [128, 8 * 512], F32)
        xh = sb("xh", [128, 8 * 1024], BF16)
        wbig = sb("wbig", [128, 32 * 128], BF16)
        weff = sb("weff", [128, 8 * 512], BF16)
        wstg = sb("wstg", [128, 8 * 512], BF16)
        sb_corr = sb("sb_corr", [128, 1024], F32)
        sqv = sb("sqv", [128, 512], F32)
        if layer == 0:
            tmp2 = sb("tmp2_sb", [128, 1024], F32)
        rowm = sb("rowm", [1, 1536], F32)
        rb = sb("rb", [128, 512], F32)
        mb = sb("mb", [128, 512], F32)
        tmp1 = sb("tmp1", [128, 512], F32)
        ones_f = sb("ones_f", [128, 1], F32)
        ones_b = sb("ones_b", [128, 1], BF16)
        ones_r = sb("ones_r", [1, 128], F32)
        scl = sb("scl_sb", [128, 4], F32)
        scl4 = sb("scl4_sb", [128, 4], F32)
        if layer == 1:
            sv1 = sb("sv1_sb", [128, 32], F32)
            sv2 = sb("sv2_sb", [128, 32], F32)
            sh1 = sb("sh1_sb", [128, 32], F32)
            sh2 = sb("sh2_sb", [128, 32], F32)
            sv3 = sb("sv3_sb", [128, 8], F32)
            gtmp = sb("gtmp", [128, 1024], BF16)

        ring = psa("ring", [128, 2048])
        pb4 = psa("pb4", [128, 512])
        pb5 = psa("pb5", [128, 512])
        pb6 = psa("pb6", [128, 512])
        pb7 = psa("pb7", [128, 512])

        s_ldg = sem("s_ldg")
        s_lds = sem("s_lds")
        s_wo = sem("s_wo")
        s_ev = sem("s_ev")
        s_ch = sem("s_ch")
        s_msc = sem("s_msc")
        s_st1 = sem("s_st1")
        s_out = sem("s_out")

        with nc.Block() as block:
            def t0_g(g):
                g.dma_start(out=bufA[:, 0:16384].rearrange("p (c n) -> p c n", c=8),
                            in_=o_in.rearrange("(c p) n -> p c n", p=128)).then_inc(s_ldg.h, 16)
                g.dma_start(out=bufB[:, 0:8192].rearrange("p (c n) -> p c n", c=8),
                            in_=wo_d.rearrange("(c p) n -> p c n", p=128)).then_inc(s_ldg.h, 16)
            block.gpsimd(t0_g)
            s_ldg.c += 32

            def t0_s(sy):
                if layer == 0:
                    sy.dma_start(out=scl[0:1, :], in_=scl_d).then_inc(s_lds.h, 16)
                else:
                    sy.dma_start(out=sv1[:], in_=sv1_d).then_inc(s_lds.h, 16)
                    sy.dma_start(out=sv2[:], in_=sv2_d).then_inc(s_lds.h, 16)
                    sy.dma_start(out=sh1[:], in_=sh1_d).then_inc(s_lds.h, 16)
                    sy.dma_start(out=sh2[:], in_=sh2_d).then_inc(s_lds.h, 16)
                    sy.dma_start(out=sv3[:], in_=sv3_d).then_inc(s_lds.h, 16)
            block.sync(t0_s)
            s_lds.c += 16 if layer == 0 else 80

            def t0_v(v):
                v.memset(ones_f[:], 1.0)
                v.memset(ones_b[:], 1.0)
                v.memset(ones_r[:], 1.0).then_inc(s_msc.h, 1)
            block.vector(t0_v)
            s_msc.c += 1

            if layer == 0:
                def t0_t(t):
                    t.wait_ge(s_lds.h, s_lds.c)
                    t.wait_ge(s_msc.h, 1)
                    t.matmul(out=pb6[:, 0:4], lhsT=ones_r[:], rhs=scl[0:1, 0:4],
                             start=True, stop=True).then_inc(s_ch.h, 1)
                block.tensor(t0_t)
                s_ch.c += 1

                def t0_v2(v):
                    v.wait_ge(s_ch.h, s_ch.c)
                    v.tensor_scalar(out=scl[:], in0=pb6[:, 0:4], scalar1=2.0,
                                    scalar2=None, op0=ALU.mult).then_inc(s_ch.h, 1)
                block.vector(t0_v2)
                s_ch.c += 1

                def t0_v3(v):
                    v.wait_ge(s_ch.h, s_ch.c)
                    v.tensor_scalar(out=scl4[:], in0=scl[:], scalar1=2.0,
                                    scalar2=None, op0=ALU.mult).then_inc(s_ch.h, 1)
                block.vector(t0_v3)
                s_ch.c += 1

            def chain_t(emitfn):
                need = s_ch.c

                def f(t, need=need):
                    t.wait_ge(s_ch.h, need)
                    emitfn(t).then_inc(s_ch.h, 1)
                block.tensor(f)
                s_ch.c += 1

            def chain_v(emitfn):
                need = s_ch.c

                def f(v, need=need):
                    v.wait_ge(s_ch.h, need)
                    emitfn(v).then_inc(s_ch.h, 1)
                block.vector(f)
                s_ch.c += 1

            def chain_s(emitfn):
                need = s_ch.c

                def f(s, need=need):
                    s.wait_ge(s_ch.h, need)
                    emitfn(s).then_inc(s_ch.h, 1)
                block.scalar(f)
                s_ch.c += 1

            def emit_ln(src_resid, dst_xh, extra_pe_wait=None):
                def sum_mm(t):
                    if extra_pe_wait is not None:
                        extra_pe_wait(t)
                    for c in range(8):
                        inst = t.matmul(out=pb4[0:1, :], lhsT=ones_f[:],
                                        rhs=src_resid(c), start=(c == 0),
                                        stop=(c == 7))
                    return inst
                chain_t(sum_mm)
                for c in range(8):
                    chain_s(lambda s, c=c: s.activation(
                        out=sqv[:], in_=src_resid(c), func=AF.Square))
                    chain_t(lambda t, c=c: t.matmul(
                        out=pb5[0:1, :], lhsT=ones_f[:], rhs=sqv[:],
                        start=(c == 0), stop=(c == 7)))

                def v1(v):
                    v.tensor_scalar(out=rowm[0:1, 0:512], in0=pb4[0:1, :],
                                    scalar1=1.0 / D, scalar2=None, op0=ALU.mult)
                    return v.tensor_scalar(out=rowm[0:1, 512:1024], in0=pb5[0:1, :],
                                           scalar1=1.0 / D, scalar2=None,
                                           op0=ALU.mult)
                chain_v(v1)
                chain_s(lambda s: s.activation(out=rowm[0:1, 1024:1536],
                                               in_=rowm[0:1, 0:512], func=AF.Square))
                chain_v(lambda v: v.scalar_tensor_tensor(
                    out=rowm[0:1, 512:1024], in0=rowm[0:1, 512:1024], scalar=1e-5,
                    in1=rowm[0:1, 1024:1536], op0=ALU.add, op1=ALU.subtract))
                chain_s(lambda s: s.activation(out=rowm[0:1, 1024:1536],
                                               in_=rowm[0:1, 512:1024], func=AF.Ln))
                chain_s(lambda s: s.activation(out=rowm[0:1, 512:1024],
                                               in_=rowm[0:1, 1024:1536],
                                               func=AF.Exp, scale=-0.5))
                chain_v(lambda v: v.tensor_tensor(out=rowm[0:1, 1024:1536],
                                                  in0=rowm[0:1, 0:512],
                                                  in1=rowm[0:1, 512:1024],
                                                  op=ALU.mult))
                chain_v(lambda v: v.tensor_scalar(out=rowm[0:1, 1024:1536],
                                                  in0=rowm[0:1, 1024:1536],
                                                  scalar1=-1.0, scalar2=None,
                                                  op0=ALU.mult))

                def bc(t):
                    t.matmul(out=pb6[:, :], lhsT=ones_r[:],
                             rhs=rowm[0:1, 512:1024], start=True, stop=True)
                    return t.matmul(out=pb7[:, :], lhsT=ones_r[:],
                                    rhs=rowm[0:1, 1024:1536], start=True, stop=True)
                chain_t(bc)

                def cp(v):
                    v.tensor_copy(out=rb[:], in_=pb6[:, :])
                    return v.tensor_copy(out=mb[:], in_=pb7[:, :])
                chain_v(cp)
                for c in range(8):
                    chain_v(lambda v, c=c: v.tensor_tensor(
                        out=tmp1[:], in0=src_resid(c), in1=rb[:], op=ALU.mult))
                    chain_v(lambda v, c=c: v.tensor_tensor(
                        out=src_resid(c), in0=tmp1[:], in1=mb[:], op=ALU.add))
                    if dst_xh is not None:
                        chain_s(lambda s, c=c: s.activation(
                            out=dst_xh(c), in_=src_resid(c), func=AF.Copy))

            for tb in range(ntb):
                def t1_s(sy, tb=tb, prev=s_lds.c, ch_need=s_ch.c):
                    sy.wait_ge(s_lds.h, prev)
                    sy.wait_ge(s_ch.h, ch_need)
                    sy.dma_start(
                        out=rsm[:].rearrange("p (c n) -> p c n", c=8),
                        in_=resid_in[:, tb * 512:(tb + 1) * 512].rearrange("(c p) n -> p c n", p=128),
                    ).then_inc(s_lds.h, 16)
                block.sync(t1_s)
                s_lds.c += 16

                def t1_t(t, tb=tb):
                    if tb == 0:
                        t.wait_ge(s_ldg.h, 32)
                    for dc in range(8):
                        bk = dc % 4
                        if tb * 8 + dc >= 4:
                            t.wait_ge(s_ev.h, tb * 8 + dc - 3)
                        for k in range(8):
                            inst = t.matmul(
                                out=ring[:, bk * 512:(bk + 1) * 512],
                                lhsT=bufB[:, k * 1024 + dc * 128: k * 1024 + (dc + 1) * 128],
                                rhs=bufA[:, k * 2048 + tb * 512: k * 2048 + tb * 512 + 512],
                                start=(k == 0), stop=(k == 7))
                        inst.then_inc(s_wo.h, 1)
                block.tensor(t1_t)
                s_wo.c += 8

                def t1_v(v, tb=tb, lds_need=s_lds.c, ch_need=s_ch.c):
                    v.wait_ge(s_lds.h, lds_need)
                    v.wait_ge(s_ch.h, ch_need)
                    for dc in range(8):
                        bk = dc % 4
                        v.wait_ge(s_wo.h, tb * 8 + dc + 1)
                        v.tensor_tensor(out=rsm[:, dc * 512:(dc + 1) * 512],
                                        in0=rsm[:, dc * 512:(dc + 1) * 512],
                                        in1=ring[:, bk * 512:(bk + 1) * 512],
                                        op=ALU.add).then_inc(s_ev.h, 1)
                block.vector(t1_v)
                s_ev.c += 8

                ev_need = s_ev.c

                def pe_wait(t, ev_need=ev_need):
                    t.wait_ge(s_ev.h, ev_need)

                def src_resid(c):
                    return rsm[:, c * 512:(c + 1) * 512]

                emit_ln(src_resid, None, extra_pe_wait=pe_wait)

                def park_chain(sy, tb=tb, need=s_ch.c, prev=s_st1.c):
                    sy.wait_ge(s_ch.h, need)
                    sy.wait_ge(s_st1.h, prev)
                    sy.dma_start(
                        out=s1_dram[:, tb * 512:(tb + 1) * 512].rearrange("(c p) n -> p c n", p=128),
                        in_=rsm[:].rearrange("p (c n) -> p c n", c=8),
                    ).then_inc(s_st1.h, 16)
                    if dump:
                        sy.dma_start(
                            out=s1_out[:, tb * 512:(tb + 1) * 512].rearrange("(c p) n -> p c n", p=128),
                            in_=rsm[:].rearrange("p (c n) -> p c n", c=8),
                        ).then_inc(s_st1.h, 16)
                block.sync(park_chain)
                s_st1.c += 32 if dump else 16
                def park_guard(v, need=s_st1.c):
                    v.wait_ge(s_st1.h, need)
                    v.nop().then_inc(s_ch.h, 1)
                block.vector(park_guard)
                s_ch.c += 1

            s_fw = sem("s_fw")
            fwsem = [sem("s_fwA"), sem("s_fwB")]
            s_fm = sem("s_fm")
            s_ed = sem("s_ed")
            s_es = sem("s_es")
            wbig2 = wstg

            class Banks:
                def __init__(self):
                    self.aps = [ring[:, i * 512:(i + 1) * 512] for i in range(4)] +                                [pb4[:, :], pb5[:, :], pb6[:, :], pb7[:, :]]
                    self.freed = [(None, 0)] * 8

                def wait_free(self, t, b):
                    semx, cnt = self.freed[b]
                    if semx is not None and cnt > 0:
                        t.wait_ge(semx.h, cnt)

            bk = Banks()

            def pipe_mm_stage(nblocks, nbuf, emit_dma, n_dma, emit_mm_group,
                              emit_evac, pe_extra_first=None, mm_per_group=1,
                              groups_per_block=2):
                m_after_block = {}
                entry_waits = [(s_fm, s_fm.c), (s_ed, s_ed.c), (s_es, s_es.c)]
                for i in range(nblocks):
                    buf = i % nbuf

                    def f_g(g, i=i, buf=buf, prev_m=m_after_block.get(i - nbuf, 0),
                            prev_fw=fwsem[buf].c, entry_waits=entry_waits,
                            nbuf=nbuf):
                        if prev_fw > 0:
                            g.wait_ge(fwsem[buf].h, prev_fw)
                        if i < nbuf:
                            for semx, cnt in entry_waits:
                                if cnt > 0:
                                    g.wait_ge(semx.h, cnt)
                        if prev_m > 0:
                            g.wait_ge(s_fm.h, prev_m)
                        emit_dma(g, i, buf)
                    block.gpsimd(f_g)
                    fwsem[buf].c += 16 * n_dma
                    fw_need = (fwsem[buf], fwsem[buf].c)

                    for ts in range(groups_per_block):
                        b = (2 * (i % 2) + ts) % 8 if groups_per_block == 2 else (i % 4)

                        def f_t(t, i=i, buf=buf, ts=ts, b=b, fw_need=fw_need,
                                first=(i == 0 and ts == 0)):
                            t.wait_ge(fw_need[0].h, fw_need[1])
                            if first and pe_extra_first is not None:
                                pe_extra_first(t)
                            bk.wait_free(t, b)
                            emit_mm_group(t, i, buf, ts, bk.aps[b]).then_inc(s_fm.h, 1)
                        block.tensor(f_t)
                        s_fm.c += 1
                        bk.freed[b] = emit_evac(i, ts, b, s_fm.c)
                    m_after_block[i] = s_fm.c

            def evac_act(fn):
                def mk(i, ts, b, m_cnt):
                    def f(s, fn=fn, i=i, ts=ts, b=b, m_cnt=m_cnt):
                        s.wait_ge(s_fm.h, m_cnt)
                        fn(s, i, ts, bk.aps[b]).then_inc(s_es.h, 1)
                    block.scalar(f)
                    s_es.c += 1
                    return (s_es, s_es.c)
                return mk

            def evac_dve(fn):
                def mk(i, ts, b, m_cnt):
                    def f(v, fn=fn, i=i, ts=ts, b=b, m_cnt=m_cnt):
                        v.wait_ge(s_fm.h, m_cnt)
                        fn(v, i, ts, bk.aps[b]).then_inc(s_ed.h, 1)
                    block.vector(f)
                    s_ed.c += 1
                    return (s_ed, s_ed.c)
                return mk

            def stage_barrier(t):
                t.wait_ge(s_ed.h, s_ed.c)
                t.wait_ge(s_es.h, s_es.c)

            n_s2out = [0]
            if do_ffn:
                wo_done = s_wo.c
                park_all = s_st1.c
                lds_all = s_lds.c
                for th in range(2):
                    stage1_mm_after = [0]

                    def fA_g(g, th=th, st1=park_all, prev_m=stage1_mm_after[0],
                             prev_fw=s_fw.c):
                        g.wait_ge(s_st1.h, st1)
                        if prev_fw > 0:
                            g.wait_ge(s_fw.h, prev_fw)
                        if prev_m > 0:
                            g.wait_ge(s_fm.h, prev_m)
                        g.dma_start(
                            out=xh[:].rearrange("p (c n) -> p c n", c=8),
                            in_=s1_dram[:, th * 1024:(th + 1) * 1024].rearrange("(c p) n -> p c n", p=128),
                        ).then_inc(s_fw.h, 16)
                    block.gpsimd(fA_g)
                    s_fw.c += 16
                    xh_ready = s_fw.c

                    def pe_first(t, th=th, xh_ready=xh_ready, ch_now=s_ch.c,
                                 ed_now=s_ed.c, es_now=s_es.c):
                        t.wait_ge(s_fw.h, xh_ready)
                        t.wait_ge(s_ch.h, ch_now)
                        if th == 0:
                            t.wait_ge(s_wo.h, wo_done)
                            t.wait_ge(s_lds.h, lds_all)
                        else:
                            t.wait_ge(s_ed.h, ed_now)
                            t.wait_ge(s_es.h, es_now)
                    s_ch_entry = [s_ch.c]

                    if layer == 1:
                        def dma1(g, i, buf):
                            g.dma_start(
                                out=weff[:, buf * 2048: buf * 2048 + 1024].rearrange("p (c n) -> p c n", c=8),
                                in_=w1_d[:, i * 128:(i + 1) * 128].rearrange("(c p) n -> p c n", p=128),
                            ).then_inc(fwsem[buf].h, 16)
                            g.dma_start(
                                out=weff[:, buf * 2048 + 1024: buf * 2048 + 2048].rearrange("p (c n) -> p c n", c=8),
                                in_=w2_d[:, i * 128:(i + 1) * 128].rearrange("(c p) n -> p c n", p=128),
                            ).then_inc(fwsem[buf].h, 16)

                        def mm1(t, i, buf, ts, bank_ap):
                            bA = 2 * (i % 2) + ts
                            bB = 4 + bA
                            bk.wait_free(t, bB)
                            for k in range(8):
                                t.matmul(out=bk.aps[bA],
                                         lhsT=weff[:, buf * 2048 + k * 128: buf * 2048 + (k + 1) * 128],
                                         rhs=xh[:, k * 1024 + ts * 512: k * 1024 + ts * 512 + 512],
                                         start=(k == 0), stop=(k == 7))
                            for k in range(8):
                                inst = t.matmul(out=bk.aps[bB],
                                         lhsT=weff[:, buf * 2048 + 1024 + k * 128: buf * 2048 + 1024 + (k + 1) * 128],
                                         rhs=xh[:, k * 1024 + ts * 512: k * 1024 + ts * 512 + 512],
                                         start=(k == 0), stop=(k == 7))
                            return inst

                        gslot_free = {0: 0, 1: 0}

                        def evac1(i, ts, b, m_cnt):
                            bA = 2 * (i % 2) + ts
                            bB = 4 + bA
                            gslot = ts * 512

                            def f_s(s, i=i, gslot=gslot, bA=bA, m_cnt=m_cnt,
                                    war=gslot_free[ts]):
                                s.wait_ge(s_fm.h, m_cnt)
                                if war > 0:
                                    s.wait_ge(s_ed.h, war)
                                s.activation(out=gtmp[:, gslot:gslot + 512],
                                             in_=bk.aps[bA], func=AF.Silu,
                                             scale=sv1[:, i:i + 1]).then_inc(s_es.h, 1)
                            block.scalar(f_s)
                            s_es.c += 1
                            esc = s_es.c

                            def f_v(v, i=i, ts=ts, gslot=gslot, bB=bB, esc=esc):
                                v.wait_ge(s_es.h, esc)
                                v.scalar_tensor_tensor(
                                    out=bufA[:, i * 1024 + ts * 512: i * 1024 + ts * 512 + 512],
                                    in0=bk.aps[bB], scalar=sv2[:, i:i + 1],
                                    in1=gtmp[:, gslot:gslot + 512],
                                    op0=ALU.mult, op1=ALU.mult).then_inc(s_ed.h, 1)
                            block.vector(f_v)
                            s_ed.c += 1
                            gslot_free[ts] = s_ed.c
                            bk.freed[bA] = (s_es, esc)
                            return (s_ed, s_ed.c)

                        pipe_mm_stage(32, 2, dma1, 2, mm1, evac1,
                                      pe_extra_first=pe_first)
                        stage1_mm_after[0] = s_fm.c

                        def dma2(g, i, buf, wsrc=h1_d):
                            dst = wbig if buf == 0 else wbig2
                            g.dma_start(
                                out=dst[:].rearrange("p (c n) -> p c n", c=32),
                                in_=wsrc[:, i * 128:(i + 1) * 128].rearrange("(c p) n -> p c n", p=128),
                            ).then_inc(fwsem[buf].h, 16)

                        def mk_mm_big(src_buf):
                            def mm(t, i, buf, ts, bank_ap):
                                w = wbig if buf == 0 else wbig2
                                for k in range(32):
                                    inst = t.matmul(out=bank_ap,
                                             lhsT=w[:, k * 128:(k + 1) * 128],
                                             rhs=src_buf[:, k * 1024 + ts * 512: k * 1024 + ts * 512 + 512],
                                             start=(k == 0), stop=(k == 31))
                                return inst
                            return mm

                        def mk_gelu_evac(dst_buf, scol):
                            def fn(s, i, ts, bank_ap):
                                return s.activation(out=dst_buf[:, i * 1024 + ts * 512: i * 1024 + ts * 512 + 512],
                                             in_=bank_ap, func=AF.Gelu,
                                             scale=scol[:, i:i + 1])
                            return evac_act(fn)

                        def barrier_first(t):
                            stage_barrier(t)
                        pipe_mm_stage(32, 2, dma2, 1, mk_mm_big(bufA),
                                      mk_gelu_evac(bufB, sh1)(0, 0, 0, 0) if False else mk_gelu_evac(bufB, sh1),
                                      pe_extra_first=barrier_first)

                        def dma3(g, i, buf):
                            dst = wbig if buf == 0 else wbig2
                            g.dma_start(
                                out=dst[:].rearrange("p (c n) -> p c n", c=32),
                                in_=h2_d[:, i * 128:(i + 1) * 128].rearrange("(c p) n -> p c n", p=128),
                            ).then_inc(fwsem[buf].h, 16)
                        pipe_mm_stage(32, 2, dma3, 1, mk_mm_big(bufB),
                                      mk_gelu_evac(bufA, sh2),
                                      pe_extra_first=barrier_first)

                        for ts in range(2):
                            def s4_s(sy, th=th, ts=ts, prev=s_lds.c, need=s_ch.c,
                                     ed=s_ed.c, es=s_es.c):
                                sy.wait_ge(s_lds.h, prev)
                                sy.wait_ge(s_ch.h, need)
                                col = th * 1024 + ts * 512
                                sy.dma_start(
                                    out=rsm[:].rearrange("p (c n) -> p c n", c=8),
                                    in_=s1_dram[:, col:col + 512].rearrange("(c p) n -> p c n", p=128),
                                ).then_inc(s_lds.h, 16)
                            block.sync(s4_s)
                            s_lds.c += 16
                            lds_need = s_lds.c

                            def dma4(g, i, buf):
                                dst = wbig if buf == 0 else wbig2
                                g.dma_start(
                                    out=dst[:].rearrange("p (c n) -> p c n", c=32),
                                    in_=w3_d[:, i * 128:(i + 1) * 128].rearrange("(c p) n -> p c n", p=128),
                                ).then_inc(fwsem[buf].h, 16)

                            def mm4(t, i, buf, _ts, bank_ap, ts=ts,
                                    lds_need=lds_need):
                                if i == 0:
                                    t.wait_ge(s_lds.h, lds_need)
                                w = wbig if buf == 0 else wbig2
                                for k in range(32):
                                    inst = t.matmul(out=bank_ap,
                                             lhsT=w[:, k * 128:(k + 1) * 128],
                                             rhs=bufA[:, k * 1024 + ts * 512: k * 1024 + ts * 512 + 512],
                                             start=(k == 0), stop=(k == 31))
                                return inst

                            def ev4(v, i, _ts, bank_ap, ts=ts):
                                return v.scalar_tensor_tensor(
                                    out=rsm[:, i * 512:(i + 1) * 512],
                                    in0=bank_ap, scalar=sv3[:, i:i + 1],
                                    in1=rsm[:, i * 512:(i + 1) * 512],
                                    op0=ALU.mult, op1=ALU.add)
                            pipe_mm_stage(8, 2, dma4, 1, mm4, evac_dve(ev4),
                                          pe_extra_first=(barrier_first if ts == 0 else None),
                                          groups_per_block=1)

                            def lnent(t, ed=s_ed.c, es=s_es.c):
                                t.wait_ge(s_ed.h, ed)
                                t.wait_ge(s_es.h, es)
                            emit_ln(lambda c: rsm[:, c * 512:(c + 1) * 512], None,
                                    extra_pe_wait=lnent)

                            def s2out_s(sy, th=th, ts=ts, need=s_ch.c, prev=s_out.c):
                                sy.wait_ge(s_ch.h, need)
                                sy.wait_ge(s_out.h, prev)
                                col = th * 1024 + ts * 512
                                sy.dma_start(
                                    out=s2_out[:, col:col + 512].rearrange("(c p) n -> p c n", p=128),
                                    in_=rsm[:].rearrange("p (c n) -> p c n", c=8),
                                ).then_inc(s_out.h, 16)
                            block.sync(s2out_s)
                            s_out.c += 16
                            n_s2out[0] += 1

                            def s2out_guard(v, need=s_out.c):
                                v.wait_ge(s_out.h, need)
                                v.nop().then_inc(s_ch.h, 1)
                            block.vector(s2out_guard)
                            s_ch.c += 1

                    else:
                        def dma1(g, i, buf):
                            g.dma_start(
                                out=weff[:, buf * 2048: buf * 2048 + 2048].rearrange("p (c n) -> p c n", c=8),
                                in_=k0t_d[:, i * 256:(i + 1) * 256].rearrange("(c p) n -> p c n", p=128),
                            ).then_inc(fwsem[buf].h, 16)
                            g.dma_start(
                                out=wstg[:, buf * 2048: buf * 2048 + 2048].rearrange("p (c n) -> p c n", c=8),
                                in_=k0b_d[:, i * 256:(i + 1) * 256].rearrange("(c p) n -> p c n", p=128),
                            ).then_inc(fwsem[buf].h, 16)

                        sub_done = {}

                        def mm1(t, i, buf, ts, bank_ap, sub_done=sub_done):
                            t.wait_ge(s_ed.h, sub_done[i])
                            for d in range(2):
                                b2 = 2 * (i % 2) + d
                                if ts == 0:
                                    bk.wait_free(t, b2)
                                    bk2 = 4 + b2
                                    bk.wait_free(t, bk2)
                                for k in range(8):
                                    inst = t.matmul(
                                        out=bk.aps[2 * (i % 2) + d] if ts == 0 else bk.aps[4 + 2 * (i % 2) + d],
                                        lhsT=weff[:, buf * 2048 + k * 256 + d * 128: buf * 2048 + k * 256 + (d + 1) * 128],
                                        rhs=xh[:, k * 1024 + ts * 512: k * 1024 + ts * 512 + 512],
                                        start=(k == 0), stop=(k == 7))
                            return inst

                        def evac1(i, ts, b, m_cnt):
                            def f_v(v, i=i, ts=ts, m_cnt=m_cnt):
                                v.wait_ge(s_fm.h, m_cnt)
                                for d in range(2):
                                    bidx = (2 * (i % 2) + d) + (0 if ts == 0 else 4)
                                    dd = i * 2 + d
                                    inst = v.tensor_scalar(
                                        out=bufA[:, dd * 1024 + ts * 512: dd * 1024 + ts * 512 + 512],
                                        in0=bk.aps[bidx], scalar1=scl[:, 0:1],
                                        scalar2=0.0, op0=ALU.mult, op1=ALU.max)
                                inst.then_inc(s_ed.h, 1)
                            block.vector(f_v)
                            s_ed.c += 1
                            for d in range(2):
                                bidx = (2 * (i % 2) + d) + (0 if ts == 0 else 4)
                                bk.freed[bidx] = (s_ed, s_ed.c)
                            return (s_ed, s_ed.c)

                        m_after = {}
                        s1_entry = [(s_fm, s_fm.c), (s_ed, s_ed.c), (s_es, s_es.c)]
                        for i in range(16):
                            buf = i % 2

                            def f_g(g, i=i, buf=buf, prev_m=m_after.get(i - 2, 0),
                                    prev_fw=fwsem[buf].c, s1_entry=s1_entry):
                                if prev_fw > 0:
                                    g.wait_ge(fwsem[buf].h, prev_fw)
                                if i < 2:
                                    for semx, cnt in s1_entry:
                                        if cnt > 0:
                                            g.wait_ge(semx.h, cnt)
                                if prev_m > 0:
                                    g.wait_ge(s_fm.h, prev_m)
                                dma1(g, i, buf)
                            block.gpsimd(f_g)
                            fwsem[buf].c += 32
                            fw_need = (fwsem[buf], fwsem[buf].c)

                            def f_sub(v, i=i, buf=buf, fw_need=fw_need,
                                      prev_m=m_after.get(i - 2, 0)):
                                v.wait_ge(fw_need[0].h, fw_need[1])
                                if prev_m > 0:
                                    v.wait_ge(s_fm.h, prev_m)
                                v.tensor_tensor(out=weff[:, buf * 2048: buf * 2048 + 2048],
                                                in0=weff[:, buf * 2048: buf * 2048 + 2048],
                                                in1=wstg[:, buf * 2048: buf * 2048 + 2048],
                                                op=ALU.subtract).then_inc(s_ed.h, 1)
                            block.vector(f_sub)
                            s_ed.c += 1
                            sub_done[i] = s_ed.c

                            for ts in range(2):
                                def f_t(t, i=i, buf=buf, ts=ts,
                                        first=(i == 0 and ts == 0)):
                                    if first:
                                        pe_first(t)
                                    mm1(t, i, buf, ts, None).then_inc(s_fm.h, 1)
                                block.tensor(f_t)
                                s_fm.c += 1
                                evac1(i, ts, None, s_fm.c)
                            m_after[i] = s_fm.c
                        stage1_mm_after[0] = s_fm.c

                        def barrier_first(t):
                            stage_barrier(t)

                        for ts in range(2):
                            def srow_t(t, ts=ts, first=(ts == 0)):
                                if first:
                                    stage_barrier(t)
                                bk.wait_free(t, 4 + ts)
                                for k in range(32):
                                    inst = t.matmul(out=bk.aps[4 + ts][0:1, :],
                                                    lhsT=ones_b[:],
                                                    rhs=bufA[:, k * 1024 + ts * 512: k * 1024 + ts * 512 + 512],
                                                    start=(k == 0), stop=(k == 31))
                                inst.then_inc(s_fm.h, 1)
                            block.tensor(srow_t)
                            s_fm.c += 1

                            def srow_v(v, ts=ts, m_cnt=s_fm.c):
                                v.wait_ge(s_fm.h, m_cnt)
                                v.tensor_scalar(out=rowm[0:1, 0:512],
                                                in0=bk.aps[4 + ts][0:1, :],
                                                scalar1=scl[0:1, 1:2], scalar2=None,
                                                op0=ALU.mult).then_inc(s_ed.h, 1)
                            block.vector(srow_v)
                            s_ed.c += 1
                            bk.freed[4 + ts] = (s_ed, s_ed.c)

                            def srow_t2(t, ts=ts, e_cnt=s_ed.c):
                                t.wait_ge(s_ed.h, e_cnt)
                                bk.wait_free(t, 6 + ts)
                                t.matmul(out=bk.aps[6 + ts], lhsT=ones_r[:],
                                         rhs=rowm[0:1, 0:512], start=True,
                                         stop=True).then_inc(s_fm.h, 1)
                            block.tensor(srow_t2)
                            s_fm.c += 1

                            def srow_v2(v, ts=ts, m_cnt=s_fm.c):
                                v.wait_ge(s_fm.h, m_cnt)
                                v.tensor_copy(out=sb_corr[:, ts * 512:(ts + 1) * 512],
                                              in_=bk.aps[6 + ts]).then_inc(s_ed.h, 1)
                            block.vector(srow_v2)
                            s_ed.c += 1
                            bk.freed[6 + ts] = (s_ed, s_ed.c)

                        def dma2(g, i, buf):
                            dst = wbig if buf == 0 else wbig2
                            g.dma_start(
                                out=dst[:].rearrange("p (c n) -> p c n", c=32),
                                in_=k1_d[:, i * 128:(i + 1) * 128].rearrange("(c p) n -> p c n", p=128),
                            ).then_inc(fwsem[buf].h, 16)

                        def mk_mm_big(src_buf):
                            def mm(t, i, buf, ts, bank_ap):
                                w = wbig if buf == 0 else wbig2
                                for k in range(32):
                                    inst = t.matmul(out=bank_ap,
                                             lhsT=w[:, k * 128:(k + 1) * 128],
                                             rhs=src_buf[:, k * 1024 + ts * 512: k * 1024 + ts * 512 + 512],
                                             start=(k == 0), stop=(k == 31))
                                return inst
                            return mm

                        tslot_free = {0: 0, 1: 0}

                        def evac2(i, ts, b, m_cnt):
                            tslot = ts * 512

                            def f_v(v, i=i, ts=ts, b=b, m_cnt=m_cnt, tslot=tslot,
                                    war=tslot_free[ts]):
                                v.wait_ge(s_fm.h, m_cnt)
                                if war > 0:
                                    v.wait_ge(s_es.h, war)
                                v.scalar_tensor_tensor(
                                    out=tmp2[:, tslot:tslot + 512],
                                    in0=bk.aps[b], scalar=scl4[:, 1:2],
                                    in1=sb_corr[:, ts * 512:(ts + 1) * 512],
                                    op0=ALU.mult, op1=ALU.subtract).then_inc(s_ed.h, 1)
                            block.vector(f_v)
                            s_ed.c += 1
                            edc = s_ed.c

                            def f_s(s, i=i, ts=ts, edc=edc, tslot=tslot):
                                s.wait_ge(s_ed.h, edc)
                                s.activation(
                                    out=bufB[:, i * 1024 + ts * 512: i * 1024 + ts * 512 + 512],
                                    in_=tmp2[:, tslot:tslot + 512], func=AF.Relu).then_inc(s_es.h, 1)
                            block.scalar(f_s)
                            s_es.c += 1
                            tslot_free[ts] = s_es.c
                            return (s_ed, edc)
                        pipe_mm_stage(32, 2, dma2, 1, mk_mm_big(bufA), evac2)

                        for ts in range(2):
                            def srow_t(t, ts=ts, first=(ts == 0)):
                                if first:
                                    stage_barrier(t)
                                bk.wait_free(t, 4 + ts)
                                for k in range(32):
                                    inst = t.matmul(out=bk.aps[4 + ts][0:1, :],
                                                    lhsT=ones_b[:],
                                                    rhs=bufB[:, k * 1024 + ts * 512: k * 1024 + ts * 512 + 512],
                                                    start=(k == 0), stop=(k == 31))
                                inst.then_inc(s_fm.h, 1)
                            block.tensor(srow_t)
                            s_fm.c += 1

                            def srow_v(v, ts=ts, m_cnt=s_fm.c):
                                v.wait_ge(s_fm.h, m_cnt)
                                v.tensor_scalar(out=rowm[0:1, 0:512],
                                                in0=bk.aps[4 + ts][0:1, :],
                                                scalar1=scl[0:1, 2:3], scalar2=None,
                                                op0=ALU.mult).then_inc(s_ed.h, 1)
                            block.vector(srow_v)
                            s_ed.c += 1
                            bk.freed[4 + ts] = (s_ed, s_ed.c)

                            def srow_t2(t, ts=ts, e_cnt=s_ed.c):
                                t.wait_ge(s_ed.h, e_cnt)
                                bk.wait_free(t, 6 + ts)
                                t.matmul(out=bk.aps[6 + ts], lhsT=ones_r[:],
                                         rhs=rowm[0:1, 0:512], start=True,
                                         stop=True).then_inc(s_fm.h, 1)
                            block.tensor(srow_t2)
                            s_fm.c += 1

                            def srow_v2(v, ts=ts, m_cnt=s_fm.c):
                                v.wait_ge(s_fm.h, m_cnt)
                                v.tensor_copy(out=sb_corr[:, ts * 512:(ts + 1) * 512],
                                              in_=bk.aps[6 + ts]).then_inc(s_ed.h, 1)
                            block.vector(srow_v2)
                            s_ed.c += 1
                            bk.freed[6 + ts] = (s_ed, s_ed.c)

                        for ts in range(2):
                            def s3_s(sy, th=th, ts=ts, prev=s_lds.c, need=s_ch.c):
                                sy.wait_ge(s_lds.h, prev)
                                sy.wait_ge(s_ch.h, need)
                                col = th * 1024 + ts * 512
                                sy.dma_start(
                                    out=rsm[:].rearrange("p (c n) -> p c n", c=8),
                                    in_=s1_dram[:, col:col + 512].rearrange("(c p) n -> p c n", p=128),
                                ).then_inc(s_lds.h, 16)
                            block.sync(s3_s)
                            s_lds.c += 16
                            lds_need = s_lds.c

                            def dma3(g, i, buf):
                                dst = wbig if buf == 0 else wbig2
                                g.dma_start(
                                    out=dst[:].rearrange("p (c n) -> p c n", c=32),
                                    in_=k2_d[:, i * 128:(i + 1) * 128].rearrange("(c p) n -> p c n", p=128),
                                ).then_inc(fwsem[buf].h, 16)

                            def mm3(t, i, buf, _ts, bank_ap, ts=ts,
                                    lds_need=lds_need):
                                if i == 0:
                                    t.wait_ge(s_lds.h, lds_need)
                                w = wbig if buf == 0 else wbig2
                                for k in range(32):
                                    inst = t.matmul(out=bank_ap,
                                             lhsT=w[:, k * 128:(k + 1) * 128],
                                             rhs=bufB[:, k * 1024 + ts * 512: k * 1024 + ts * 512 + 512],
                                             start=(k == 0), stop=(k == 31))
                                return inst

                            def ev3(i, _ts, b, m_cnt, ts=ts):
                                def f_v(v, i=i, b=b, m_cnt=m_cnt, ts=ts,
                                        war=s_ed.c):
                                    v.wait_ge(s_fm.h, m_cnt)
                                    if war > 0:
                                        v.wait_ge(s_ed.h, war)
                                    v.scalar_tensor_tensor(
                                        out=tmp1[:], in0=bk.aps[b],
                                        scalar=scl4[:, 2:3],
                                        in1=sb_corr[:, ts * 512:(ts + 1) * 512],
                                        op0=ALU.mult, op1=ALU.subtract).then_inc(s_ed.h, 1)
                                block.vector(f_v)
                                s_ed.c += 1
                                edc = s_ed.c

                                def f_v2(v, i=i, edc=edc):
                                    v.wait_ge(s_ed.h, edc)
                                    v.tensor_tensor(
                                        out=rsm[:, i * 512:(i + 1) * 512],
                                        in0=rsm[:, i * 512:(i + 1) * 512],
                                        in1=tmp1[:], op=ALU.add).then_inc(s_ed.h, 1)
                                block.vector(f_v2)
                                s_ed.c += 1
                                return (s_ed, edc)
                            pipe_mm_stage(8, 2, dma3, 1, mm3, ev3,
                                          pe_extra_first=(barrier_first if ts == 0 else None),
                                          groups_per_block=1)

                            def lnent(t, ed=s_ed.c, es=s_es.c):
                                t.wait_ge(s_ed.h, ed)
                                t.wait_ge(s_es.h, es)
                            emit_ln(lambda c: rsm[:, c * 512:(c + 1) * 512], None,
                                    extra_pe_wait=lnent)

                            def s2out_s(sy, th=th, ts=ts, need=s_ch.c, prev=s_out.c):
                                sy.wait_ge(s_ch.h, need)
                                sy.wait_ge(s_out.h, prev)
                                col = th * 1024 + ts * 512
                                sy.dma_start(
                                    out=s2_out[:, col:col + 512].rearrange("(c p) n -> p c n", p=128),
                                    in_=rsm[:].rearrange("p (c n) -> p c n", c=8),
                                ).then_inc(s_out.h, 16)
                            block.sync(s2out_s)
                            s_out.c += 16
                            n_s2out[0] += 1

                            def s2out_guard(v, need=s_out.c):
                                v.wait_ge(s_out.h, need)
                                v.nop().then_inc(s_ch.h, 1)
                            block.vector(s2out_guard)
                            s_ch.c += 1

            def tE(sy):
                sy.wait_ge(s_st1.h, s_st1.c)
                sy.wait_ge(s_ch.h, s_ch.c)
                if not do_ffn:
                    sy.dma_start(out=s2_out.rearrange("(c p) n -> p c n", p=128),
                                 in_=s1_dram.rearrange("(c p) n -> p c n", p=128)
                                 ).then_inc(s_out.h, 16)
                    s_out.c += 16
                sy.wait_ge(s_out.h, s_out.c)
                sy.wait_ge(s_ldg.h, s_ldg.c)
                sy.wait_ge(s_lds.h, s_lds.c)
                sy.wait_ge(s_fw.h, s_fw.c)
                sy.wait_ge(fwsem[0].h, fwsem[0].c)
                sy.wait_ge(fwsem[1].h, fwsem[1].c)
            block.sync(tE)

    return nc



def _f32c(a):
    return np.ascontiguousarray(np.asarray(a, dtype=np.float32))


def make_cmask():
    cm = np.zeros((4, 128, 512), dtype=ml_dtypes.bfloat16)
    tri = np.triu(np.ones((128, 128), dtype=np.float32))
    for d in range(4):
        for st in range(4):
            if st > d:
                cm[d, :, st * 128:(st + 1) * 128] = 1.0
            elif st == d:
                cm[d, :, st * 128:(st + 1) * 128] = tri.astype(ml_dtypes.bfloat16)
    return cm


def stage_l0_attn(x, Wq, Wk, Wv):
    xT = np.ascontiguousarray(x.transpose(0, 2, 1))
    WqT = _f32c(np.asarray(Wq).T)
    WkT = _f32c(np.asarray(Wk).T)
    WvT = _f32c(np.asarray(Wv).T)
    cm = make_cmask()
    in_maps = []
    for core in range(NC):
        b, g = core // 4, core % 4
        wvz = np.zeros((D, 65), dtype=np.float32)
        wvz[:, 0:64] = WvT[:, 64 * g:64 * g + 64]
        in_maps.append({
            "xfm": np.ascontiguousarray(xT[b]),
            "wq": np.ascontiguousarray(WqT[:, 256 * g:256 * g + 256]),
            "wk": np.ascontiguousarray(WkT[:, 64 * g:64 * g + 64]),
            "wvz": wvz,
            "cmask": cm,
        })
    return in_maps


def gather_o(results):
    o_full = np.zeros((B, D, S), dtype=ml_dtypes.bfloat16)
    for core in range(NC):
        b, g = core // 4, core % 4
        o_full[b, 256 * g:256 * g + 128] = results[core]["o_out0"]
        o_full[b, 256 * g + 128:256 * g + 256] = results[core]["o_out1"]
    return o_full


def _rope_tables():
    inv = 1.0 / (10000.0 ** (np.arange(0, D, 2, dtype=np.float64) / D))
    t = np.arange(S, dtype=np.float64)[:, None]
    ang = t * inv[None, :]
    return np.cos(ang).astype(np.float32), np.sin(ang).astype(np.float32)


_EO = np.concatenate([np.arange(0, 64, 2), np.arange(1, 64, 2)])


def stage_l1_attn(xfm_full, Wq, Wk, Wv):
    Wq = _f32c(Wq)
    Wk = _f32c(Wk)
    Wv = _f32c(Wv)
    cosT, sinT = _rope_tables()
    in_maps = []
    for core in range(NC):
        b, g = core // 4, core % 4
        wq = np.zeros((D, 256), np.float32)
        for i in range(4):
            h = 4 * g + i
            wq[:, 64 * i:64 * i + 64] = Wq[:, 64 * h:64 * h + 64][:, _EO]
        wk = np.ascontiguousarray(Wk[:, 64 * g:64 * g + 64][:, _EO])
        wvz = np.zeros((D, 65), np.float32)
        wvz[:, 0:64] = Wv[:, 64 * g:64 * g + 64]
        rqc = np.zeros((128, 2 * S), np.float32)
        rqs = np.zeros((128, 2 * S), np.float32)
        for c in range(2):
            for half in range(2):
                h = 4 * g + 2 * c + half
                cf = cosT[:, 32 * h:32 * h + 32].T
                sf = sinT[:, 32 * h:32 * h + 32].T
                r0 = half * 64
                rqc[r0:r0 + 32, c * S:(c + 1) * S] = cf
                rqc[r0 + 32:r0 + 64, c * S:(c + 1) * S] = cf
                rqs[r0:r0 + 32, c * S:(c + 1) * S] = -sf
                rqs[r0 + 32:r0 + 64, c * S:(c + 1) * S] = sf
        rkc = np.zeros((64, S), np.float32)
        rks = np.zeros((64, S), np.float32)
        ckf = cosT[:, 0:32].T
        skf = sinT[:, 0:32].T
        rkc[0:32] = ckf
        rkc[32:64] = ckf
        rks[0:32] = -skf
        rks[32:64] = skf
        in_maps.append({
            "xfm": np.ascontiguousarray(xfm_full[b]),
            "wq": wq, "wk": wk, "wvz": wvz,
            "rqc": rqc, "rqs": rqs, "rkc": rkc, "rks": rks,
        })
    return in_maps


def stage_l0_tail(o_full, x, Wo, k0, k1, k2, s0, s1, s2):
    xT = np.ascontiguousarray(x.transpose(0, 2, 1))
    WoT = _f32c(np.asarray(Wo).T)
    k0 = np.asarray(k0)
    k0t = np.ascontiguousarray(k0[0:D]).astype(np.uint8)
    k0b = np.ascontiguousarray(k0[D:2 * D]).astype(np.uint8)
    k1u = np.ascontiguousarray(np.asarray(k1)).astype(np.uint8)
    k2u = np.ascontiguousarray(np.asarray(k2)).astype(np.uint8)
    scl = np.array([[float(s0), float(s1), float(s2), 0.0]], np.float32)
    in_maps = []
    for core in range(NC):
        b, j = core // 4, core % 4
        in_maps.append({
            "o_in": np.ascontiguousarray(o_full[b][:, TPC * j:TPC * (j + 1)]),
            "resid_in": np.ascontiguousarray(xT[b][:, TPC * j:TPC * (j + 1)]),
            "wo": WoT,
            "k0t": k0t, "k0b": k0b, "k1": k1u, "k2": k2u, "scl": scl,
        })
    return in_maps


def gather_stream(results, key="s2_out"):
    s = np.zeros((B, D, S), dtype=np.float32)
    for core in range(NC):
        b, j = core // 4, core % 4
        s[b][:, TPC * j:TPC * (j + 1)] = results[core][key]
    return s


def run_layer0(x, inp):
    nc1 = build_attn_core(layer=0)
    r1 = run_bass_kernel_spmd(nc1, stage_l0_attn(
        x, inp["L0_attn_Wq"], inp["L0_attn_Wk"], inp["L0_attn_Wv"]),
        core_ids=list(range(NC)))
    o_full = gather_o(r1.results)
    nc2 = build_block_tail(layer=0)
    r2 = run_bass_kernel_spmd(nc2, stage_l0_tail(
        o_full, x, inp["L0_attn_Wo"], inp["L0_ffn_k0"], inp["L0_ffn_k1"],
        inp["L0_ffn_k2"], inp["L0_ffn_s0"], inp["L0_ffn_s1"], inp["L0_ffn_s2"]),
        core_ids=list(range(NC)))
    return gather_stream(r2.results)


def stage_l1_tail(o_full, stream1, inp):
    WoT = _f32c(inp["L1_attn_Wo"])
    w1t = _f32c(np.asarray(inp["L1_ffn_w1_w"]).T)
    w2t = _f32c(np.asarray(inp["L1_ffn_w2_w"]).T)
    h1t = _f32c(np.asarray(inp["L1_ffn_h1_w"]).T)
    h2t = _f32c(np.asarray(inp["L1_ffn_h2_w"]).T)
    w3t = _f32c(np.asarray(inp["L1_ffn_w3_w"]).T)
    pmj = lambda v: np.ascontiguousarray(np.asarray(v, np.float32).reshape(-1, 128).T)
    sv1 = pmj(inp["L1_ffn_w1_s"])
    sv2 = pmj(inp["L1_ffn_w2_s"])
    sh1 = pmj(inp["L1_ffn_h1_s"])
    sh2 = pmj(inp["L1_ffn_h2_s"])
    sv3 = pmj(inp["L1_ffn_w3_s"])
    in_maps = []
    for core in range(NC):
        b, j = core // 4, core % 4
        in_maps.append({
            "o_in": np.ascontiguousarray(o_full[b][:, TPC * j:TPC * (j + 1)]),
            "resid_in": np.ascontiguousarray(stream1[b][:, TPC * j:TPC * (j + 1)]),
            "wo": WoT,
            "w1t": w1t, "w2t": w2t, "h1t": h1t, "h2t": h2t, "w3t": w3t,
            "sv1": sv1, "sv2": sv2, "sh1": sh1, "sh2": sh2, "sv3": sv3,
        })
    return in_maps


_BUILT = {}


def _get(builder, key, *args, **kw):
    if key not in _BUILT:
        _BUILT[key] = builder(*args, **kw)
    return _BUILT[key]


def kernel(**inputs):
    x = np.asarray(inputs["L0_attn_hidden_states"], np.float32)
    cores = list(range(NC))

    nc = _get(build_attn_core, "a0", layer=0)
    r = run_bass_kernel_spmd(nc, stage_l0_attn(
        x, inputs["L0_attn_Wq"], inputs["L0_attn_Wk"], inputs["L0_attn_Wv"]),
        core_ids=cores)
    o_full = gather_o(r.results)
    nc = _get(build_block_tail, "t0", layer=0)
    r = run_bass_kernel_spmd(nc, stage_l0_tail(
        o_full, x, inputs["L0_attn_Wo"], inputs["L0_ffn_k0"], inputs["L0_ffn_k1"],
        inputs["L0_ffn_k2"], inputs["L0_ffn_s0"], inputs["L0_ffn_s1"],
        inputs["L0_ffn_s2"]), core_ids=cores)
    s2 = gather_stream(r.results)

    nc = _get(build_attn_core, "a1", layer=1)
    r = run_bass_kernel_spmd(nc, stage_l1_attn(
        s2, inputs["L1_attn_Wq"], inputs["L1_attn_Wk"], inputs["L1_attn_Wv"]),
        core_ids=cores)
    o_full = gather_o(r.results)
    nc = _get(build_block_tail, "t1", layer=1)
    r = run_bass_kernel_spmd(nc, stage_l1_tail(o_full, s2, inputs), core_ids=cores)
    s4 = gather_stream(r.results)

    return np.ascontiguousarray(s4.transpose(0, 2, 1))


# version 06
# speedup vs baseline: 1.2718x; 1.1973x over previous
import numpy as np
import ml_dtypes
from contextlib import ExitStack

import concourse.bass as bass
import concourse.mybir as mybir
from concourse.bass_utils import run_bass_kernel_spmd

F32 = mybir.dt.float32
BF16 = mybir.dt.bfloat16
U8 = mybir.dt.uint8
AF = mybir.ActivationFunctionType
ALU = mybir.AluOpType
AX = mybir.AxisListType

B, S, D = 2, 8192, 1024
NC = 8
TPC = 2048
NH, DH, KVH = 16, 64, 4
NKT = S // 128
NQG = S // 512
SHIFT = 44.0
LEPS = 1e-30
NCH = 16


class Sem:
    def __init__(self, h):
        self.h = h
        self.c = 0



def build_attn_core(layer, nqg=None, dump=False, skip_stats=False, skip_b=False):
    causal = (layer == 0)
    qscale = (DH ** -0.5) if layer == 0 else (D ** -0.5)
    if nqg is None:
        nqg = NQG

    nc = bass.Bass("TRN2", target_bir_lowering=False, debug=False)
    xfm = nc.dram_tensor("xfm", [D, S], F32, kind="ExternalInput").ap()
    wq = nc.dram_tensor("wq", [D, 256], F32, kind="ExternalInput").ap()
    wk = nc.dram_tensor("wk", [D, 64], F32, kind="ExternalInput").ap()
    wvz = nc.dram_tensor("wvz", [D, 65], F32, kind="ExternalInput").ap()
    o_out0 = nc.dram_tensor("o_out0", [128, S], BF16, kind="ExternalOutput").ap()
    o_out1 = nc.dram_tensor("o_out1", [128, S], BF16, kind="ExternalOutput").ap()
    if causal:
        cmask_d = nc.dram_tensor("cmask", [4, 128, 512], BF16, kind="ExternalInput").ap()
    else:
        rqc_d = nc.dram_tensor("rqc", [128, 2 * S], F32, kind="ExternalInput").ap()
        rqs_d = nc.dram_tensor("rqs", [128, 2 * S], F32, kind="ExternalInput").ap()
        rkc_d = nc.dram_tensor("rkc", [64, S], F32, kind="ExternalInput").ap()
        rks_d = nc.dram_tensor("rks", [64, S], F32, kind="ExternalInput").ap()
    if dump:
        qdmp = nc.dram_tensor("qdmp", [128, 2 * S], BF16, kind="ExternalOutput").ap()
        kdmp = nc.dram_tensor("kdmp", [128, S], BF16, kind="ExternalOutput").ap()
        vdmp = nc.dram_tensor("vdmp", [128, NKT * 65], BF16, kind="ExternalOutput").ap()
        bdmp = nc.dram_tensor("bdmp", [128, 4], F32, kind="ExternalOutput").ap()

    ctx = ExitStack()
    with ctx:
        sb = lambda n, sh, dt: ctx.enter_context(nc.sbuf_tensor(n, sh, dt))
        psa = lambda n, sh: ctx.enter_context(nc.psum_tensor(n, sh, F32))
        sem = lambda n: Sem(ctx.enter_context(nc.semaphore(n)))

        q_pk = sb("q_pk", [128, 2 * S], BF16)
        q_od = sb("q_od", [128, 2 * S], BF16)
        ksb = sb("ksb", [128, S], BF16)
        vsb = sb("vsb", [128, NKT * 65], BF16)
        o_fm = sb("o_fm", [128, 2 * S], BF16)
        xin = sb("xin", [128, 2 * 4096], BF16)
        ptb = sb("ptb", [128, 8 * 512], BF16)
        wq_sb = sb("wq_sb", [128, 8 * 256], BF16)
        wk_sb = sb("wk_sb", [128, 8 * 64], BF16)
        wv_sb = sb("wv_sb", [128, 8 * 65], BF16)
        qmx = sb("qmx", [128, 8], F32)
        sqmx = sb("sqmx", [128, 8], F32)
        ltmp = sb("ltmp", [32, 512], F32)
        linv = sb("linv", [1, 512], F32)
        linv2 = sb("linv2", [1, 512], F32)
        lb = sb("lb", [64, 512], F32)
        ones_f = sb("ones_f", [128, 1], F32)
        ones_r = sb("ones_r", [1, 128], F32)
        stat = sb("stat", [1, 16], F32)
        stat2 = sb("stat2", [1, 16], F32)
        bias_c = sb("bias_c", [128, 4], F32)
        if causal:
            cm_sb = sb("cm_sb", [128, 4 * 512], BF16)
        else:
            q32 = sb("q32", [128, 512], F32)
            qsw = sb("qsw", [128, 512], F32)
            qt1 = sb("qt1", [128, 512], F32)
            qt2 = sb("qt2", [128, 512], F32)
            rtc = sb("rtc", [128, 2 * 512], F32)
            rts = sb("rts", [128, 2 * 512], F32)
            ktc = sb("ktc", [64, 512], F32)
            kts = sb("kts", [64, 512], F32)

        ring = psa("ring", [128, 3072])
        av = [psa("av0", [128, 512]), psa("av1", [128, 512])]
        psb_ = ring[:, 2048:2560]
        pst = ring[:, 2560:3072]

        s_wld = sem("s_wld")
        s_xin = sem("s_xin")
        s_xfree = sem("s_xfree")
        s_pmm = sem("s_pmm")
        s_pev = sem("s_pev")
        s_pevv = sem("s_pevv")
        s_st = sem("s_st")
        s_exp = sem("s_exp")
        s_msk = sem("s_msk")
        s_av = sem("s_av")
        s_ep = sem("s_ep")
        s_on = sem("s_on")
        s_bias = sem("s_bias")
        s_msc = sem("s_msc")
        s_out = sem("s_out")
        s_cml = sem("s_cml")

        with nc.Block() as block:
            def ph0_g(g):
                g.dma_start(out=wq_sb[:].rearrange("p (c n) -> p c n", c=8),
                            in_=wq.rearrange("(c p) n -> p c n", p=128)).then_inc(s_wld.h, 16)
                g.dma_start(out=wk_sb[:].rearrange("p (c n) -> p c n", c=8),
                            in_=wk.rearrange("(c p) n -> p c n", p=128)).then_inc(s_wld.h, 16)
                g.dma_start(out=wv_sb[:].rearrange("p (c n) -> p c n", c=8),
                            in_=wvz.rearrange("(c p) n -> p c n", p=128)).then_inc(s_wld.h, 16)
            block.gpsimd(ph0_g)
            s_wld.c += 48

            if causal:
                def ph0_s(sy):
                    sy.dma_start(out=cm_sb[:].rearrange("p (d n) -> p d n", d=4),
                                 in_=cmask_d.rearrange("d p n -> p d n")).then_inc(s_cml.h, 16)
                block.sync(ph0_s)
                s_cml.c += 16

            def ph0_v(v):
                v.memset(q_pk[64:128, :], 0.0)
                v.memset(q_od[0:64, :], 0.0)
                v.memset(o_fm[:], 0.0)
                v.memset(ones_f[:], 1.0)
                v.memset(ones_r[:], 1.0)
                v.memset(stat[:], 0.0)
                v.memset(qmx[:], 0.0)
                v.memset(stat2[:], 0.0)
                v.memset(av[0][64:96, :], 0.0)
                v.memset(av[1][64:96, :], 0.0)
                v.memset(vsb[:, 64::65], 1.0).then_inc(s_msc.h, 1)
            block.vector(ph0_v)
            s_msc.c += 1

            s_rt = sem("s_rt")
            s_rp = sem("s_rp")
            rope_guard = [0]
            rope_end = {}
            rope_sub = {}
            for kc in range(0 if skip_b else NCH):
                xo = (kc % 2) * 4096

                def fB_g(g, kc=kc, xo=xo):
                    if kc >= 1:
                        g.wait_ge(s_xin.h, 16 * kc)
                    if kc >= 2:
                        if causal:
                            g.wait_ge(s_pev.h, 2 * (kc - 1))
                            g.wait_ge(s_pevv.h, kc - 1)
                        else:
                            g.wait_ge(s_rp.h, rope_end[kc - 2])
                    g.dma_start(
                        out=xin[:, xo:xo + 4096].rearrange("p (c t) -> p c t", c=8),
                        in_=xfm[:, kc * 512:(kc + 1) * 512].rearrange("(c p) t -> p c t", p=128),
                    ).then_inc(s_xin.h, 16)
                block.gpsimd(fB_g)
                s_xin.c += 16

                def fB_t(t, kc=kc, xo=xo):
                    t.wait_ge(s_xin.h, 16 * (kc + 1))
                    if kc == 0:
                        t.wait_ge(s_wld.h, 48)
                    else:
                        if causal:
                            t.wait_ge(s_pev.h, 2 * kc)
                            t.wait_ge(s_pevv.h, kc)
                        else:
                            t.wait_ge(s_rp.h, rope_sub[kc - 1][0])
                    for c in range(2):
                        if (not causal) and kc > 0 and c == 1:
                            t.wait_ge(s_rp.h, rope_sub[kc - 1][1])
                        for k in range(8):
                            inst = t.matmul(out=ring[:, c * 512:(c + 1) * 512],
                                     lhsT=wq_sb[:, k * 256 + c * 128: k * 256 + (c + 1) * 128],
                                     rhs=xin[:, xo + k * 512: xo + (k + 1) * 512],
                                     start=(k == 0), stop=(k == 7))
                            if c == 1 and k == 7:
                                inst.then_inc(s_pmm.h, 1)
                    if (not causal) and kc > 0:
                        t.wait_ge(s_rp.h, rope_sub[kc - 1][2])
                    for k in range(8):
                        inst = t.matmul(out=ring[0:64, 1024:1536],
                                 lhsT=wk_sb[:, k * 64:(k + 1) * 64],
                                 rhs=xin[:, xo + k * 512: xo + (k + 1) * 512],
                                 start=(k == 0), stop=(k == 7))
                        if k == 7:
                            inst.then_inc(s_pmm.h, 1)
                    if (not causal) and kc > 0:
                        t.wait_ge(s_rp.h, rope_sub[kc - 1][3])
                    for kt in range(4):
                        for k in range(8):
                            inst = t.matmul(out=ring[:, 1536 + kt * 65: 1536 + kt * 65 + 65],
                                     lhsT=xin[:, xo + k * 512 + kt * 128: xo + k * 512 + (kt + 1) * 128],
                                     rhs=wv_sb[:, k * 65:(k + 1) * 65],
                                     start=(k == 0), stop=(k == 7))
                            if kt == 3 and k == 7:
                                inst.then_inc(s_pmm.h, 1)
                block.tensor(fB_t)
                s_pmm.c += 3

                if causal:
                    def fB_s(s, kc=kc):
                        s.wait_ge(s_pmm.h, 3 * kc + 1)
                        if kc == 0:
                            s.wait_ge(s_msc.h, 1)
                        for c in range(2):
                            s.activation(out=q_pk[0:64, c * S + kc * 512: c * S + (kc + 1) * 512],
                                         in_=ring[0:64, c * 512:(c + 1) * 512],
                                         func=AF.Copy, scale=float(qscale))
                            inst = s.activation(out=q_od[64:128, c * S + kc * 512: c * S + (kc + 1) * 512],
                                         in_=ring[64:128, c * 512:(c + 1) * 512],
                                         func=AF.Copy, scale=float(qscale))
                            if c == 1:
                                inst.then_inc(s_pev.h, 1)
                        s.wait_ge(s_pmm.h, 3 * kc + 3)
                        for kt in range(4):
                            inst = s.activation(out=vsb[:, (kc * 4 + kt) * 65: (kc * 4 + kt) * 65 + 64],
                                         in_=ring[:, 1536 + kt * 65: 1536 + kt * 65 + 64],
                                         func=AF.Copy)
                            if kt == 3:
                                inst.then_inc(s_pev.h, 1)
                    block.scalar(fB_s)
                    s_pev.c += 2

                    def fB_v(v, kc=kc):
                        v.wait_ge(s_pmm.h, 3 * kc + 2)
                        v.tensor_copy(out=ksb[0:64, kc * 512:(kc + 1) * 512], in_=ring[0:64, 1024:1536])
                        v.tensor_copy(out=ksb[64:128, kc * 512:(kc + 1) * 512],
                                      in_=ring[0:64, 1024:1536]).then_inc(s_pevv.h, 1)
                    block.vector(fB_v)
                    s_pevv.c += 1
                else:
                    def rt_s(sy, kc=kc, prev=s_rt.c, guard=rope_guard[0]):
                        sy.wait_ge(s_rt.h, prev)
                        if guard > 0:
                            sy.wait_ge(s_rp.h, guard)
                        for c in range(2):
                            sy.dma_start(out=rtc[:, c * 512:(c + 1) * 512],
                                         in_=rqc_d[:, c * S + kc * 512: c * S + (kc + 1) * 512]).then_inc(s_rt.h, 16)
                            sy.dma_start(out=rts[:, c * 512:(c + 1) * 512],
                                         in_=rqs_d[:, c * S + kc * 512: c * S + (kc + 1) * 512]).then_inc(s_rt.h, 16)
                        sy.dma_start(out=ktc[:], in_=rkc_d[:, kc * 512:(kc + 1) * 512]).then_inc(s_rt.h, 16)
                        sy.dma_start(out=kts[:], in_=rks_d[:, kc * 512:(kc + 1) * 512]).then_inc(s_rt.h, 16)
                    block.sync(rt_s)
                    s_rt.c += 96
                    rt_need = s_rt.c

                    def rchain_s(emitfn):
                        need = s_rp.c

                        def f(s, need=need):
                            s.wait_ge(s_rp.h, need)
                            emitfn(s).then_inc(s_rp.h, 1)
                        block.scalar(f)
                        s_rp.c += 1

                    def rchain_v(emitfn):
                        need = s_rp.c

                        def f(v, need=need):
                            v.wait_ge(s_rp.h, need)
                            emitfn(v).then_inc(s_rp.h, 1)
                        block.vector(f)
                        s_rp.c += 1

                    _sub = []
                    for c in range(2):
                        def ev_q(s, kc=kc, c=c, rt_need=rt_need):
                            s.wait_ge(s_pmm.h, 3 * kc + 1)
                            s.wait_ge(s_rt.h, rt_need)
                            return s.activation(out=q32[:], in_=ring[:, c * 512:(c + 1) * 512],
                                                func=AF.Copy)
                        rchain_s(ev_q)
                        _sub.append(s_rp.c)

                        def swp(v, c=c):
                            v.tensor_copy(out=qsw[0:32, :], in_=q32[32:64, :])
                            v.tensor_copy(out=qsw[32:64, :], in_=q32[0:32, :])
                            v.tensor_copy(out=qsw[64:96, :], in_=q32[96:128, :])
                            v.tensor_copy(out=qsw[96:128, :], in_=q32[64:96, :])
                            return v.tensor_tensor(out=qt1[:], in0=q32[:],
                                                   in1=rtc[:, c * 512:(c + 1) * 512],
                                                   op=ALU.mult)
                        rchain_v(swp)
                        rchain_v(lambda v, c=c: v.tensor_tensor(
                            out=qt2[:], in0=qsw[:], in1=rts[:, c * 512:(c + 1) * 512],
                            op=ALU.mult))
                        rchain_v(lambda v: v.tensor_tensor(
                            out=qt1[:], in0=qt1[:], in1=qt2[:], op=ALU.add))
                        def ev_qout(s, kc=kc, c=c):
                            s.activation(
                                out=q_pk[0:64, c * S + kc * 512: c * S + (kc + 1) * 512],
                                in_=qt1[0:64, :], func=AF.Copy, scale=float(qscale))
                            return s.activation(
                                out=q_od[64:128, c * S + kc * 512: c * S + (kc + 1) * 512],
                                in_=qt1[64:128, :], func=AF.Copy, scale=float(qscale))
                        rchain_s(ev_qout)
                    def ev_k(s, kc=kc):
                        s.wait_ge(s_pmm.h, 3 * kc + 2)
                        return s.activation(out=q32[0:64, :], in_=ring[0:64, 1024:1536],
                                            func=AF.Copy)
                    rchain_s(ev_k)
                    _sub.append(s_rp.c)

                    def swpk(v):
                        v.tensor_copy(out=qsw[0:32, :], in_=q32[32:64, :])
                        v.tensor_copy(out=qsw[32:64, :], in_=q32[0:32, :])
                        return v.tensor_tensor(out=qt1[0:64, :], in0=q32[0:64, :],
                                               in1=ktc[:], op=ALU.mult)
                    rchain_v(swpk)
                    rchain_v(lambda v: v.tensor_tensor(
                        out=qt2[0:64, :], in0=qsw[0:64, :], in1=kts[:], op=ALU.mult))
                    rchain_v(lambda v: v.tensor_tensor(
                        out=qt1[0:64, :], in0=qt1[0:64, :], in1=qt2[0:64, :], op=ALU.add))
                    rchain_s(lambda s, kc=kc: s.activation(
                        out=ksb[0:64, kc * 512:(kc + 1) * 512], in_=qt1[0:64, :],
                        func=AF.Copy))
                    rchain_v(lambda v, kc=kc: v.tensor_copy(
                        out=ksb[64:128, kc * 512:(kc + 1) * 512],
                        in_=ksb[0:64, kc * 512:(kc + 1) * 512]))
                    def ev_v(s, kc=kc):
                        s.wait_ge(s_pmm.h, 3 * kc + 3)
                        for kt in range(4):
                            inst = s.activation(out=vsb[:, (kc * 4 + kt) * 65: (kc * 4 + kt) * 65 + 64],
                                         in_=ring[:, 1536 + kt * 65: 1536 + kt * 65 + 64],
                                         func=AF.Copy)
                        return inst
                    rchain_s(ev_v)
                    _sub.append(s_rp.c)
                    rope_guard[0] = s_rp.c
                    rope_end[kc] = s_rp.c
                    rope_sub[kc] = tuple(_sub)
            if skip_stats:
                NCH_C = 0
            else:
                NCH_C = NCH
            s_sq = sem("s_sq")
            s_qn = sem("s_qn")
            s_qnv = sem("s_qnv")
            sqq = sb("sqq", [128, 2 * 512], F32)
            sqk = sb("sqk", [64, 2 * 512], F32)
            qmaxrow = sb("qmaxrow", [1, 80], F32)
            statbank = [pst, psb_]

            sqq2 = sb("sqq2", [128, 2 * 512], F32)

            def phC_act(s):
                for kc in range(NCH_C):
                    if kc >= 2:
                        s.wait_ge(s_qn.h, 5 * (kc - 1))
                    if kc == 0:
                        s.wait_ge(s_pev.h, s_pev.c)
                        s.wait_ge(s_pevv.h, s_pevv.c)
                        if not causal:
                            s.wait_ge(s_rp.h, s_rp.c)
                    bo = (kc % 2) * 512
                    s.activation(out=sqq[0:64, bo:bo + 512],
                                 in_=q_pk[0:64, kc * 512:kc * 512 + 512],
                                 func=AF.Square).then_inc(s_sq.h, 1)
                    s.activation(out=sqq[64:128, bo:bo + 512],
                                 in_=q_od[64:128, kc * 512:kc * 512 + 512],
                                 func=AF.Square).then_inc(s_sq.h, 1)
                    s.activation(out=sqq2[0:64, bo:bo + 512],
                                 in_=q_pk[0:64, S + kc * 512: S + kc * 512 + 512],
                                 func=AF.Square).then_inc(s_sq.h, 1)
                    s.activation(out=sqq2[64:128, bo:bo + 512],
                                 in_=q_od[64:128, S + kc * 512: S + kc * 512 + 512],
                                 func=AF.Square).then_inc(s_sq.h, 1)
                    s.activation(out=sqk[:, bo:bo + 512],
                                 in_=ksb[0:64, kc * 512:kc * 512 + 512],
                                 func=AF.Square).then_inc(s_sq.h, 1)
            block.scalar(phC_act)
            s_sq.c += 5 * NCH_C

            def phC_pe(t):
                for kc in range(NCH_C):
                    bo = (kc % 2) * 512
                    for j in range(5):
                        mi = kc * 5 + j
                        if mi >= 2:
                            t.wait_ge(s_qnv.h, mi - 1)
                        if j < 4:
                            c, half = j // 2, j % 2
                            t.wait_ge(s_sq.h, 5 * kc + 2 * c + half + 1)
                            src_ = (sqq if c == 0 else sqq2)
                            base = half * 64
                            t.matmul(out=statbank[mi % 2][0:1, 0:512],
                                     lhsT=ones_f[base:base + 64, :],
                                     rhs=src_[base:base + 64, bo:bo + 512],
                                     start=True, stop=True).then_inc(s_qn.h, 1)
                        else:
                            t.wait_ge(s_sq.h, 5 * kc + 5)
                            t.matmul(out=statbank[mi % 2][0:1, 0:512],
                                     lhsT=ones_f[0:64, :],
                                     rhs=sqk[:, bo:bo + 512],
                                     start=True, stop=True).then_inc(s_qn.h, 1)
            block.tensor(phC_pe)
            s_qn.c += 5 * NCH_C

            def phC_dve(v):
                for kc in range(NCH_C):
                    for j in range(5):
                        mi = kc * 5 + j
                        v.wait_ge(s_qn.h, mi + 1)
                        if j < 4:
                            h = 2 * (j // 2) + (j % 2)
                            col = h * 16 + kc
                        else:
                            col = 64 + kc
                        v.reduce_max(out=qmaxrow[0:1, col:col + 1],
                                     in_=statbank[mi % 2][0:1, 0:512],
                                     axis=AX.X).then_inc(s_qnv.h, 1)
                v.nop().then_inc(s_qnv.h, 1)
            block.vector(phC_dve)
            s_qnv.c += 5 * NCH_C + 1

            def phC_fin(v):
                v.wait_ge(s_qnv.h, s_qnv.c)
                if skip_stats:
                    v.memset(stat[0:1, 0:5], 1.0).then_inc(s_ep.h, 1)
                    return
                for h in range(4):
                    v.reduce_max(out=stat[0:1, h:h + 1],
                                 in_=qmaxrow[0:1, h * 16:(h + 1) * 16], axis=AX.X)
                v.reduce_max(out=stat[0:1, 4:5],
                             in_=qmaxrow[0:1, 64:80], axis=AX.X).then_inc(s_ep.h, 1)
            block.vector(phC_fin)
            s_ep.c += 1

            def phC_s2(s):
                s.wait_ge(s_ep.h, s_ep.c)
                s.activation(out=stat2[0:1, 0:4], in_=stat[0:1, 0:4], func=AF.Ln)
                s.activation(out=stat2[0:1, 4:5], in_=stat[0:1, 4:5], func=AF.Ln).then_inc(s_ep.h, 1)
            block.scalar(phC_s2)
            s_ep.c += 1

            def phC_v3(v):
                v.wait_ge(s_ep.h, s_ep.c)
                v.tensor_tensor(out=stat2[0:1, 8:12], in0=stat2[0:1, 0:4],
                                in1=stat2[0:1, 4:5].broadcast_to([1, 4]),
                                op=ALU.add).then_inc(s_ep.h, 1)
            block.vector(phC_v3)
            s_ep.c += 1

            def phC_s3(s):
                s.wait_ge(s_ep.h, s_ep.c)
                s.activation(out=stat2[0:1, 12:16], in_=stat2[0:1, 8:12],
                             func=AF.Exp, scale=0.5).then_inc(s_ep.h, 1)
            block.scalar(phC_s3)
            s_ep.c += 1

            def phC_v4(v):
                v.wait_ge(s_ep.h, s_ep.c)
                v.tensor_scalar(out=stat[0:1, 8:12], in0=stat2[0:1, 12:16],
                                scalar1=-1.0, scalar2=SHIFT,
                                op0=ALU.mult, op1=ALU.add).then_inc(s_ep.h, 1)
            block.vector(phC_v4)
            s_ep.c += 1

            def phC_t2(t):
                t.wait_ge(s_ep.h, s_ep.c)
                for h in range(4):
                    inst = t.matmul(out=psb_[:, h:h + 1], lhsT=ones_r[:],
                             rhs=stat[0:1, 8 + h:9 + h], start=True, stop=True)
                inst.then_inc(s_ep.h, 1)
            block.tensor(phC_t2)
            s_ep.c += 1

            def phC_v5(v):
                v.wait_ge(s_ep.h, s_ep.c)
                v.tensor_copy(out=bias_c[:], in_=psb_[:, 0:4]).then_inc(s_bias.h, 1)
            block.vector(phC_v5)
            s_bias.c += 1

            class WC:
                def __init__(self):
                    self.seen = {}

                def w(self, eng, key, semx, val):
                    if val <= 0:
                        return
                    if self.seen.get((key, id(semx)), -1) >= val:
                        return
                    eng.wait_ge(semx.h, val)
                    self.seen[(key, id(semx))] = val

            wc = WC()
            bank_free_at = [0, 0, 0, 0, 0, 0]
            av_free_at = [0, 0]
            first_mask = [True]
            pending_e4 = []
            avi = 0
            first_attn = True

            for qg in range(nqg):
                bound = 4 * qg + 4 if causal else NKT
                npairs = bound // 2
                for h in range(4):
                    base = 64 * (h % 2)
                    cb = (h // 2) * S
                    avb = av[avi]
                    av_S0 = s_av.c
                    st_S0 = s_st.c
                    exp_S0 = s_exp.c
                    av_guard = av_free_at[avi]
                    exp_at = {}
                    msk_at = {}
                    for pi in range(npairs):
                        s_exp.c += 1
                        exp_at[2 * pi] = s_exp.c
                        exp_at[2 * pi + 1] = s_exp.c
                    if causal:
                        for kt in range(bound - 4, bound):
                            s_msk.c += 1
                            msk_at[kt] = s_msk.c
                    ring_entry = list(bank_free_at)

                    def fD_t(t, qg=qg, h=h, base=base, cb=cb, avb=avb,
                             bound=bound, npairs=npairs, exp_at=exp_at,
                             msk_at=msk_at, av_S0=av_S0, av_guard=av_guard,
                             first=first_attn, ring_entry=ring_entry,
                             pend=list(pending_e4)):
                        if first:
                            t.wait_ge(s_pev.h, s_pev.c)
                            t.wait_ge(s_pevv.h, s_pevv.c)
                            t.wait_ge(s_rp.h, s_rp.c)
                            t.wait_ge(s_msc.h, 1)
                            t.wait_ge(s_bias.h, 1)
                        n_av = [0]

                        def emit_av(kt):
                            if kt == 0:
                                wc.w(t, "pe", s_on, av_guard)
                            wc.w(t, "pe", s_exp, exp_at[kt])
                            if kt in msk_at:
                                wc.w(t, "pe", s_msk, msk_at[kt])
                            t.matmul(out=avb[0:65, :],
                                     lhsT=vsb[:, kt * 65:(kt + 1) * 65],
                                     rhs=ptb[:, (kt % 8) * 512:(kt % 8 + 1) * 512],
                                     start=(kt == 0), stop=(kt == bound - 1)
                                     ).then_inc(s_av.h, 1)
                            n_av[0] += 1

                        for pi in range(npairs):
                            for kt in (2 * pi, 2 * pi + 1):
                                bkx = ((kt // 2) % 3) * 2 + (kt % 2)
                                if kt >= 6:
                                    wc.w(t, "pe", s_exp, exp_at[kt - 6])
                                else:
                                    wc.w(t, "pe", s_exp, ring_entry[bkx])
                                qzx = q_pk if h % 2 == 0 else q_od
                                inst = t.matmul(
                                    out=ring[:, bkx * 512:(bkx + 1) * 512],
                                    lhsT=ksb[:, kt * 128:(kt + 1) * 128],
                                    rhs=qzx[:, cb + qg * 512: cb + (qg + 1) * 512],
                                    start=True, stop=True)
                                if kt % 2 == 1:
                                    inst.then_inc(s_st.h, 1)
                            if pi == 1 and pend:
                                for e4 in pend:
                                    e4(t)
                                pend.clear()
                            if pi >= 2:
                                emit_av(2 * pi - 4)
                                emit_av(2 * pi - 3)
                        for e4 in pend:
                            e4(t)
                        pend.clear()
                        for kt in range(n_av[0], bound):
                            emit_av(kt)
                    block.tensor(fD_t)
                    pending_e4 = []
                    first_attn = False
                    s_st.c += npairs
                    s_av.c += bound
                    for kt in range(max(0, bound - 6), bound):
                        bank_free_at[((kt // 2) % 3) * 2 + (kt % 2)] = exp_at[kt]

                    def fD_s(s, h=h, npairs=npairs, st_S0=st_S0, exp_S0=exp_S0,
                             av_S0=av_S0):
                        for pi in range(npairs):
                            kt0 = 2 * pi
                            wc.w(s, "act", s_st, st_S0 + pi + 1)
                            if exp_S0 + pi == 0:
                                s.wait_ge(s_bias.h, 1)
                            if kt0 >= 8:
                                wc.w(s, "act", s_av, av_S0 + kt0 - 6)
                            pb = (kt0 // 2) % 3
                            slot = kt0 % 8
                            s.activation(out=ptb[:, slot * 512:(slot + 2) * 512],
                                         in_=ring[:, pb * 1024:(pb + 1) * 1024],
                                         func=AF.Exp, bias=bias_c[:, h:h + 1]
                                         ).then_inc(s_exp.h, 1)
                    block.scalar(fD_s)

                    if causal:
                        def fD_vm(v, qg=qg, bound=bound, exp_at=dict(exp_at)):
                            if first_mask[0]:
                                v.wait_ge(s_cml.h, 16)
                                first_mask[0] = False
                            for kt in range(bound - 4, bound):
                                d = kt - 4 * qg
                                sl = (kt % 8) * 512
                                wc.w(v, "dve", s_exp, exp_at[kt])
                                v.tensor_tensor(out=ptb[:, sl:sl + 512],
                                                in0=ptb[:, sl:sl + 512],
                                                in1=cm_sb[:, d * 512:(d + 1) * 512],
                                                op=ALU.mult).then_inc(s_msk.h, 1)
                        block.vector(fD_vm)

                    e1_val = s_ep.c + 1
                    e2_val = s_ep.c + 2
                    e3_val = s_ep.c + 3
                    e4_val = s_ep.c + 4
                    e5_val = s_ep.c + 5

                    def fD_v(v, qg=qg, h=h, base=base, cb=cb, avb=avb,
                             av_tot=s_av.c, e1_val=e1_val, e2_val=e2_val,
                             e4_val=e4_val, e5_val=e5_val):
                        wc.w(v, "dve", s_av, av_tot)
                        v.tensor_copy(out=ltmp[:], in_=avb[64:96, :]).then_inc(s_ep.h, 1)
                        v.wait_ge(s_ep.h, e1_val)
                        v.tensor_scalar(out=linv[:], in0=ltmp[0:1, :], scalar1=LEPS,
                                        scalar2=None, op0=ALU.add).then_inc(s_ep.h, 1)
                        v.wait_ge(s_ep.h, e2_val)
                        v.reciprocal(out=linv2[:], in_=linv[:]).then_inc(s_ep.h, 1)
                        v.wait_ge(s_ep.h, e4_val)
                        v.tensor_copy(out=lb[:], in_=avb[64:128, :]).then_inc(s_ep.h, 1)
                        v.wait_ge(s_ep.h, e5_val)
                        v.tensor_tensor(out=o_fm[base:base + 64, cb + qg * 512: cb + (qg + 1) * 512],
                                        in0=avb[0:64, :], in1=lb[:],
                                        op=ALU.mult).then_inc(s_on.h, 1)
                    block.vector(fD_v)
                    s_ep.c += 5
                    s_on.c += 1
                    av_free_at[avi] = s_on.c
                    avi ^= 1

                    def mk_e4(e3_val=e3_val, avb=avb):
                        def e4(t):
                            t.wait_ge(s_ep.h, e3_val)
                            t.matmul(out=avb[64:128, :], lhsT=ones_r[0:1, 0:64],
                                     rhs=linv2[:], start=True, stop=True,
                                     tile_position=(0, 64)).then_inc(s_ep.h, 1)
                        return e4
                    pending_e4.append(mk_e4())

            def fD_flush(t):
                for e4 in pending_e4:
                    e4(t)
                pending_e4.clear()
            block.tensor(fD_flush)

            ndma = [2]

            def phE(sy):
                sy.wait_ge(s_msc.h, 1)
                sy.wait_ge(s_cml.h, s_cml.c)
                sy.wait_ge(s_xin.h, s_xin.c)
                sy.wait_ge(s_wld.h, s_wld.c)
                sy.wait_ge(s_rt.h, s_rt.c)
                sy.wait_ge(s_on.h, s_on.c)
                sy.dma_start(out=o_out0, in_=o_fm[:, 0:S]).then_inc(s_out.h, 16)
                sy.dma_start(out=o_out1, in_=o_fm[:, S:2 * S]).then_inc(s_out.h, 16)
                if dump:
                    sy.wait_ge(s_bias.h, 1)
                    sy.dma_start(out=qdmp[0:64], in_=q_pk[0:64, :]).then_inc(s_out.h, 16)
                    sy.dma_start(out=qdmp[64:128], in_=q_od[64:128, :]).then_inc(s_out.h, 16)
                    sy.dma_start(out=kdmp, in_=ksb[:]).then_inc(s_out.h, 16)
                    sy.dma_start(out=vdmp, in_=vsb[:]).then_inc(s_out.h, 16)
                    sy.dma_start(out=bdmp, in_=bias_c[:]).then_inc(s_out.h, 16)
                    ndma[0] += 5
                sy.wait_ge(s_out.h, 16 * ndma[0])
            block.sync(phE)
            s_out.c += 16 * ndma[0]

    return nc



def build_block_tail(layer, ntb=4, dump=False, do_ffn=True):
    nc = bass.Bass("TRN2", target_bir_lowering=False, debug=False)
    o_in = nc.dram_tensor("o_in", [D, TPC], BF16, kind="ExternalInput").ap()
    resid_in = nc.dram_tensor("resid_in", [D, TPC], F32, kind="ExternalInput").ap()
    wo_d = nc.dram_tensor("wo", [D, D], F32, kind="ExternalInput").ap()
    s2_out = nc.dram_tensor("s2_out", [D, TPC], F32, kind="ExternalOutput").ap()
    if layer == 0:
        k0t_d = nc.dram_tensor("k0t", [D, 4096], U8, kind="ExternalInput").ap()
        k0b_d = nc.dram_tensor("k0b", [D, 4096], U8, kind="ExternalInput").ap()
        k1_d = nc.dram_tensor("k1", [4096, 4096], U8, kind="ExternalInput").ap()
        k2_d = nc.dram_tensor("k2", [4096, D], U8, kind="ExternalInput").ap()
        scl_d = nc.dram_tensor("scl", [1, 4], F32, kind="ExternalInput").ap()
    else:
        w1_d = nc.dram_tensor("w1t", [D, 4096], F32, kind="ExternalInput").ap()
        w2_d = nc.dram_tensor("w2t", [D, 4096], F32, kind="ExternalInput").ap()
        h1_d = nc.dram_tensor("h1t", [4096, 4096], F32, kind="ExternalInput").ap()
        h2_d = nc.dram_tensor("h2t", [4096, 4096], F32, kind="ExternalInput").ap()
        w3_d = nc.dram_tensor("w3t", [4096, D], F32, kind="ExternalInput").ap()
        sv1_d = nc.dram_tensor("sv1", [128, 32], F32, kind="ExternalInput").ap()
        sv2_d = nc.dram_tensor("sv2", [128, 32], F32, kind="ExternalInput").ap()
        sh1_d = nc.dram_tensor("sh1", [128, 32], F32, kind="ExternalInput").ap()
        sh2_d = nc.dram_tensor("sh2", [128, 32], F32, kind="ExternalInput").ap()
        sv3_d = nc.dram_tensor("sv3", [128, 8], F32, kind="ExternalInput").ap()
    if dump:
        s1_out = nc.dram_tensor("s1_out", [D, TPC], F32, kind="ExternalOutput").ap()
    s1_dram = nc.dram_tensor("s1_dram", [D, TPC], F32, kind="Internal").ap()

    ctx = ExitStack()
    with ctx:
        sb = lambda n, sh, dt: ctx.enter_context(nc.sbuf_tensor(n, sh, dt))
        psa = lambda n, sh: ctx.enter_context(nc.psum_tensor(n, sh, F32))
        sem = lambda n: Sem(ctx.enter_context(nc.semaphore(n)))

        bufA = sb("bufA", [128, 32 * 1024], BF16)
        bufB = sb("bufB", [128, 32 * 1024], BF16)
        rsm = sb("rsm", [128, 8 * 512], F32)
        xh = sb("xh", [128, 8 * 1024], BF16)
        wbig = sb("wbig", [128, 32 * 128], BF16)
        weff = sb("weff", [128, 8 * 512], BF16)
        wstg = sb("wstg", [128, 8 * 512], BF16)
        sb_corr = sb("sb_corr", [128, 1024], F32)
        sqv = sb("sqv", [128, 512], F32)
        if layer == 0:
            tmp2 = sb("tmp2_sb", [128, 1024], F32)
        rowm = sb("rowm", [1, 1536], F32)
        rb = sb("rb", [128, 512], F32)
        mb = sb("mb", [128, 512], F32)
        tmp1 = sb("tmp1", [128, 512], F32)
        ones_f = sb("ones_f", [128, 1], F32)
        ones_b = sb("ones_b", [128, 1], BF16)
        ones_r = sb("ones_r", [1, 128], F32)
        scl = sb("scl_sb", [128, 4], F32)
        scl4 = sb("scl4_sb", [128, 4], F32)
        if layer == 1:
            sv1 = sb("sv1_sb", [128, 32], F32)
            sv2 = sb("sv2_sb", [128, 32], F32)
            sh1 = sb("sh1_sb", [128, 32], F32)
            sh2 = sb("sh2_sb", [128, 32], F32)
            sv3 = sb("sv3_sb", [128, 8], F32)
            gtmp = sb("gtmp", [128, 1024], BF16)

        ring = psa("ring", [128, 2048])
        pb4 = psa("pb4", [128, 512])
        pb5 = psa("pb5", [128, 512])
        pb6 = psa("pb6", [128, 512])
        pb7 = psa("pb7", [128, 512])

        s_ldg = sem("s_ldg")
        s_lds = sem("s_lds")
        s_wo = sem("s_wo")
        s_ev = sem("s_ev")
        s_ch = sem("s_ch")
        s_msc = sem("s_msc")
        s_st1 = sem("s_st1")
        s_out = sem("s_out")

        with nc.Block() as block:
            def t0_g(g):
                g.dma_start(out=bufA[:, 0:16384].rearrange("p (c n) -> p c n", c=8),
                            in_=o_in.rearrange("(c p) n -> p c n", p=128)).then_inc(s_ldg.h, 16)
                g.dma_start(out=bufB[:, 0:8192].rearrange("p (c n) -> p c n", c=8),
                            in_=wo_d.rearrange("(c p) n -> p c n", p=128)).then_inc(s_ldg.h, 16)
            block.gpsimd(t0_g)
            s_ldg.c += 32

            def t0_s(sy):
                if layer == 0:
                    sy.dma_start(out=scl[0:1, :], in_=scl_d).then_inc(s_lds.h, 16)
                else:
                    sy.dma_start(out=sv1[:], in_=sv1_d).then_inc(s_lds.h, 16)
                    sy.dma_start(out=sv2[:], in_=sv2_d).then_inc(s_lds.h, 16)
                    sy.dma_start(out=sh1[:], in_=sh1_d).then_inc(s_lds.h, 16)
                    sy.dma_start(out=sh2[:], in_=sh2_d).then_inc(s_lds.h, 16)
                    sy.dma_start(out=sv3[:], in_=sv3_d).then_inc(s_lds.h, 16)
            block.sync(t0_s)
            s_lds.c += 16 if layer == 0 else 80

            def t0_v(v):
                v.memset(ones_f[:], 1.0)
                v.memset(ones_b[:], 1.0)
                v.memset(ones_r[:], 1.0).then_inc(s_msc.h, 1)
            block.vector(t0_v)
            s_msc.c += 1

            if layer == 0:
                def t0_t(t):
                    t.wait_ge(s_lds.h, s_lds.c)
                    t.wait_ge(s_msc.h, 1)
                    t.matmul(out=pb6[:, 0:4], lhsT=ones_r[:], rhs=scl[0:1, 0:4],
                             start=True, stop=True).then_inc(s_ch.h, 1)
                block.tensor(t0_t)
                s_ch.c += 1

                def t0_v2(v):
                    v.wait_ge(s_ch.h, s_ch.c)
                    v.tensor_scalar(out=scl[:], in0=pb6[:, 0:4], scalar1=2.0,
                                    scalar2=None, op0=ALU.mult).then_inc(s_ch.h, 1)
                block.vector(t0_v2)
                s_ch.c += 1

                def t0_v3(v):
                    v.wait_ge(s_ch.h, s_ch.c)
                    v.tensor_scalar(out=scl4[:], in0=scl[:], scalar1=2.0,
                                    scalar2=None, op0=ALU.mult).then_inc(s_ch.h, 1)
                block.vector(t0_v3)
                s_ch.c += 1

            def chain_t(emitfn):
                need = s_ch.c

                def f(t, need=need):
                    t.wait_ge(s_ch.h, need)
                    emitfn(t).then_inc(s_ch.h, 1)
                block.tensor(f)
                s_ch.c += 1

            def chain_v(emitfn):
                need = s_ch.c

                def f(v, need=need):
                    v.wait_ge(s_ch.h, need)
                    emitfn(v).then_inc(s_ch.h, 1)
                block.vector(f)
                s_ch.c += 1

            def chain_s(emitfn):
                need = s_ch.c

                def f(s, need=need):
                    s.wait_ge(s_ch.h, need)
                    emitfn(s).then_inc(s_ch.h, 1)
                block.scalar(f)
                s_ch.c += 1

            def emit_ln(src_resid, dst_xh, extra_pe_wait=None):
                def sum_mm(t):
                    if extra_pe_wait is not None:
                        extra_pe_wait(t)
                    for c in range(8):
                        inst = t.matmul(out=pb4[0:1, :], lhsT=ones_f[:],
                                        rhs=src_resid(c), start=(c == 0),
                                        stop=(c == 7))
                    return inst
                chain_t(sum_mm)
                for c in range(8):
                    chain_s(lambda s, c=c: s.activation(
                        out=sqv[:], in_=src_resid(c), func=AF.Square))
                    chain_t(lambda t, c=c: t.matmul(
                        out=pb5[0:1, :], lhsT=ones_f[:], rhs=sqv[:],
                        start=(c == 0), stop=(c == 7)))

                def v1(v):
                    v.tensor_scalar(out=rowm[0:1, 0:512], in0=pb4[0:1, :],
                                    scalar1=1.0 / D, scalar2=None, op0=ALU.mult)
                    return v.tensor_scalar(out=rowm[0:1, 512:1024], in0=pb5[0:1, :],
                                           scalar1=1.0 / D, scalar2=None,
                                           op0=ALU.mult)
                chain_v(v1)
                chain_s(lambda s: s.activation(out=rowm[0:1, 1024:1536],
                                               in_=rowm[0:1, 0:512], func=AF.Square))
                chain_v(lambda v: v.scalar_tensor_tensor(
                    out=rowm[0:1, 512:1024], in0=rowm[0:1, 512:1024], scalar=1e-5,
                    in1=rowm[0:1, 1024:1536], op0=ALU.add, op1=ALU.subtract))
                chain_s(lambda s: s.activation(out=rowm[0:1, 1024:1536],
                                               in_=rowm[0:1, 512:1024], func=AF.Ln))
                chain_s(lambda s: s.activation(out=rowm[0:1, 512:1024],
                                               in_=rowm[0:1, 1024:1536],
                                               func=AF.Exp, scale=-0.5))
                chain_v(lambda v: v.tensor_tensor(out=rowm[0:1, 1024:1536],
                                                  in0=rowm[0:1, 0:512],
                                                  in1=rowm[0:1, 512:1024],
                                                  op=ALU.mult))
                chain_v(lambda v: v.tensor_scalar(out=rowm[0:1, 1024:1536],
                                                  in0=rowm[0:1, 1024:1536],
                                                  scalar1=-1.0, scalar2=None,
                                                  op0=ALU.mult))

                def bc(t):
                    t.matmul(out=pb6[:, :], lhsT=ones_r[:],
                             rhs=rowm[0:1, 512:1024], start=True, stop=True)
                    return t.matmul(out=pb7[:, :], lhsT=ones_r[:],
                                    rhs=rowm[0:1, 1024:1536], start=True, stop=True)
                chain_t(bc)

                def cp(v):
                    v.tensor_copy(out=rb[:], in_=pb6[:, :])
                    return v.tensor_copy(out=mb[:], in_=pb7[:, :])
                chain_v(cp)
                for c in range(8):
                    chain_v(lambda v, c=c: v.tensor_tensor(
                        out=tmp1[:], in0=src_resid(c), in1=rb[:], op=ALU.mult))
                    chain_v(lambda v, c=c: v.tensor_tensor(
                        out=src_resid(c), in0=tmp1[:], in1=mb[:], op=ALU.add))
                    if dst_xh is not None:
                        chain_s(lambda s, c=c: s.activation(
                            out=dst_xh(c), in_=src_resid(c), func=AF.Copy))

            for tb in range(ntb):
                def t1_s(sy, tb=tb, prev=s_lds.c, ch_need=s_ch.c):
                    sy.wait_ge(s_lds.h, prev)
                    sy.wait_ge(s_ch.h, ch_need)
                    sy.dma_start(
                        out=rsm[:].rearrange("p (c n) -> p c n", c=8),
                        in_=resid_in[:, tb * 512:(tb + 1) * 512].rearrange("(c p) n -> p c n", p=128),
                    ).then_inc(s_lds.h, 16)
                block.sync(t1_s)
                s_lds.c += 16

                def t1_t(t, tb=tb):
                    if tb == 0:
                        t.wait_ge(s_ldg.h, 32)
                    for dc in range(8):
                        bk = dc % 4
                        if tb * 8 + dc >= 4:
                            t.wait_ge(s_ev.h, tb * 8 + dc - 3)
                        for k in range(8):
                            inst = t.matmul(
                                out=ring[:, bk * 512:(bk + 1) * 512],
                                lhsT=bufB[:, k * 1024 + dc * 128: k * 1024 + (dc + 1) * 128],
                                rhs=bufA[:, k * 2048 + tb * 512: k * 2048 + tb * 512 + 512],
                                start=(k == 0), stop=(k == 7))
                        inst.then_inc(s_wo.h, 1)
                block.tensor(t1_t)
                s_wo.c += 8

                def t1_v(v, tb=tb, lds_need=s_lds.c, ch_need=s_ch.c):
                    v.wait_ge(s_lds.h, lds_need)
                    v.wait_ge(s_ch.h, ch_need)
                    for dc in range(8):
                        bk = dc % 4
                        v.wait_ge(s_wo.h, tb * 8 + dc + 1)
                        v.tensor_tensor(out=rsm[:, dc * 512:(dc + 1) * 512],
                                        in0=rsm[:, dc * 512:(dc + 1) * 512],
                                        in1=ring[:, bk * 512:(bk + 1) * 512],
                                        op=ALU.add).then_inc(s_ev.h, 1)
                block.vector(t1_v)
                s_ev.c += 8

                ev_need = s_ev.c

                def pe_wait(t, ev_need=ev_need):
                    t.wait_ge(s_ev.h, ev_need)

                def src_resid(c):
                    return rsm[:, c * 512:(c + 1) * 512]

                emit_ln(src_resid, None, extra_pe_wait=pe_wait)

                def park_chain(sy, tb=tb, need=s_ch.c, prev=s_st1.c):
                    sy.wait_ge(s_ch.h, need)
                    sy.wait_ge(s_st1.h, prev)
                    sy.dma_start(
                        out=s1_dram[:, tb * 512:(tb + 1) * 512].rearrange("(c p) n -> p c n", p=128),
                        in_=rsm[:].rearrange("p (c n) -> p c n", c=8),
                    ).then_inc(s_st1.h, 16)
                    if dump:
                        sy.dma_start(
                            out=s1_out[:, tb * 512:(tb + 1) * 512].rearrange("(c p) n -> p c n", p=128),
                            in_=rsm[:].rearrange("p (c n) -> p c n", c=8),
                        ).then_inc(s_st1.h, 16)
                block.sync(park_chain)
                s_st1.c += 32 if dump else 16
                def park_guard(v, need=s_st1.c):
                    v.wait_ge(s_st1.h, need)
                    v.nop().then_inc(s_ch.h, 1)
                block.vector(park_guard)
                s_ch.c += 1

            s_fw = sem("s_fw")
            fwsem = [sem("s_fwA"), sem("s_fwB")]
            s_fm = sem("s_fm")
            s_ed = sem("s_ed")
            s_es = sem("s_es")
            wbig2 = wstg

            class Banks:
                def __init__(self):
                    self.aps = [ring[:, i * 512:(i + 1) * 512] for i in range(4)] +                                [pb4[:, :], pb5[:, :], pb6[:, :], pb7[:, :]]
                    self.freed = [(None, 0)] * 8

                def wait_free(self, t, b):
                    semx, cnt = self.freed[b]
                    if semx is not None and cnt > 0:
                        t.wait_ge(semx.h, cnt)

            bk = Banks()

            def pipe_mm_stage(nblocks, nbuf, emit_dma, n_dma, emit_mm_group,
                              emit_evac, pe_extra_first=None, mm_per_group=1,
                              groups_per_block=2):
                m_after_block = {}
                entry_waits = [(s_fm, s_fm.c), (s_ed, s_ed.c), (s_es, s_es.c)]
                for i in range(nblocks):
                    buf = i % nbuf

                    def f_g(g, i=i, buf=buf, prev_m=m_after_block.get(i - nbuf, 0),
                            prev_fw=fwsem[buf].c, entry_waits=entry_waits,
                            nbuf=nbuf):
                        if prev_fw > 0:
                            g.wait_ge(fwsem[buf].h, prev_fw)
                        if i < nbuf:
                            for semx, cnt in entry_waits:
                                if cnt > 0:
                                    g.wait_ge(semx.h, cnt)
                        if prev_m > 0:
                            g.wait_ge(s_fm.h, prev_m)
                        emit_dma(g, i, buf)
                    block.gpsimd(f_g)
                    fwsem[buf].c += 16 * n_dma
                    fw_need = (fwsem[buf], fwsem[buf].c)

                    for ts in range(groups_per_block):
                        b = (2 * (i % 2) + ts) % 8 if groups_per_block == 2 else (i % 4)

                        def f_t(t, i=i, buf=buf, ts=ts, b=b, fw_need=fw_need,
                                first=(i == 0 and ts == 0)):
                            t.wait_ge(fw_need[0].h, fw_need[1])
                            if first and pe_extra_first is not None:
                                pe_extra_first(t)
                            bk.wait_free(t, b)
                            emit_mm_group(t, i, buf, ts, bk.aps[b]).then_inc(s_fm.h, 1)
                        block.tensor(f_t)
                        s_fm.c += 1
                        bk.freed[b] = emit_evac(i, ts, b, s_fm.c)
                    m_after_block[i] = s_fm.c

            def evac_act(fn):
                def mk(i, ts, b, m_cnt):
                    def f(s, fn=fn, i=i, ts=ts, b=b, m_cnt=m_cnt):
                        s.wait_ge(s_fm.h, m_cnt)
                        fn(s, i, ts, bk.aps[b]).then_inc(s_es.h, 1)
                    block.scalar(f)
                    s_es.c += 1
                    return (s_es, s_es.c)
                return mk

            def evac_dve(fn):
                def mk(i, ts, b, m_cnt):
                    def f(v, fn=fn, i=i, ts=ts, b=b, m_cnt=m_cnt):
                        v.wait_ge(s_fm.h, m_cnt)
                        fn(v, i, ts, bk.aps[b]).then_inc(s_ed.h, 1)
                    block.vector(f)
                    s_ed.c += 1
                    return (s_ed, s_ed.c)
                return mk

            def stage_barrier(t):
                t.wait_ge(s_ed.h, s_ed.c)
                t.wait_ge(s_es.h, s_es.c)

            n_s2out = [0]
            if do_ffn:
                wo_done = s_wo.c
                park_all = s_st1.c
                lds_all = s_lds.c
                for th in range(2):
                    stage1_mm_after = [0]

                    def fA_g(g, th=th, st1=park_all, prev_m=stage1_mm_after[0],
                             prev_fw=s_fw.c):
                        g.wait_ge(s_st1.h, st1)
                        if prev_fw > 0:
                            g.wait_ge(s_fw.h, prev_fw)
                        if prev_m > 0:
                            g.wait_ge(s_fm.h, prev_m)
                        g.dma_start(
                            out=xh[:].rearrange("p (c n) -> p c n", c=8),
                            in_=s1_dram[:, th * 1024:(th + 1) * 1024].rearrange("(c p) n -> p c n", p=128),
                        ).then_inc(s_fw.h, 16)
                    block.gpsimd(fA_g)
                    s_fw.c += 16
                    xh_ready = s_fw.c

                    def pe_first(t, th=th, xh_ready=xh_ready, ch_now=s_ch.c,
                                 ed_now=s_ed.c, es_now=s_es.c):
                        t.wait_ge(s_fw.h, xh_ready)
                        t.wait_ge(s_ch.h, ch_now)
                        if th == 0:
                            t.wait_ge(s_wo.h, wo_done)
                            t.wait_ge(s_lds.h, lds_all)
                        else:
                            t.wait_ge(s_ed.h, ed_now)
                            t.wait_ge(s_es.h, es_now)
                    s_ch_entry = [s_ch.c]

                    if layer == 1:
                        def dma1(g, i, buf):
                            g.dma_start(
                                out=weff[:, buf * 2048: buf * 2048 + 1024].rearrange("p (c n) -> p c n", c=8),
                                in_=w1_d[:, i * 128:(i + 1) * 128].rearrange("(c p) n -> p c n", p=128),
                            ).then_inc(fwsem[buf].h, 16)
                            g.dma_start(
                                out=weff[:, buf * 2048 + 1024: buf * 2048 + 2048].rearrange("p (c n) -> p c n", c=8),
                                in_=w2_d[:, i * 128:(i + 1) * 128].rearrange("(c p) n -> p c n", p=128),
                            ).then_inc(fwsem[buf].h, 16)

                        def mm1(t, i, buf, ts, bank_ap):
                            bA = 2 * (i % 2) + ts
                            bB = 4 + bA
                            bk.wait_free(t, bB)
                            for k in range(8):
                                t.matmul(out=bk.aps[bA],
                                         lhsT=weff[:, buf * 2048 + k * 128: buf * 2048 + (k + 1) * 128],
                                         rhs=xh[:, k * 1024 + ts * 512: k * 1024 + ts * 512 + 512],
                                         start=(k == 0), stop=(k == 7))
                            for k in range(8):
                                inst = t.matmul(out=bk.aps[bB],
                                         lhsT=weff[:, buf * 2048 + 1024 + k * 128: buf * 2048 + 1024 + (k + 1) * 128],
                                         rhs=xh[:, k * 1024 + ts * 512: k * 1024 + ts * 512 + 512],
                                         start=(k == 0), stop=(k == 7))
                            return inst

                        gslot_free = {0: 0, 1: 0}

                        def evac1(i, ts, b, m_cnt):
                            bA = 2 * (i % 2) + ts
                            bB = 4 + bA
                            gslot = ts * 512

                            def f_s(s, i=i, gslot=gslot, bA=bA, m_cnt=m_cnt,
                                    war=gslot_free[ts]):
                                s.wait_ge(s_fm.h, m_cnt)
                                if war > 0:
                                    s.wait_ge(s_ed.h, war)
                                s.activation(out=gtmp[:, gslot:gslot + 512],
                                             in_=bk.aps[bA], func=AF.Silu,
                                             scale=sv1[:, i:i + 1]).then_inc(s_es.h, 1)
                            block.scalar(f_s)
                            s_es.c += 1
                            esc = s_es.c

                            def f_v(v, i=i, ts=ts, gslot=gslot, bB=bB, esc=esc):
                                v.wait_ge(s_es.h, esc)
                                v.scalar_tensor_tensor(
                                    out=bufA[:, i * 1024 + ts * 512: i * 1024 + ts * 512 + 512],
                                    in0=bk.aps[bB], scalar=sv2[:, i:i + 1],
                                    in1=gtmp[:, gslot:gslot + 512],
                                    op0=ALU.mult, op1=ALU.mult).then_inc(s_ed.h, 1)
                            block.vector(f_v)
                            s_ed.c += 1
                            gslot_free[ts] = s_ed.c
                            bk.freed[bA] = (s_es, esc)
                            return (s_ed, s_ed.c)

                        pipe_mm_stage(32, 2, dma1, 2, mm1, evac1,
                                      pe_extra_first=pe_first)
                        stage1_mm_after[0] = s_fm.c

                        def dma2(g, i, buf, wsrc=h1_d):
                            dst = wbig if buf == 0 else wbig2
                            g.dma_start(
                                out=dst[:].rearrange("p (c n) -> p c n", c=32),
                                in_=wsrc[:, i * 128:(i + 1) * 128].rearrange("(c p) n -> p c n", p=128),
                            ).then_inc(fwsem[buf].h, 16)

                        def mk_mm_big(src_buf):
                            def mm(t, i, buf, ts, bank_ap):
                                w = wbig if buf == 0 else wbig2
                                for k in range(32):
                                    inst = t.matmul(out=bank_ap,
                                             lhsT=w[:, k * 128:(k + 1) * 128],
                                             rhs=src_buf[:, k * 1024 + ts * 512: k * 1024 + ts * 512 + 512],
                                             start=(k == 0), stop=(k == 31))
                                return inst
                            return mm

                        def mk_gelu_evac(dst_buf, scol):
                            def fn(s, i, ts, bank_ap):
                                return s.activation(out=dst_buf[:, i * 1024 + ts * 512: i * 1024 + ts * 512 + 512],
                                             in_=bank_ap, func=AF.Gelu,
                                             scale=scol[:, i:i + 1])
                            return evac_act(fn)

                        def barrier_first(t):
                            stage_barrier(t)
                        pipe_mm_stage(32, 2, dma2, 1, mk_mm_big(bufA),
                                      mk_gelu_evac(bufB, sh1)(0, 0, 0, 0) if False else mk_gelu_evac(bufB, sh1),
                                      pe_extra_first=barrier_first)

                        def dma3(g, i, buf):
                            dst = wbig if buf == 0 else wbig2
                            g.dma_start(
                                out=dst[:].rearrange("p (c n) -> p c n", c=32),
                                in_=h2_d[:, i * 128:(i + 1) * 128].rearrange("(c p) n -> p c n", p=128),
                            ).then_inc(fwsem[buf].h, 16)
                        pipe_mm_stage(32, 2, dma3, 1, mk_mm_big(bufB),
                                      mk_gelu_evac(bufA, sh2),
                                      pe_extra_first=barrier_first)

                        for ts in range(2):
                            def s4_s(sy, th=th, ts=ts, prev=s_lds.c, need=s_ch.c,
                                     ed=s_ed.c, es=s_es.c):
                                sy.wait_ge(s_lds.h, prev)
                                sy.wait_ge(s_ch.h, need)
                                col = th * 1024 + ts * 512
                                sy.dma_start(
                                    out=rsm[:].rearrange("p (c n) -> p c n", c=8),
                                    in_=s1_dram[:, col:col + 512].rearrange("(c p) n -> p c n", p=128),
                                ).then_inc(s_lds.h, 16)
                            block.sync(s4_s)
                            s_lds.c += 16
                            lds_need = s_lds.c

                            def dma4(g, i, buf):
                                dst = wbig if buf == 0 else wbig2
                                g.dma_start(
                                    out=dst[:].rearrange("p (c n) -> p c n", c=32),
                                    in_=w3_d[:, i * 128:(i + 1) * 128].rearrange("(c p) n -> p c n", p=128),
                                ).then_inc(fwsem[buf].h, 16)

                            def mm4(t, i, buf, _ts, bank_ap, ts=ts,
                                    lds_need=lds_need):
                                if i == 0:
                                    t.wait_ge(s_lds.h, lds_need)
                                w = wbig if buf == 0 else wbig2
                                for k in range(32):
                                    inst = t.matmul(out=bank_ap,
                                             lhsT=w[:, k * 128:(k + 1) * 128],
                                             rhs=bufA[:, k * 1024 + ts * 512: k * 1024 + ts * 512 + 512],
                                             start=(k == 0), stop=(k == 31))
                                return inst

                            def ev4(v, i, _ts, bank_ap, ts=ts):
                                return v.scalar_tensor_tensor(
                                    out=rsm[:, i * 512:(i + 1) * 512],
                                    in0=bank_ap, scalar=sv3[:, i:i + 1],
                                    in1=rsm[:, i * 512:(i + 1) * 512],
                                    op0=ALU.mult, op1=ALU.add)
                            pipe_mm_stage(8, 2, dma4, 1, mm4, evac_dve(ev4),
                                          pe_extra_first=(barrier_first if ts == 0 else None),
                                          groups_per_block=1)

                            def lnent(t, ed=s_ed.c, es=s_es.c):
                                t.wait_ge(s_ed.h, ed)
                                t.wait_ge(s_es.h, es)
                            emit_ln(lambda c: rsm[:, c * 512:(c + 1) * 512], None,
                                    extra_pe_wait=lnent)

                            def s2out_s(sy, th=th, ts=ts, need=s_ch.c, prev=s_out.c):
                                sy.wait_ge(s_ch.h, need)
                                sy.wait_ge(s_out.h, prev)
                                col = th * 1024 + ts * 512
                                sy.dma_start(
                                    out=s2_out[:, col:col + 512].rearrange("(c p) n -> p c n", p=128),
                                    in_=rsm[:].rearrange("p (c n) -> p c n", c=8),
                                ).then_inc(s_out.h, 16)
                            block.sync(s2out_s)
                            s_out.c += 16
                            n_s2out[0] += 1

                            def s2out_guard(v, need=s_out.c):
                                v.wait_ge(s_out.h, need)
                                v.nop().then_inc(s_ch.h, 1)
                            block.vector(s2out_guard)
                            s_ch.c += 1

                    else:
                        def dma1(g, i, buf):
                            g.dma_start(
                                out=weff[:, buf * 2048: buf * 2048 + 2048].rearrange("p (c n) -> p c n", c=8),
                                in_=k0t_d[:, i * 256:(i + 1) * 256].rearrange("(c p) n -> p c n", p=128),
                            ).then_inc(fwsem[buf].h, 16)
                            g.dma_start(
                                out=wstg[:, buf * 2048: buf * 2048 + 2048].rearrange("p (c n) -> p c n", c=8),
                                in_=k0b_d[:, i * 256:(i + 1) * 256].rearrange("(c p) n -> p c n", p=128),
                            ).then_inc(fwsem[buf].h, 16)

                        sub_done = {}

                        def mm1(t, i, buf, ts, bank_ap, sub_done=sub_done):
                            t.wait_ge(s_ed.h, sub_done[i])
                            for d in range(2):
                                b2 = 2 * (i % 2) + d
                                if ts == 0:
                                    bk.wait_free(t, b2)
                                    bk2 = 4 + b2
                                    bk.wait_free(t, bk2)
                                for k in range(8):
                                    inst = t.matmul(
                                        out=bk.aps[2 * (i % 2) + d] if ts == 0 else bk.aps[4 + 2 * (i % 2) + d],
                                        lhsT=weff[:, buf * 2048 + k * 256 + d * 128: buf * 2048 + k * 256 + (d + 1) * 128],
                                        rhs=xh[:, k * 1024 + ts * 512: k * 1024 + ts * 512 + 512],
                                        start=(k == 0), stop=(k == 7))
                            return inst

                        def evac1(i, ts, b, m_cnt):
                            def f_v(v, i=i, ts=ts, m_cnt=m_cnt):
                                v.wait_ge(s_fm.h, m_cnt)
                                for d in range(2):
                                    bidx = (2 * (i % 2) + d) + (0 if ts == 0 else 4)
                                    dd = i * 2 + d
                                    inst = v.tensor_scalar(
                                        out=bufA[:, dd * 1024 + ts * 512: dd * 1024 + ts * 512 + 512],
                                        in0=bk.aps[bidx], scalar1=scl[:, 0:1],
                                        scalar2=0.0, op0=ALU.mult, op1=ALU.max)
                                inst.then_inc(s_ed.h, 1)
                            block.vector(f_v)
                            s_ed.c += 1
                            for d in range(2):
                                bidx = (2 * (i % 2) + d) + (0 if ts == 0 else 4)
                                bk.freed[bidx] = (s_ed, s_ed.c)
                            return (s_ed, s_ed.c)

                        m_after = {}
                        s1_entry = [(s_fm, s_fm.c), (s_ed, s_ed.c), (s_es, s_es.c)]
                        for i in range(16):
                            buf = i % 2

                            def f_g(g, i=i, buf=buf, prev_m=m_after.get(i - 2, 0),
                                    prev_fw=fwsem[buf].c, s1_entry=s1_entry):
                                if prev_fw > 0:
                                    g.wait_ge(fwsem[buf].h, prev_fw)
                                if i < 2:
                                    for semx, cnt in s1_entry:
                                        if cnt > 0:
                                            g.wait_ge(semx.h, cnt)
                                if prev_m > 0:
                                    g.wait_ge(s_fm.h, prev_m)
                                dma1(g, i, buf)
                            block.gpsimd(f_g)
                            fwsem[buf].c += 32
                            fw_need = (fwsem[buf], fwsem[buf].c)

                            def f_sub(v, i=i, buf=buf, fw_need=fw_need,
                                      prev_m=m_after.get(i - 2, 0)):
                                v.wait_ge(fw_need[0].h, fw_need[1])
                                if prev_m > 0:
                                    v.wait_ge(s_fm.h, prev_m)
                                v.tensor_tensor(out=weff[:, buf * 2048: buf * 2048 + 2048],
                                                in0=weff[:, buf * 2048: buf * 2048 + 2048],
                                                in1=wstg[:, buf * 2048: buf * 2048 + 2048],
                                                op=ALU.subtract).then_inc(s_ed.h, 1)
                            block.vector(f_sub)
                            s_ed.c += 1
                            sub_done[i] = s_ed.c

                            for ts in range(2):
                                def f_t(t, i=i, buf=buf, ts=ts,
                                        first=(i == 0 and ts == 0)):
                                    if first:
                                        pe_first(t)
                                    mm1(t, i, buf, ts, None).then_inc(s_fm.h, 1)
                                block.tensor(f_t)
                                s_fm.c += 1
                                evac1(i, ts, None, s_fm.c)
                            m_after[i] = s_fm.c
                        stage1_mm_after[0] = s_fm.c

                        def barrier_first(t):
                            stage_barrier(t)

                        for ts in range(2):
                            def srow_t(t, ts=ts, first=(ts == 0)):
                                if first:
                                    stage_barrier(t)
                                bk.wait_free(t, 4 + ts)
                                for k in range(32):
                                    inst = t.matmul(out=bk.aps[4 + ts][0:1, :],
                                                    lhsT=ones_b[:],
                                                    rhs=bufA[:, k * 1024 + ts * 512: k * 1024 + ts * 512 + 512],
                                                    start=(k == 0), stop=(k == 31))
                                inst.then_inc(s_fm.h, 1)
                            block.tensor(srow_t)
                            s_fm.c += 1

                            def srow_v(v, ts=ts, m_cnt=s_fm.c):
                                v.wait_ge(s_fm.h, m_cnt)
                                v.tensor_scalar(out=rowm[0:1, 0:512],
                                                in0=bk.aps[4 + ts][0:1, :],
                                                scalar1=scl[0:1, 1:2], scalar2=None,
                                                op0=ALU.mult).then_inc(s_ed.h, 1)
                            block.vector(srow_v)
                            s_ed.c += 1
                            bk.freed[4 + ts] = (s_ed, s_ed.c)

                            def srow_t2(t, ts=ts, e_cnt=s_ed.c):
                                t.wait_ge(s_ed.h, e_cnt)
                                bk.wait_free(t, 6 + ts)
                                t.matmul(out=bk.aps[6 + ts], lhsT=ones_r[:],
                                         rhs=rowm[0:1, 0:512], start=True,
                                         stop=True).then_inc(s_fm.h, 1)
                            block.tensor(srow_t2)
                            s_fm.c += 1

                            def srow_v2(v, ts=ts, m_cnt=s_fm.c):
                                v.wait_ge(s_fm.h, m_cnt)
                                v.tensor_copy(out=sb_corr[:, ts * 512:(ts + 1) * 512],
                                              in_=bk.aps[6 + ts]).then_inc(s_ed.h, 1)
                            block.vector(srow_v2)
                            s_ed.c += 1
                            bk.freed[6 + ts] = (s_ed, s_ed.c)

                        def dma2(g, i, buf):
                            dst = wbig if buf == 0 else wbig2
                            g.dma_start(
                                out=dst[:].rearrange("p (c n) -> p c n", c=32),
                                in_=k1_d[:, i * 128:(i + 1) * 128].rearrange("(c p) n -> p c n", p=128),
                            ).then_inc(fwsem[buf].h, 16)

                        def mk_mm_big(src_buf):
                            def mm(t, i, buf, ts, bank_ap):
                                w = wbig if buf == 0 else wbig2
                                for k in range(32):
                                    inst = t.matmul(out=bank_ap,
                                             lhsT=w[:, k * 128:(k + 1) * 128],
                                             rhs=src_buf[:, k * 1024 + ts * 512: k * 1024 + ts * 512 + 512],
                                             start=(k == 0), stop=(k == 31))
                                return inst
                            return mm

                        tslot_free = {0: 0, 1: 0}

                        def evac2(i, ts, b, m_cnt):
                            tslot = ts * 512

                            def f_v(v, i=i, ts=ts, b=b, m_cnt=m_cnt, tslot=tslot,
                                    war=tslot_free[ts]):
                                v.wait_ge(s_fm.h, m_cnt)
                                if war > 0:
                                    v.wait_ge(s_es.h, war)
                                v.scalar_tensor_tensor(
                                    out=tmp2[:, tslot:tslot + 512],
                                    in0=bk.aps[b], scalar=scl4[:, 1:2],
                                    in1=sb_corr[:, ts * 512:(ts + 1) * 512],
                                    op0=ALU.mult, op1=ALU.subtract).then_inc(s_ed.h, 1)
                            block.vector(f_v)
                            s_ed.c += 1
                            edc = s_ed.c

                            def f_s(s, i=i, ts=ts, edc=edc, tslot=tslot):
                                s.wait_ge(s_ed.h, edc)
                                s.activation(
                                    out=bufB[:, i * 1024 + ts * 512: i * 1024 + ts * 512 + 512],
                                    in_=tmp2[:, tslot:tslot + 512], func=AF.Relu).then_inc(s_es.h, 1)
                            block.scalar(f_s)
                            s_es.c += 1
                            tslot_free[ts] = s_es.c
                            return (s_ed, edc)
                        pipe_mm_stage(32, 2, dma2, 1, mk_mm_big(bufA), evac2)

                        for ts in range(2):
                            def srow_t(t, ts=ts, first=(ts == 0)):
                                if first:
                                    stage_barrier(t)
                                bk.wait_free(t, 4 + ts)
                                for k in range(32):
                                    inst = t.matmul(out=bk.aps[4 + ts][0:1, :],
                                                    lhsT=ones_b[:],
                                                    rhs=bufB[:, k * 1024 + ts * 512: k * 1024 + ts * 512 + 512],
                                                    start=(k == 0), stop=(k == 31))
                                inst.then_inc(s_fm.h, 1)
                            block.tensor(srow_t)
                            s_fm.c += 1

                            def srow_v(v, ts=ts, m_cnt=s_fm.c):
                                v.wait_ge(s_fm.h, m_cnt)
                                v.tensor_scalar(out=rowm[0:1, 0:512],
                                                in0=bk.aps[4 + ts][0:1, :],
                                                scalar1=scl[0:1, 2:3], scalar2=None,
                                                op0=ALU.mult).then_inc(s_ed.h, 1)
                            block.vector(srow_v)
                            s_ed.c += 1
                            bk.freed[4 + ts] = (s_ed, s_ed.c)

                            def srow_t2(t, ts=ts, e_cnt=s_ed.c):
                                t.wait_ge(s_ed.h, e_cnt)
                                bk.wait_free(t, 6 + ts)
                                t.matmul(out=bk.aps[6 + ts], lhsT=ones_r[:],
                                         rhs=rowm[0:1, 0:512], start=True,
                                         stop=True).then_inc(s_fm.h, 1)
                            block.tensor(srow_t2)
                            s_fm.c += 1

                            def srow_v2(v, ts=ts, m_cnt=s_fm.c):
                                v.wait_ge(s_fm.h, m_cnt)
                                v.tensor_copy(out=sb_corr[:, ts * 512:(ts + 1) * 512],
                                              in_=bk.aps[6 + ts]).then_inc(s_ed.h, 1)
                            block.vector(srow_v2)
                            s_ed.c += 1
                            bk.freed[6 + ts] = (s_ed, s_ed.c)

                        for ts in range(2):
                            def s3_s(sy, th=th, ts=ts, prev=s_lds.c, need=s_ch.c):
                                sy.wait_ge(s_lds.h, prev)
                                sy.wait_ge(s_ch.h, need)
                                col = th * 1024 + ts * 512
                                sy.dma_start(
                                    out=rsm[:].rearrange("p (c n) -> p c n", c=8),
                                    in_=s1_dram[:, col:col + 512].rearrange("(c p) n -> p c n", p=128),
                                ).then_inc(s_lds.h, 16)
                            block.sync(s3_s)
                            s_lds.c += 16
                            lds_need = s_lds.c

                            def dma3(g, i, buf):
                                dst = wbig if buf == 0 else wbig2
                                g.dma_start(
                                    out=dst[:].rearrange("p (c n) -> p c n", c=32),
                                    in_=k2_d[:, i * 128:(i + 1) * 128].rearrange("(c p) n -> p c n", p=128),
                                ).then_inc(fwsem[buf].h, 16)

                            def mm3(t, i, buf, _ts, bank_ap, ts=ts,
                                    lds_need=lds_need):
                                if i == 0:
                                    t.wait_ge(s_lds.h, lds_need)
                                w = wbig if buf == 0 else wbig2
                                for k in range(32):
                                    inst = t.matmul(out=bank_ap,
                                             lhsT=w[:, k * 128:(k + 1) * 128],
                                             rhs=bufB[:, k * 1024 + ts * 512: k * 1024 + ts * 512 + 512],
                                             start=(k == 0), stop=(k == 31))
                                return inst

                            def ev3(i, _ts, b, m_cnt, ts=ts):
                                def f_v(v, i=i, b=b, m_cnt=m_cnt, ts=ts,
                                        war=s_ed.c):
                                    v.wait_ge(s_fm.h, m_cnt)
                                    if war > 0:
                                        v.wait_ge(s_ed.h, war)
                                    v.scalar_tensor_tensor(
                                        out=tmp1[:], in0=bk.aps[b],
                                        scalar=scl4[:, 2:3],
                                        in1=sb_corr[:, ts * 512:(ts + 1) * 512],
                                        op0=ALU.mult, op1=ALU.subtract).then_inc(s_ed.h, 1)
                                block.vector(f_v)
                                s_ed.c += 1
                                edc = s_ed.c

                                def f_v2(v, i=i, edc=edc):
                                    v.wait_ge(s_ed.h, edc)
                                    v.tensor_tensor(
                                        out=rsm[:, i * 512:(i + 1) * 512],
                                        in0=rsm[:, i * 512:(i + 1) * 512],
                                        in1=tmp1[:], op=ALU.add).then_inc(s_ed.h, 1)
                                block.vector(f_v2)
                                s_ed.c += 1
                                return (s_ed, edc)
                            pipe_mm_stage(8, 2, dma3, 1, mm3, ev3,
                                          pe_extra_first=(barrier_first if ts == 0 else None),
                                          groups_per_block=1)

                            def lnent(t, ed=s_ed.c, es=s_es.c):
                                t.wait_ge(s_ed.h, ed)
                                t.wait_ge(s_es.h, es)
                            emit_ln(lambda c: rsm[:, c * 512:(c + 1) * 512], None,
                                    extra_pe_wait=lnent)

                            def s2out_s(sy, th=th, ts=ts, need=s_ch.c, prev=s_out.c):
                                sy.wait_ge(s_ch.h, need)
                                sy.wait_ge(s_out.h, prev)
                                col = th * 1024 + ts * 512
                                sy.dma_start(
                                    out=s2_out[:, col:col + 512].rearrange("(c p) n -> p c n", p=128),
                                    in_=rsm[:].rearrange("p (c n) -> p c n", c=8),
                                ).then_inc(s_out.h, 16)
                            block.sync(s2out_s)
                            s_out.c += 16
                            n_s2out[0] += 1

                            def s2out_guard(v, need=s_out.c):
                                v.wait_ge(s_out.h, need)
                                v.nop().then_inc(s_ch.h, 1)
                            block.vector(s2out_guard)
                            s_ch.c += 1

            def tE(sy):
                sy.wait_ge(s_st1.h, s_st1.c)
                sy.wait_ge(s_ch.h, s_ch.c)
                if not do_ffn:
                    sy.dma_start(out=s2_out.rearrange("(c p) n -> p c n", p=128),
                                 in_=s1_dram.rearrange("(c p) n -> p c n", p=128)
                                 ).then_inc(s_out.h, 16)
                    s_out.c += 16
                sy.wait_ge(s_out.h, s_out.c)
                sy.wait_ge(s_ldg.h, s_ldg.c)
                sy.wait_ge(s_lds.h, s_lds.c)
                sy.wait_ge(s_fw.h, s_fw.c)
                sy.wait_ge(fwsem[0].h, fwsem[0].c)
                sy.wait_ge(fwsem[1].h, fwsem[1].c)
            block.sync(tE)

    return nc



def _f32c(a):
    return np.ascontiguousarray(np.asarray(a, dtype=np.float32))


def make_cmask():
    cm = np.zeros((4, 128, 512), dtype=ml_dtypes.bfloat16)
    tri = np.triu(np.ones((128, 128), dtype=np.float32))
    for d in range(4):
        for st in range(4):
            if st > d:
                cm[d, :, st * 128:(st + 1) * 128] = 1.0
            elif st == d:
                cm[d, :, st * 128:(st + 1) * 128] = tri.astype(ml_dtypes.bfloat16)
    return cm


def stage_l0_attn(x, Wq, Wk, Wv):
    xT = np.ascontiguousarray(x.transpose(0, 2, 1))
    WqT = _f32c(np.asarray(Wq).T)
    WkT = _f32c(np.asarray(Wk).T)
    WvT = _f32c(np.asarray(Wv).T)
    cm = make_cmask()
    in_maps = []
    for core in range(NC):
        b, g = core // 4, core % 4
        wvz = np.zeros((D, 65), dtype=np.float32)
        wvz[:, 0:64] = WvT[:, 64 * g:64 * g + 64]
        in_maps.append({
            "xfm": np.ascontiguousarray(xT[b]),
            "wq": np.ascontiguousarray(WqT[:, 256 * g:256 * g + 256]),
            "wk": np.ascontiguousarray(WkT[:, 64 * g:64 * g + 64]),
            "wvz": wvz,
            "cmask": cm,
        })
    return in_maps


def gather_o(results):
    o_full = np.zeros((B, D, S), dtype=ml_dtypes.bfloat16)
    for core in range(NC):
        b, g = core // 4, core % 4
        o_full[b, 256 * g:256 * g + 128] = results[core]["o_out0"]
        o_full[b, 256 * g + 128:256 * g + 256] = results[core]["o_out1"]
    return o_full


def _rope_tables():
    inv = 1.0 / (10000.0 ** (np.arange(0, D, 2, dtype=np.float64) / D))
    t = np.arange(S, dtype=np.float64)[:, None]
    ang = t * inv[None, :]
    return np.cos(ang).astype(np.float32), np.sin(ang).astype(np.float32)


_EO = np.concatenate([np.arange(0, 64, 2), np.arange(1, 64, 2)])


def stage_l1_attn(xfm_full, Wq, Wk, Wv):
    Wq = _f32c(Wq)
    Wk = _f32c(Wk)
    Wv = _f32c(Wv)
    cosT, sinT = _rope_tables()
    in_maps = []
    for core in range(NC):
        b, g = core // 4, core % 4
        wq = np.zeros((D, 256), np.float32)
        for i in range(4):
            h = 4 * g + i
            wq[:, 64 * i:64 * i + 64] = Wq[:, 64 * h:64 * h + 64][:, _EO]
        wk = np.ascontiguousarray(Wk[:, 64 * g:64 * g + 64][:, _EO])
        wvz = np.zeros((D, 65), np.float32)
        wvz[:, 0:64] = Wv[:, 64 * g:64 * g + 64]
        rqc = np.zeros((128, 2 * S), np.float32)
        rqs = np.zeros((128, 2 * S), np.float32)
        for c in range(2):
            for half in range(2):
                h = 4 * g + 2 * c + half
                cf = cosT[:, 32 * h:32 * h + 32].T
                sf = sinT[:, 32 * h:32 * h + 32].T
                r0 = half * 64
                rqc[r0:r0 + 32, c * S:(c + 1) * S] = cf
                rqc[r0 + 32:r0 + 64, c * S:(c + 1) * S] = cf
                rqs[r0:r0 + 32, c * S:(c + 1) * S] = -sf
                rqs[r0 + 32:r0 + 64, c * S:(c + 1) * S] = sf
        rkc = np.zeros((64, S), np.float32)
        rks = np.zeros((64, S), np.float32)
        ckf = cosT[:, 0:32].T
        skf = sinT[:, 0:32].T
        rkc[0:32] = ckf
        rkc[32:64] = ckf
        rks[0:32] = -skf
        rks[32:64] = skf
        in_maps.append({
            "xfm": np.ascontiguousarray(xfm_full[b]),
            "wq": wq, "wk": wk, "wvz": wvz,
            "rqc": rqc, "rqs": rqs, "rkc": rkc, "rks": rks,
        })
    return in_maps


def stage_l0_tail(o_full, x, Wo, k0, k1, k2, s0, s1, s2):
    xT = np.ascontiguousarray(x.transpose(0, 2, 1))
    WoT = _f32c(np.asarray(Wo).T)
    k0 = np.asarray(k0)
    k0t = np.ascontiguousarray(k0[0:D]).astype(np.uint8)
    k0b = np.ascontiguousarray(k0[D:2 * D]).astype(np.uint8)
    k1u = np.ascontiguousarray(np.asarray(k1)).astype(np.uint8)
    k2u = np.ascontiguousarray(np.asarray(k2)).astype(np.uint8)
    scl = np.array([[float(s0), float(s1), float(s2), 0.0]], np.float32)
    in_maps = []
    for core in range(NC):
        b, j = core // 4, core % 4
        in_maps.append({
            "o_in": np.ascontiguousarray(o_full[b][:, TPC * j:TPC * (j + 1)]),
            "resid_in": np.ascontiguousarray(xT[b][:, TPC * j:TPC * (j + 1)]),
            "wo": WoT,
            "k0t": k0t, "k0b": k0b, "k1": k1u, "k2": k2u, "scl": scl,
        })
    return in_maps


def gather_stream(results, key="s2_out"):
    s = np.zeros((B, D, S), dtype=np.float32)
    for core in range(NC):
        b, j = core // 4, core % 4
        s[b][:, TPC * j:TPC * (j + 1)] = results[core][key]
    return s


def run_layer0(x, inp):
    nc1 = build_attn_core(layer=0)
    r1 = run_bass_kernel_spmd(nc1, stage_l0_attn(
        x, inp["L0_attn_Wq"], inp["L0_attn_Wk"], inp["L0_attn_Wv"]),
        core_ids=list(range(NC)))
    o_full = gather_o(r1.results)
    nc2 = build_block_tail(layer=0)
    r2 = run_bass_kernel_spmd(nc2, stage_l0_tail(
        o_full, x, inp["L0_attn_Wo"], inp["L0_ffn_k0"], inp["L0_ffn_k1"],
        inp["L0_ffn_k2"], inp["L0_ffn_s0"], inp["L0_ffn_s1"], inp["L0_ffn_s2"]),
        core_ids=list(range(NC)))
    return gather_stream(r2.results)


def stage_l1_tail(o_full, stream1, inp):
    WoT = _f32c(inp["L1_attn_Wo"])
    w1t = _f32c(np.asarray(inp["L1_ffn_w1_w"]).T)
    w2t = _f32c(np.asarray(inp["L1_ffn_w2_w"]).T)
    h1t = _f32c(np.asarray(inp["L1_ffn_h1_w"]).T)
    h2t = _f32c(np.asarray(inp["L1_ffn_h2_w"]).T)
    w3t = _f32c(np.asarray(inp["L1_ffn_w3_w"]).T)
    pmj = lambda v: np.ascontiguousarray(np.asarray(v, np.float32).reshape(-1, 128).T)
    sv1 = pmj(inp["L1_ffn_w1_s"])
    sv2 = pmj(inp["L1_ffn_w2_s"])
    sh1 = pmj(inp["L1_ffn_h1_s"])
    sh2 = pmj(inp["L1_ffn_h2_s"])
    sv3 = pmj(inp["L1_ffn_w3_s"])
    in_maps = []
    for core in range(NC):
        b, j = core // 4, core % 4
        in_maps.append({
            "o_in": np.ascontiguousarray(o_full[b][:, TPC * j:TPC * (j + 1)]),
            "resid_in": np.ascontiguousarray(stream1[b][:, TPC * j:TPC * (j + 1)]),
            "wo": WoT,
            "w1t": w1t, "w2t": w2t, "h1t": h1t, "h2t": h2t, "w3t": w3t,
            "sv1": sv1, "sv2": sv2, "sh1": sh1, "sh2": sh2, "sv3": sv3,
        })
    return in_maps


_BUILT = {}


def _get(builder, key, *args, **kw):
    if key not in _BUILT:
        _BUILT[key] = builder(*args, **kw)
    return _BUILT[key]


def kernel(**inputs):
    x = np.asarray(inputs["L0_attn_hidden_states"], np.float32)
    cores = list(range(NC))

    nc = _get(build_attn_core, "a0", layer=0)
    r = run_bass_kernel_spmd(nc, stage_l0_attn(
        x, inputs["L0_attn_Wq"], inputs["L0_attn_Wk"], inputs["L0_attn_Wv"]),
        core_ids=cores)
    o_full = gather_o(r.results)
    nc = _get(build_block_tail, "t0", layer=0)
    r = run_bass_kernel_spmd(nc, stage_l0_tail(
        o_full, x, inputs["L0_attn_Wo"], inputs["L0_ffn_k0"], inputs["L0_ffn_k1"],
        inputs["L0_ffn_k2"], inputs["L0_ffn_s0"], inputs["L0_ffn_s1"],
        inputs["L0_ffn_s2"]), core_ids=cores)
    s2 = gather_stream(r.results)

    nc = _get(build_attn_core, "a1", layer=1)
    r = run_bass_kernel_spmd(nc, stage_l1_attn(
        s2, inputs["L1_attn_Wq"], inputs["L1_attn_Wk"], inputs["L1_attn_Wv"]),
        core_ids=cores)
    o_full = gather_o(r.results)
    nc = _get(build_block_tail, "t1", layer=1)
    r = run_bass_kernel_spmd(nc, stage_l1_tail(o_full, s2, inputs), core_ids=cores)
    s4 = gather_stream(r.results)

    return np.ascontiguousarray(s4.transpose(0, 2, 1))
